# version 10
# baseline (speedup 1.0000x reference)
import contextlib

import numpy as np

import concourse.bass as bass
import concourse.mybir as mybir
from concourse.bass_utils import run_bass_kernel_spmd

F32 = mybir.dt.float32
BF16 = mybir.dt.bfloat16
AF = mybir.ActivationFunctionType
ALU = mybir.AluOpType
AX = mybir.AxisListType

B, T, D, H = 8, 4096, 1024, 64
NE, NI = 16, 511
NSH = 1023
NT = T // 128
HALF = T // 2
GRP = 256
NG = HALF // GRP
EPS_LN = 1e-5
EPS = 1e-6

W1_OFF = 0
W3_OFF = 8 * 512
W2_OFF = 2 * 8 * 512
R0_OFF = W2_OFF + 4 * NSH
WE_COLS = R0_OFF + NSH

SW1_OFF = 0
SW3_OFF = 8 * 1024
SW2_OFF = 2 * 8 * 1024
SR0_OFF = SW2_OFF + 8 * NSH
WS_COLS = SR0_OFF + NSH


class _Chan:
    __slots__ = ("sem", "n", "name", "owner")

    def __init__(self, sem, name, owner):
        self.sem, self.n, self.name, self.owner = sem, 0, name, owner


class KB:
    ENGS = ("Y", "G", "T", "V", "S")

    def __init__(self, nc, es):
        self.nc = nc
        self.es = es
        self.items = []
        self.chans = []

    def chan(self, name, owner=None):
        ch = _Chan(self.es.enter_context(self.nc.semaphore(name)), name, owner)
        self.chans.append(ch)
        return ch

    def op(self, eng, emit, after=(), inc=None, k=1):
        idx = len(self.items)
        tok = None
        if inc is not None:
            tok = (inc, idx)
        after = tuple(t for t in after if t is not None)
        self.items.append([eng, after, emit, inc, k, idx])
        return tok

    def emit(self, block):
        used = set()
        for eng, after, _, _, _, _ in self.items:
            for (ch, idx) in after:
                if ch.owner == eng == "T":
                    continue
                used.add((ch.name, idx))
        val_at = {}
        counts = {}
        for it in self.items:
            eng, after, emitfn, inc, k, idx = it
            if inc is not None and (k == 16 or (inc.name, idx) in used):
                counts[inc.name] = counts.get(inc.name, 0) + k
                val_at[(inc.name, idx)] = counts[inc.name]
            elif inc is not None:
                it[3] = None
        decs = {"T": block.tensor, "V": block.vector, "S": block.scalar,
                "G": block.gpsimd, "Y": block.sync}
        for engname in self.ENGS:
            items = [it for it in self.items if it[0] == engname]
            if not items:
                continue

            def section(eng, items=items, engname=engname):
                waited = {}
                for _, after, emitfn, inc, k, idx in items:
                    for (ch, tidx) in after:
                        if ch.owner == engname == "T":
                            continue
                        val = val_at[(ch.name, tidx)]
                        if waited.get(ch.name, 0) < val:
                            eng.wait_ge(ch.sem, val)
                            waited[ch.name] = val
                    instr = emitfn(eng)
                    if inc is not None:
                        if instr is None:
                            instr = eng.nop()
                        instr.then_inc(inc.sem, k)

            decs[engname](section)


def _pack_T(w, rpad=None):
    R, K = w.shape
    assert K % 128 == 0
    rpad = R if rpad is None else rpad
    wt = np.zeros((K, rpad), np.float32)
    wt[:, :R] = w.T
    wt = wt.reshape(K // 128, 128, rpad)
    return np.ascontiguousarray(wt.transpose(1, 0, 2).reshape(128, (K // 128) * rpad))


def _pack_rows(wt, nrow_pad):
    R, C = wt.shape
    out = np.zeros((nrow_pad * 128, C), np.float32)
    out[:R] = wt
    out = out.reshape(nrow_pad, 128, C).transpose(1, 0, 2)
    return np.ascontiguousarray(out.reshape(128, nrow_pad * C))


def _bias_fm(b, ntile):
    out = np.zeros((ntile * 128,), np.float32)
    out[:b.shape[0]] = b
    return np.ascontiguousarray(out.reshape(ntile, 128).T)


def pack_host(inputs):
    f = np.float32
    g = {}
    g["wq"] = _pack_T(np.ascontiguousarray(inputs["L0_attn_W_q"].T))
    g["wk"] = _pack_T(np.ascontiguousarray(inputs["L0_attn_W_k"].T))
    g["wv"] = _pack_T(np.ascontiguousarray(inputs["L0_attn_W_v"].T))
    g["wo"] = np.ascontiguousarray(inputs["L0_attn_W_o"], dtype=f)
    g["gw"] = _pack_T(np.ascontiguousarray(inputs["L0_ffn_gate_w"]))
    blobs = []
    bias_blobs = []
    for e in range(NE):
        we1 = np.asarray(inputs["L0_ffn_We1"][e], f)
        we3 = np.asarray(inputs["L0_ffn_We3"][e], f)
        we2 = np.asarray(inputs["L0_ffn_We2"][e], f)
        be1 = np.asarray(inputs["L0_ffn_be1"][e], f)
        be3 = np.asarray(inputs["L0_ffn_be3"][e], f)
        be2 = np.asarray(inputs["L0_ffn_be2"][e], f)
        p1 = _pack_T(we1, 512)
        p3 = _pack_T(we3, 512)
        w2t = np.ascontiguousarray(we2.T)
        p2 = _pack_rows(w2t[1:], 4)
        r0 = np.zeros((128, NSH), f)
        r0[0] = w2t[0]
        r0[1] = be2
        blobs.append(np.concatenate([p1, p3, p2, r0], axis=1))
        bias_blobs.append(np.concatenate([_bias_fm(be1, 4), _bias_fm(be3, 4)],
                                         axis=1))
    g["wexp"] = np.stack(blobs)
    g["bexp"] = np.stack(bias_blobs)
    assert g["wexp"].shape == (NE, 128, WE_COLS), g["wexp"].shape
    ws1 = np.asarray(inputs["L0_ffn_Ws1"], f)
    ws3 = np.asarray(inputs["L0_ffn_Ws3"], f)
    ws2 = np.asarray(inputs["L0_ffn_Ws2"], f)
    sp1 = _pack_T(ws1, 1024)
    sp3 = _pack_T(ws3, 1024)
    ws2t = np.ascontiguousarray(ws2.T)
    sp2 = _pack_rows(ws2t[1:], 8)
    sr0 = np.zeros((128, NSH), f)
    sr0[0] = ws2t[0]
    sr0[1] = np.asarray(inputs["L0_ffn_bs2"], f)
    g["wsh"] = np.concatenate([sp1, sp3, sp2, sr0], axis=1)
    g["bsh"] = np.concatenate(
        [_bias_fm(np.asarray(inputs["L0_ffn_bs1"], f), 8),
         _bias_fm(np.asarray(inputs["L0_ffn_bs3"], f), 8)], axis=1)
    assert g["wsh"].shape == (128, WS_COLS), g["wsh"].shape
    import ml_dtypes
    ident = np.eye(128, dtype=f)
    g["identf"] = ident
    g["identb"] = ident.astype(ml_dtypes.bfloat16)
    ut = np.triu(np.ones((128, 128), f))
    g["utf"] = ut
    g["utb"] = ut.astype(ml_dtypes.bfloat16)
    onesb = np.ones((1, 128), f)
    g["onesb"] = onesb.astype(ml_dtypes.bfloat16)
    x = np.asarray(inputs["L0_attn_x"], f)
    return g, x


def build_nc(debug=False, stop_after=None, a3_gate=True, a3_xtspill=True,
             a3_level=9, detect_races=True):
    nc = bass.Bass("TRN2", target_bir_lowering=False,
                   detect_race_conditions=detect_races)
    es = contextlib.ExitStack()
    kb = KB(nc, es)

    dbg_kind = "ExternalOutput" if debug else "Internal"
    d_x = nc.dram_tensor("x", [T, D], F32, kind="ExternalInput").ap()
    d_wq = nc.dram_tensor("wq", [128, 8 * H], F32, kind="ExternalInput").ap()
    d_wk = nc.dram_tensor("wk", [128, 8 * H], F32, kind="ExternalInput").ap()
    d_wv = nc.dram_tensor("wv", [128, 8 * H], F32, kind="ExternalInput").ap()
    d_wo = nc.dram_tensor("wo", [H, D], F32, kind="ExternalInput").ap()
    d_gw = nc.dram_tensor("gw", [128, 8 * 16], F32, kind="ExternalInput").ap()
    d_wexp = nc.dram_tensor("wexp", [NE, 128, WE_COLS], F32, kind="ExternalInput").ap()
    d_wsh = nc.dram_tensor("wsh", [128, WS_COLS], F32, kind="ExternalInput").ap()
    d_bexp = nc.dram_tensor("bexp", [NE, 128, 8], F32, kind="ExternalInput").ap()
    d_bsh = nc.dram_tensor("bsh", [128, 16], F32, kind="ExternalInput").ap()
    d_idf = nc.dram_tensor("identf", [128, 128], F32, kind="ExternalInput").ap()
    d_idb = nc.dram_tensor("identb", [128, 128], BF16, kind="ExternalInput").ap()
    d_utf = nc.dram_tensor("utf", [128, 128], F32, kind="ExternalInput").ap()
    d_utb = nc.dram_tensor("utb", [128, 128], BF16, kind="ExternalInput").ap()
    d_onesb = nc.dram_tensor("onesb", [1, 128], BF16, kind="ExternalInput").ap()
    d_out = nc.dram_tensor("out", [T, D], F32, kind="ExternalOutput").ap()
    d_wec = nc.dram_tensor("wec", [NE, 128, WE_COLS], BF16, kind="Internal").ap()
    d_wshc = nc.dram_tensor("wshc", [128, WS_COLS], BF16, kind="Internal").ap()
    d_x2 = nc.dram_tensor("x2sp", [T, D], F32, kind=dbg_kind).ap()
    d_xt = nc.dram_tensor("xtsp", [8, 128, T], BF16, kind=dbg_kind).ap()
    d_dbg = None
    if debug:
        d_dbg = nc.dram_tensor("dbg", [128, 4096], F32, kind="ExternalOutput").ap()

    chL = [kb.chan("chL0"), kb.chan("chL1")]
    chLg = [kb.chan("chLg0"), kb.chan("chLg1")]
    chCv = [kb.chan("chCv0"), kb.chan("chCv1")]
    chC = kb.chan("chC")
    chX = [kb.chan("chX0"), kb.chan("chX1")]
    chSp = [kb.chan("chSp0"), kb.chan("chSp1")]
    chXt = [kb.chan("chXt0"), kb.chan("chXt1")]
    chXL = [kb.chan("chXL0"), kb.chan("chXL1")]
    chWt = [kb.chan("chWt0"), kb.chan("chWt1")]
    chBe = [kb.chan("chBe0"), kb.chan("chBe1")]
    chOut = [kb.chan("chOut0"), kb.chan("chOut1")]
    chT = kb.chan("chT", "T")
    chV = kb.chan("chV", "V")
    chS = kb.chan("chS", "S")
    chGc = kb.chan("chGc", "G")

    st = nc.sbuf_tensor

    s_x = es.enter_context(st("s_x", [128, 2 * D], F32))
    s_w32 = es.enter_context(st("s_w32", [128, 2 * D], F32))
    s_b16 = es.enter_context(st("s_b16", [128, 2 * D], BF16))
    s_sqs = es.enter_context(st("s_sqs", [128, D], BF16))
    s_gw = es.enter_context(st("s_gw", [128, 8 * 16], F32))
    s_idf = es.enter_context(st("s_idf", [128, 128], F32))
    s_idb = es.enter_context(st("s_idb", [128, 128], BF16))
    s_utf = es.enter_context(st("s_utf", [128, 128], F32))
    s_utb = es.enter_context(st("s_utb", [128, 128], BF16))
    s_onesb = es.enter_context(st("s_onesb", [1, 128], BF16))
    s_wte = es.enter_context(st("s_wte", [128, NT * 16], F32))
    s_hit = es.enter_context(st("s_hit", [128, NT * 16], BF16))
    s_y = es.enter_context(st("s_y", [128, (HALF // 128) * D], F32))
    s_c = es.enter_context(st("s_c", [128, 64], F32))
    s_on1 = es.enter_context(st("s_on1", [128, 1], BF16))
    s_cst = es.enter_context(st("s_cst", [128, 3], F32))

    def C(i):
        return s_c[:, i:i + 1]

    c_epsln = s_cst[:, 0:1]
    c_one = s_cst[:, 1:2]
    c_mhalf = s_cst[:, 2:3]
    t_cst = kb.op("G", lambda g: g.memset(s_cst[:, 0:1], EPS_LN), inc=None)
    t_cst = kb.op("G", lambda g: g.memset(s_cst[:, 1:2], 1.0), inc=None)
    t_cst = kb.op("G", lambda g: g.memset(s_cst[:, 2:3], -0.5), inc=chGc)

    esW = contextlib.ExitStack()
    s_wst = esW.enter_context(st("s_wst", [128, 2 * 3328], F32))
    s_wsb = esW.enter_context(st("s_wsb", [128, 2 * 3328], BF16))
    CH = 3328

    def conv_chunk(i, src_ap, dst_ap, width, eng, tok_prev):
        p = i % 2
        tin = kb.op("G", lambda gp, s=src_ap, pp=p, w=width:
                    gp.dma_start(out=s_wst[:, pp * CH:pp * CH + w], in_=s),
                    after=(tok_prev[p],), inc=chLg[p], k=16)
        tcv = kb.op("G", lambda gp, pp=p, w=width:
                    gp.tensor_copy(out=s_wsb[:, pp * CH:pp * CH + w],
                                   in_=s_wst[:, pp * CH:pp * CH + w]),
                    after=(tin,), inc=chGc)
        tst = kb.op("G", lambda gp, d=dst_ap, pp=p, w=width:
                    gp.dma_start(out=d, in_=s_wsb[:, pp * CH:pp * CH + w]),
                    after=(tcv,), inc=chCv[p], k=16)
        return tst

    prev = {0: None, 1: None}

    def emit_weight_conversions():
        ci = 0
        for e in range(NE):
            off = 0
            while off < WE_COLS:
                w = min(CH, WE_COLS - off)
                tok = conv_chunk(ci, d_wexp[e, :, off:off + w],
                                 d_wec[e, :, off:off + w], w, "G", prev)
                prev[ci % 2] = tok
                ci += 1
                off += w
        off = 0
        while off < WS_COLS:
            w = min(CH, WS_COLS - off)
            tok = conv_chunk(ci, d_wsh[:, off:off + w], d_wshc[:, off:off + w],
                             w, "G", prev)
            prev[ci % 2] = tok
            ci += 1
            off += w
        return (prev[0], prev[1])

    def ld(dst, src):
        return kb.op("Y", lambda y: y.dma_start(out=dst, in_=src), inc=chC, k=16)

    for dst, srcc in ((s_idf[:, :], d_idf), (s_idb[:, :], d_idb),
                      (s_utf[:, :], d_utf), (s_utb[:, :], d_utb),
                      (s_onesb[:, :], d_onesb)):
        ld(dst, srcc)
    t_consts = [ld(s_gw[:, :], d_gw)]

    if stop_after == "W":
        kb.op("G", lambda g: g.nop(),
              after=(prev[0], prev[1], t_consts[0]))
        with nc.Block() as block:
            kb.emit(block)
        esW.close()
        es.close()
        return nc

    esA = contextlib.ExitStack()
    pt = nc.psum_tensor
    s_qf = esA.enter_context(st("s_qf", [128, T], BF16))
    s_vf = esA.enter_context(st("s_vf", [128, T], BF16))
    s_qt = esA.enter_context(st("s_qt", [128, NT * H], F32))
    s_kt = esA.enter_context(st("s_kt", [128, NT * H], BF16))
    s_vt = esA.enter_context(st("s_vt", [128, NT * H], BF16))
    s_ot = esA.enter_context(st("s_ot", [128, NT * H], BF16))
    s_of = esA.enter_context(st("s_of", [128, T], BF16))
    s_lnf = esA.enter_context(st("s_lnf", [128, 2 * D], BF16))
    s_ph = esA.enter_context(st("s_ph", [128, 3 * 128], F32))
    s_S = esA.enter_context(st("s_S", [64, 2 * H], F32))
    s_Sb = esA.enter_context(st("s_Sb", [64, H], BF16))
    s_z = esA.enter_context(st("s_z", [1, 2 * H], F32))
    s_zb = esA.enter_context(st("s_zb", [1, H], BF16))
    s_fm32 = esA.enter_context(st("s_fm32", [128, D], F32))
    s_g16 = esA.enter_context(st("s_g16", [128, 2 * 16], F32))
    s_g16b = esA.enter_context(st("s_g16b", [128, 2 * 16], F32))
    s_gm = esA.enter_context(st("s_gm", [128, 2 * 128], BF16))
    s_wqb = esA.enter_context(st("s_wqb", [128, 8 * H], BF16))
    s_wkb = esA.enter_context(st("s_wkb", [128, 8 * H], BF16))
    s_wvb = esA.enter_context(st("s_wvb", [128, 8 * H], BF16))
    s_wob = esA.enter_context(st("s_wob", [H, D], BF16))

    t_aw = []
    for dsrc, dst in ((d_wq, s_wqb), (d_wk, s_wkb), (d_wv, s_wvb)):
        tin = kb.op("Y", lambda y, s=dsrc: y.dma_start(out=s_wst[:, 0:512], in_=s),
                    after=(prev[0],), inc=chL[0], k=16)
        tcv = kb.op("G", lambda gp, d=dst: gp.tensor_copy(out=d[:, :], in_=s_wst[:, 0:512]),
                    after=(tin,), inc=chGc)
        prev[0] = tcv
        t_aw.append(tcv)
    tin = kb.op("Y", lambda y: y.dma_start(out=s_wst[:64, 0:1024], in_=d_wo),
                after=(prev[0],), inc=chL[0], k=16)
    tcv = kb.op("G", lambda gp: gp.tensor_copy(out=s_wob[:, :], in_=s_wst[:64, 0:1024]),
                after=(tin,), inc=chGc)
    prev[0] = tcv
    t_aw.append(tcv)
    t_consts += t_aw

    t_on1 = kb.op("G", lambda g: g.memset(s_on1[:, :], 1.0), inc=chGc)
    t_S0 = kb.op("G", lambda g: g.memset(s_S[:, :], 0.0), inc=chGc)
    t_S0b = kb.op("G", lambda g: g.memset(s_Sb[:, :], 0.0), inc=chGc)
    t_z0 = kb.op("G", lambda g: g.memset(s_z[:, :], 0.0), inc=chGc)
    t_z0b = kb.op("G", lambda g: g.memset(s_zb[:, :], 0.0), inc=chGc)

    def ln_stats(src_ap, cb, variant, after_src):
        t1 = kb.op("V", lambda v: v.tensor_reduce(
            out=C(cb), in_=src_ap, axis=AX.X, op=ALU.add),
            after=after_src, inc=chV)
        t2 = kb.op("S", lambda s: s.activation(
            out=s_sqs[:, :], in_=src_ap, func=AF.Square, accum_out=C(cb + 1)),
            after=after_src, inc=chS)
        t3 = kb.op("V", lambda v: v.tensor_scalar(
            out=C(cb + 2), in0=C(cb), scalar1=-1.0 / D, scalar2=None,
            op0=ALU.mult), after=(t1,), inc=chV)
        t4 = kb.op("V", lambda v: v.tensor_tensor(
            out=C(cb + 3), in0=C(cb + 2), in1=C(cb + 2), op=ALU.mult),
            after=(t3,), inc=chV)
        t5 = kb.op("V", lambda v: v.scalar_tensor_tensor(
            out=C(cb + 4), in0=C(cb + 1), scalar=1.0 / D, in1=C(cb + 3),
            op0=ALU.mult, op1=ALU.subtract), after=(t2, t4), inc=chV)
        if variant == "expln":
            t6 = kb.op("S", lambda s: s.activation(
                out=C(cb + 5), in_=C(cb + 4), func=AF.Ln, bias=c_epsln),
                after=(t5, t_cst), inc=chS)
            t7 = kb.op("S", lambda s: s.activation(
                out=C(cb + 6), in_=C(cb + 5), func=AF.Exp, scale=c_mhalf),
                after=(t6,), inc=chS)
            t8 = kb.op("S", lambda s: s.activation(
                out=C(cb + 7), in_=C(cb + 2), func=AF.Identity,
                scale=C(cb + 6)), after=(t7,), inc=chS)
        else:
            t6 = kb.op("S", lambda s: s.activation(
                out=C(cb + 5), in_=C(cb + 4), func=AF.Sqrt, bias=c_epsln),
                after=(t5, t_cst), inc=chS)
            t7 = kb.op("V", lambda v: v.reciprocal(
                out=C(cb + 6), in_=C(cb + 5)), after=(t6,), inc=chV)
            t8 = kb.op("V", lambda v: v.tensor_scalar(
                out=C(cb + 7), in0=C(cb + 2), scalar1=C(cb + 6), scalar2=None,
                op0=ALU.mult), after=(t7,), inc=chV)
        return C(cb + 6), C(cb + 7), (t7, t8)

    def phi(src_ap, dst_ap, w, after, np_=128):
        ta = kb.op("V", lambda v: v.tensor_scalar(
            out=s_ph[0:np_, 0:w], in0=src_ap, scalar1=0.0, scalar2=None,
            op0=ALU.min), after=after, inc=chV)
        tb = kb.op("S", lambda s: s.activation(
            out=s_ph[0:np_, 128:128 + w], in_=s_ph[0:np_, 0:w], func=AF.Exp),
            after=(ta,), inc=chS)
        tc = kb.op("V", lambda v: v.tensor_scalar(
            out=s_ph[0:np_, 256:256 + w], in0=src_ap, scalar1=0.0,
            scalar2=None, op0=ALU.max), after=after, inc=chV)
        td = kb.op("V", lambda v: v.tensor_tensor(
            out=dst_ap, in0=s_ph[0:np_, 128:128 + w],
            in1=s_ph[0:np_, 256:256 + w],
            op=ALU.add), after=(tb, tc), inc=chV)
        return td

    esA1 = contextlib.ExitStack()
    p_trA = esA1.enter_context(pt("p_trA", [128, D], BF16))
    p_qkv = esA1.enter_context(pt("p_qkv", [128, 512], F32))
    p_G = esA1.enter_context(pt("p_G", [128, 128], F32))
    p_oc = esA1.enter_context(pt("p_oc", [128, H], F32))
    p_zc = esA1.enter_context(pt("p_zc", [128, H], F32))
    p_z = esA1.enter_context(pt("p_z", [1, H], F32))
    p_S = esA1.enter_context(pt("p_S", [64, H], F32))
    p_of = esA1.enter_context(pt("p_of", [64, 128], BF16))
    g_x = [None, None]
    g_trA = None
    g_qkv = None
    tok_qf = [None] * NT
    tok_vf = [None] * NT
    tok_qt = [None] * NT
    tok_kt = [None] * NT
    tok_vt = [None] * NT
    g_lnf = [None, None]
    a1_tr_last = None

    def a1_tile(i):
        nonlocal g_trA, g_qkv, a1_tr_last
        p = i % 2
        cb = 32 * p
        xsl = s_x[:, p * D:(p + 1) * D]
        tx = kb.op("Y", lambda y, i=i, xsl=xsl: y.dma_start(
            out=xsl, in_=d_x[i * 128:(i + 1) * 128, :]),
            after=(g_x[p],) + (tuple(t_consts) if i < 2 else ()),
            inc=chX[p], k=16)
        rstd, mb, tln = ln_stats(xsl, cb, "expln", (tx,))
        lnsl = s_b16[:, p * D:(p + 1) * D]
        tln1 = kb.op("S", lambda s, xsl=xsl, lnsl=lnsl, rstd=rstd, mb=mb:
                     s.activation(out=lnsl, in_=xsl, func=AF.Identity,
                                  scale=rstd, bias=mb),
                     after=tln + (g_lnf[p],), inc=chS)
        g_x[p] = tln1
        ttr = None
        for k in range(8):
            ttr = kb.op("T", lambda t, k=k, lnsl=lnsl: t.transpose(
                p_trA[:, k * 128:(k + 1) * 128], lnsl[:, k * 128:(k + 1) * 128],
                s_idb[:, :]),
                after=(tln1, g_trA) if k == 0 else (),
                inc=chT if k == 7 else None)
        a1_tr_last = ttr
        fmsl = s_lnf[:, p * D:(p + 1) * D]
        tfm = kb.op("V", lambda v, fmsl=fmsl: v.tensor_copy(
            out=fmsl, in_=p_trA[:, :]), after=(ttr,), inc=chV)
        g_trA = tfm
        tmm = None
        forms = [
            lambda t, k, fmsl: t.matmul(
                p_qkv[0:64, 0:128], s_wqb[:, k * H:(k + 1) * H],
                fmsl[:, k * 128:(k + 1) * 128], start=(k == 0), stop=(k == 7)),
            lambda t, k, fmsl: t.matmul(
                p_qkv[0:64, 128:256], s_wvb[:, k * H:(k + 1) * H],
                fmsl[:, k * 128:(k + 1) * 128], start=(k == 0), stop=(k == 7)),
            lambda t, k, fmsl: t.matmul(
                p_qkv[:, 256:320], fmsl[:, k * 128:(k + 1) * 128],
                s_wqb[:, k * H:(k + 1) * H], start=(k == 0), stop=(k == 7)),
            lambda t, k, fmsl: t.matmul(
                p_qkv[:, 320:384], fmsl[:, k * 128:(k + 1) * 128],
                s_wkb[:, k * H:(k + 1) * H], start=(k == 0), stop=(k == 7)),
            lambda t, k, fmsl: t.matmul(
                p_qkv[:, 384:448], fmsl[:, k * 128:(k + 1) * 128],
                s_wvb[:, k * H:(k + 1) * H], start=(k == 0), stop=(k == 7)),
        ]
        for fi, fn in enumerate(forms):
            for k in range(8):
                af = (tfm, g_qkv) if (fi == 0 and k == 0) else ()
                tmm = kb.op("T",
                            lambda t, fn=fn, k=k, fmsl=fmsl: fn(t, k, fmsl),
                            after=af,
                            inc=chT if (fi == 4 and k == 7) else None)
        g_lnf[p] = tmm
        tok_qf[i] = phi(p_qkv[0:64, 0:128], s_qf[0:64, i * 128:(i + 1) * 128],
                        128, (tmm,), np_=64)
        tok_qt[i] = phi(p_qkv[:, 256:320], s_qt[:, i * H:(i + 1) * H],
                        64, (tok_qf[i],))
        tok_kt[i] = phi(p_qkv[:, 320:384], s_kt[:, i * H:(i + 1) * H],
                        64, (tok_qt[i],))
        tok_vf[i] = kb.op("V", lambda v, i=i: v.tensor_copy(
            out=s_vf[0:64, i * 128:(i + 1) * 128], in_=p_qkv[0:64, 128:256]),
            after=(tmm,), inc=chV)
        tok_vt[i] = kb.op("V", lambda v, i=i: v.tensor_copy(
            out=s_vt[:, i * H:(i + 1) * H], in_=p_qkv[:, 384:448]),
            after=(tmm,), inc=chV)
        g_qkv = tok_vt[i]

    g_G = None
    g_gm = [None, None]
    g_oc = None
    g_zc = None
    g_of = None
    t_Scp = t_S0b
    t_zcp = t_z0b
    t_dSmm = None
    t_zsum = None
    tok_o = [None] * NT

    def a2_chunk(c):
        nonlocal g_G, g_oc, g_zc, g_of, t_Scp, t_zcp, t_dSmm, t_zsum
        p = c % 2
        cbase = 40 + 4 * p
        tG = kb.op("T", lambda t, c=c: t.matmul(
            p_G[:, :], s_vf[0:64, c * 128:(c + 1) * 128],
            s_qf[0:64, c * 128:(c + 1) * 128], start=True, stop=True),
            after=(tok_vf[c], tok_qf[c], g_G), inc=chT)
        gm = s_gm[:, p * 128:(p + 1) * 128]
        tGm = kb.op("V", lambda v, gm=gm: v.tensor_tensor(
            out=gm, in0=p_G[:, :], in1=s_utf[:, :], op=ALU.mult),
            after=(tG, g_gm[p]), inc=chV)
        g_G = tGm
        to1 = kb.op("T", lambda t, c=c, gm=gm: t.matmul(
            p_oc[:, :], gm, s_kt[:, c * H:(c + 1) * H], start=True, stop=False),
            after=(tGm, tok_kt[c], g_oc), inc=None)
        to2 = kb.op("T", lambda t, c=c: t.matmul(
            p_oc[:, :], s_qf[0:64, c * 128:(c + 1) * 128], s_Sb[:, :],
            start=False, stop=True), after=(t_Scp,), inc=chT)
        g_gm[p] = to2
        tz1 = kb.op("T", lambda t, c=c: t.matmul(
            p_zc[:, :], s_utb[:, :], s_kt[:, c * H:(c + 1) * H],
            start=True, stop=False), after=(g_zc, tok_kt[c]), inc=None)
        tz2 = kb.op("T", lambda t: t.matmul(
            p_zc[:, :], s_onesb[0:1, :], s_zb[0:1, :],
            start=False, stop=True), after=(t_zcp, t_on1), inc=chT)
        t_zsum = kb.op("T", lambda t, c=c: t.matmul(
            p_z[0:1, :], s_on1[:, :], s_kt[:, c * H:(c + 1) * H],
            start=True, stop=True),
            after=(t_zcp,), inc=chT)
        t_dSmm = kb.op("T", lambda t, c=c: t.matmul(
            p_S[:, :], s_vt[:, c * H:(c + 1) * H], s_kt[:, c * H:(c + 1) * H],
            start=True, stop=True),
            after=(tok_vt[c], t_Scp, to2), inc=chT)
        td1 = kb.op("V", lambda v, c=c: v.tensor_tensor(
            out=s_ph[:, 0:H], in0=p_zc[:, :], in1=s_qt[:, c * H:(c + 1) * H],
            op=ALU.mult), after=(tz2, tok_qt[c]), inc=chV)
        g_zc = td1
        td2 = kb.op("V", lambda v, cb=cbase: v.tensor_reduce(
            out=C(cb), in_=s_ph[:, 0:H], axis=AX.X, op=ALU.add),
            after=(td1,), inc=chV)
        td3 = kb.op("V", lambda v, cb=cbase: v.tensor_scalar(
            out=C(cb + 1), in0=C(cb), scalar1=1e-6, scalar2=None,
            op0=ALU.max), after=(td2,), inc=chV)
        td4 = kb.op("V", lambda v, cb=cbase: v.reciprocal(
            out=C(cb + 2), in_=C(cb + 1)), after=(td3,), inc=chV)
        to3 = kb.op("V", lambda v, c=c, cb=cbase: v.tensor_scalar(
            out=s_ot[:, c * H:(c + 1) * H], in0=p_oc[:, :],
            scalar1=C(cb + 2), scalar2=None, op0=ALU.mult),
            after=(to2, td4), inc=chV)
        g_oc = to3
        pc, pn = c % 2, (c + 1) % 2
        tSa = kb.op("V", lambda v, pc=pc, pn=pn: v.tensor_tensor(
            out=s_S[:, pn * H:(pn + 1) * H], in0=p_S[:, :],
            in1=s_S[:, pc * H:(pc + 1) * H], op=ALU.add),
            after=(t_dSmm, t_S0, t_Scp), inc=chV)
        t_Scp = kb.op("V", lambda v, pn=pn: v.tensor_copy(
            out=s_Sb[:, :], in_=s_S[:, pn * H:(pn + 1) * H]),
            after=(tSa, t_S0b), inc=chV)
        tza = kb.op("V", lambda v, pc=pc, pn=pn: v.tensor_tensor(
            out=s_z[0:1, pn * H:(pn + 1) * H], in0=p_z[0:1, :],
            in1=s_z[0:1, pc * H:(pc + 1) * H], op=ALU.add),
            after=(t_zsum, tz2, t_z0, t_zcp), inc=chV)
        t_zcp = kb.op("V", lambda v, pn=pn: v.tensor_copy(
            out=s_zb[0:1, :], in_=s_z[0:1, pn * H:(pn + 1) * H]),
            after=(tza, t_z0b), inc=chV)
        tof = kb.op("T", lambda t, c=c: t.transpose(
            p_of[:, :], s_ot[:, c * H:(c + 1) * H], s_idb[:, :]),
            after=(to3, g_of), inc=chT)
        tok_o[c] = kb.op("V", lambda v, c=c: v.tensor_copy(
            out=s_of[0:64, c * 128:(c + 1) * 128], in_=p_of[:, :]),
            after=(tof,), inc=chV)
        g_of = tok_o[c]

    for i in range(NT + 1):
        if i < NT:
            a1_tile(i)
        if i >= 1:
            a2_chunk(i - 1)
    esA1.close()

    esA3 = contextlib.ExitStack()
    p_at = [esA3.enter_context(pt("p_at0", [128, D], F32)),
            esA3.enter_context(pt("p_at1", [128, D], F32))]
    p_tr2 = esA3.enter_context(pt("p_tr2", [128, D], F32))
    p_gate = esA3.enter_context(pt("p_gate", [128, 16], F32))
    s_at = esA3.enter_context(st("s_at", [128, 2 * D], F32))
    s_glog = esA3.enter_context(st("s_glog", [128, 2 * 16], F32))
    g_at = [None, None]
    g_atS = [None, None]
    g_x = [g_x[0], g_x[1]]
    g_tr2 = None
    g_fm32 = None
    g_gate = None
    g_b16 = [g_lnf[0], g_lnf[1]]
    tok_x2sp = [None] * NT
    tok_xtsp = [None] * NT
    g_w32 = [None, None]
    for i in range(NT):
        p = i % 2
        cb = 32 * p
        atp = p_at[p]
        t1 = kb.op("T", lambda t, i=i, atp=atp: t.matmul(
            atp[:, 0:512], s_of[0:64, i * 128:(i + 1) * 128], s_wob[:, 0:512],
            start=True, stop=True),
            after=(tok_o[i], g_at[p]) + ((g_of, g_oc) if i < 2 else ()),
            inc=None)
        t2 = kb.op("T", lambda t, i=i, atp=atp: t.matmul(
            atp[:, 512:1024], s_of[0:64, i * 128:(i + 1) * 128],
            s_wob[:, 512:1024], start=True, stop=True), after=(), inc=chT)
        if a3_level < 2:
            tev = kb.op("V", lambda v, atp=atp, p=p: v.tensor_copy(
                out=s_w32[:, p * D:(p + 1) * D], in_=atp[:, :]),
                after=(t2, g_at[p], g_w32[p]), inc=chV)
            g_at[p] = tev
            g_w32[p] = tev
            tok_x2sp[i] = tok_xtsp[i] = tw2 = tev
            continue
        atsl = s_at[:, p * D:(p + 1) * D]
        tat = kb.op("V", lambda v, atp=atp, atsl=atsl: v.tensor_copy(
            out=atsl, in_=atp[:, :]), after=(t2, g_at[p], g_atS[p]), inc=chV)
        g_at[p] = tat
        if 11 <= a3_level <= 15:
            toks = []
            t1 = kb.op("V", lambda v, atsl=atsl, cb=cb: v.tensor_reduce(
                out=C(cb), in_=atsl, axis=AX.X, op=ALU.add),
                after=(tat,), inc=chV)
            toks.append(t1)
            if a3_level >= 12:
                t2s = kb.op("S", lambda s, atsl=atsl, cb=cb: s.activation(
                    out=s_sqs[:, :], in_=atsl, func=AF.Square,
                    accum_out=C(cb + 1)), after=(tat,), inc=chS)
                toks.append(t2s)
            if a3_level >= 13:
                t3 = kb.op("V", lambda v, cb=cb: v.tensor_scalar(
                    out=C(cb + 2), in0=C(cb), scalar1=1.0 / D, scalar2=None,
                    op0=ALU.mult), after=(t1,), inc=chV)
                t4 = kb.op("V", lambda v, cb=cb: v.tensor_tensor(
                    out=C(cb + 3), in0=C(cb + 2), in1=C(cb + 2), op=ALU.mult),
                    after=(t3,), inc=chV)
                t5 = kb.op("V", lambda v, cb=cb: v.scalar_tensor_tensor(
                    out=C(cb + 4), in0=C(cb + 1), scalar=1.0 / D,
                    in1=C(cb + 3), op0=ALU.mult, op1=ALU.subtract),
                    after=(t2s, t4), inc=chV)
                t8 = kb.op("V", lambda v, cb=cb: v.tensor_scalar(
                    out=C(cb + 7), in0=C(cb + 2), scalar1=C(cb + 4),
                    scalar2=-1.0, op0=ALU.mult, op1=ALU.mult),
                    after=(t5,), inc=chV)
                toks += [t8]
            if a3_level >= 14:
                t6 = kb.op("S", lambda s, cb=cb: s.activation(
                    out=C(cb + 5), in_=C(cb + 4), func=AF.Ln, bias=c_epsln),
                    after=(t5, t_cst), inc=chS)
                t7 = kb.op("S", lambda s, cb=cb: s.activation(
                    out=C(cb + 6), in_=C(cb + 5), func=AF.Exp, scale=c_mhalf),
                    after=(t6,), inc=chS)
                toks += [t7]
            tev = kb.op("V", lambda v, atsl=atsl, p=p: v.tensor_copy(
                out=s_w32[:, p * D:(p + 1) * D], in_=atsl),
                after=tuple(toks) + (g_w32[p],), inc=chV)
            g_atS[p] = tev
            g_w32[p] = tev
            tok_x2sp[i] = tok_xtsp[i] = tw2 = tev
            continue
        rstd, mb, tln = ln_stats(atsl, cb, "expln", (tat,))
        xsl = s_x[:, p * D:(p + 1) * D]
        tx = kb.op("Y", lambda y, i=i, xsl=xsl: y.dma_start(
            out=xsl, in_=d_x[i * 128:(i + 1) * 128, :]),
            after=(g_x[p],), inc=chX[p], k=16)
        wsl = s_w32[:, p * D:(p + 1) * D]
        txf = kb.op("S", lambda s, xsl=xsl, wsl=wsl, mb=mb: s.activation(
            out=wsl, in_=xsl, func=AF.Identity, bias=mb),
            after=(tx, tln[1], g_w32[p]), inc=chS)
        tx2 = kb.op("V", lambda v, atsl=atsl, xsl=xsl, wsl=wsl, rstd=rstd:
                    v.scalar_tensor_tensor(
                        out=xsl, in0=atsl, scalar=rstd, in1=wsl,
                        op0=ALU.mult, op1=ALU.add),
                    after=(txf, tln[0]), inc=chV)
        g_atS[p] = tx2
        tok_x2sp[i] = kb.op("Y", lambda y, i=i, xsl=xsl: y.dma_start(
            out=d_x2[i * 128:(i + 1) * 128, :], in_=xsl),
            after=(tx2,), inc=chSp[p], k=16)
        if a3_level < 3:
            tok_xtsp[i] = tw2 = tok_x2sp[i]
            g_x[p] = tok_x2sp[i]
            continue
        rstd2, mb2, tln2 = ln_stats(xsl, cb + 8, "expln", (tx2,))
        tln2f = kb.op("S", lambda s, xsl=xsl, wsl=wsl, rstd2=rstd2, mb2=mb2:
                      s.activation(out=wsl, in_=xsl, func=AF.Identity,
                                   scale=rstd2, bias=mb2),
                      after=tln2 + (txf,), inc=chS)
        g_x[p] = tok_x2sp[i]
        if a3_level < 4:
            tok_xtsp[i] = tw2 = tln2f
            g_w32[p] = tln2f
            continue
        ttr = None
        for k in range(8):
            ttr = kb.op("T", lambda t, k=k, wsl=wsl: t.transpose(
                p_tr2[:, k * 128:(k + 1) * 128], wsl[:, k * 128:(k + 1) * 128],
                s_idf[:, :]),
                after=(tln2f, g_tr2) if k == 0 else (),
                inc=chT if k == 7 else None)
        g_w32[p] = ttr
        tfm32 = kb.op("V", lambda v: v.tensor_copy(
            out=s_fm32[:, :], in_=p_tr2[:, :]), after=(ttr, g_fm32), inc=chV)
        bsl = s_b16[:, p * D:(p + 1) * D]
        txtb = kb.op("V", lambda v, bsl=bsl: v.tensor_copy(
            out=bsl, in_=p_tr2[:, :]),
            after=(ttr, g_b16[p], tfm32), inc=chV)
        g_tr2 = txtb
        if a3_xtspill:
            tok_xtsp[i] = kb.op("Y", lambda y, i=i, bsl=bsl: y.dma_start(
                out=d_xt[:, :, i * 128:(i + 1) * 128].rearrange("k p n -> p k n"),
                in_=bsl.rearrange("p (k n) -> p k n", k=8)),
                after=(txtb,), inc=chXt[p], k=16)
        else:
            tok_xtsp[i] = txtb
        g_b16[p] = tok_xtsp[i]
        if not a3_gate:
            tw2 = txtb
            continue
        tg = None
        for k in range(8):
            tg = kb.op("T", lambda t, k=k: t.matmul(
                p_gate[:, :], s_fm32[:, k * 128:(k + 1) * 128],
                s_gw[:, k * 16:(k + 1) * 16], start=(k == 0), stop=(k == 7)),
                after=(tfm32, g_gate) if k == 0 else (),
                inc=chT if k == 7 else None)
        g_fm32 = tg
        gsl = s_glog[:, p * 16:(p + 1) * 16]
        tgc = kb.op("V", lambda v, gsl=gsl: v.tensor_copy(
            out=gsl, in_=p_gate[:, :]), after=(tg, g_gate), inc=chV)
        tmx = kb.op("V", lambda v, cb=cb, gsl=gsl: v.reduce_max(
            out=C(cb + 16), in_=gsl, axis=AX.X), after=(tgc,), inc=chV)
        tnm = kb.op("V", lambda v, cb=cb: v.tensor_scalar(
            out=C(cb + 17), in0=C(cb + 16), scalar1=-1.0, scalar2=None,
            op0=ALU.mult), after=(tmx,), inc=chV)
        esl = s_g16[:, p * 16:(p + 1) * 16]
        bsl2 = s_g16b[:, p * 16:(p + 1) * 16]
        te = kb.op("S", lambda s, cb=cb, esl=esl, gsl=gsl: s.activation(
            out=esl, in_=gsl, func=AF.Exp, bias=C(cb + 17)),
            after=(tnm,), inc=chS)
        g_gate = tgc
        tse = kb.op("V", lambda v, cb=cb, esl=esl: v.tensor_reduce(
            out=C(cb + 18), in_=esl, axis=AX.X, op=ALU.add),
            after=(te,), inc=chV)
        trs = kb.op("V", lambda v, cb=cb: v.reciprocal(
            out=C(cb + 19), in_=C(cb + 18)), after=(tse,), inc=chV)
        tm1 = kb.op("V", lambda v, cb=cb, esl=esl: v.reduce_max(
            out=C(cb + 20), in_=esl, axis=AX.X), after=(te,), inc=chV)
        teq = kb.op("V", lambda v, cb=cb, esl=esl, bsl2=bsl2: v.tensor_scalar(
            out=bsl2, in0=esl, scalar1=C(cb + 20), scalar2=0.0,
            op0=ALU.subtract, op1=ALU.is_ge), after=(tm1,), inc=chV)
        tmk = kb.op("V", lambda v, esl=esl, bsl2=bsl2: v.scalar_tensor_tensor(
            out=bsl2, in0=bsl2, scalar=-1e30, in1=esl,
            op0=ALU.mult, op1=ALU.add), after=(teq,), inc=chV)
        tm2 = kb.op("V", lambda v, cb=cb, bsl2=bsl2: v.reduce_max(
            out=C(cb + 21), in_=bsl2, axis=AX.X), after=(tmk,), inc=chV)
        th = kb.op("V", lambda v, cb=cb, esl=esl, i=i: v.tensor_scalar(
            out=s_hit[:, i * 16:(i + 1) * 16], in0=esl, scalar1=C(cb + 21),
            scalar2=None, op0=ALU.is_ge), after=(tm2,), inc=chV)
        tw1 = kb.op("V", lambda v, cb=cb, esl=esl, bsl2=bsl2: v.tensor_scalar(
            out=bsl2, in0=esl, scalar1=C(cb + 19), scalar2=None,
            op0=ALU.mult), after=(trs, tm2), inc=chV)
        tw2 = kb.op("V", lambda v, i=i, bsl2=bsl2: v.tensor_tensor(
            out=s_wte[:, i * 16:(i + 1) * 16], in0=bsl2,
            in1=s_hit[:, i * 16:(i + 1) * 16], op=ALU.mult),
            after=(th, tw1), inc=chV)
    esA3.close()
    esA.close()
    tok_wcache = emit_weight_conversions()
    a_done = (tok_x2sp[NT - 2], tok_x2sp[NT - 1], tok_xtsp[NT - 2],
              tok_xtsp[NT - 1], tw2) + tok_wcache

    if stop_after == "A":
        kb.op("G", lambda g: g.nop(), after=a_done)
        with nc.Block() as block:
            kb.emit(block)
        esW.close()
        es.close()
        return nc

    esW.close()
    esF = contextlib.ExitStack()
    s_we = esF.enter_context(st("s_we", [128, 2 * WE_COLS], BF16))
    s_beb = esF.enter_context(st("s_beb", [128, 2 * 8], F32))
    s_xtb = esF.enter_context(st("s_xtb", [128, 2 * 8 * 512], BF16))
    s_sf = esF.enter_context(st("s_sf", [128, NG * 1024], BF16))
    s_a = esF.enter_context(st("s_a", [128, 2048], BF16))
    s_s3b = esF.enter_context(st("s_s3b", [128, 2048], BF16))
    s_sq = esF.enter_context(st("s_sq", [128, 2 * 2048], BF16))
    s_sums = esF.enter_context(st("s_sums", [1, NG * GRP], F32))
    s_timeb = esF.enter_context(st("s_timeb", [2, NG * GRP], BF16))
    s_onef = esF.enter_context(st("s_onef", [128, 1], F32))
    s_s1f = esF.enter_context(st("s_s1f", [128, 1024], F32))
    s_wshb = None

    t_onef = kb.op("G", lambda g: g.memset(s_onef[:, :], 1.0),
                   after=a_done, inc=chGc)
    t_ones_time = kb.op("G", lambda g: g.memset(s_timeb[:, :], 1.0),
                        after=a_done, inc=chGc)

    g_y = [None] * (HALF // 128)
    g_we = [None, None]
    g_xt = [None, None]
    g_zz = [None, None]
    g_o2 = [(None, None)] * 4
    g_sfgrp = [None] * NG
    g_time = None
    g_beb = [None, None]
    g_ps1 = [None, None]
    g_ps3 = [None, None]
    g_s1f = [None]
    g_bscr = [None, None]
    g_sq = [None, None]

    def emit_wload(pe, we_cols, d_wsrc, d_bsrc, first=False):
        wsl = s_we[:, pe * WE_COLS: pe * WE_COLS + we_cols]
        w_after = ((g_we[0], g_we[1]) if we_cols > WE_COLS else (g_we[pe],))
        tw = kb.op("Y", lambda y: y.dma_start(out=wsl, in_=d_wsrc),
                   after=w_after + (a_done if first else ()),
                   inc=chWt[pe], k=16)
        if d_bsrc is not None:
            bsl = s_beb[:, pe * 8:(pe + 1) * 8]
            tb = kb.op("Y", lambda y, bsl=bsl: y.dma_start(out=bsl, in_=d_bsrc),
                       after=(g_beb[pe],), inc=chBe[pe], k=16)
        else:
            bsl = s_beb[:, 0:16]
            tb = kb.op("Y", lambda y, bsl=bsl: y.dma_start(out=bsl, in_=d_bsh),
                       after=(g_beb[0], g_beb[1]), inc=chBe[0], k=16)
        kb.wcache[(pe, we_cols)] = (tw, tb)
    out_toks = []

    def ffn_expert(h, e, pe, first, ps1_of, ps3_of, p_o2, we_cols, w1o, w3o,
                   w2o, r0o, nit, nkt, d_wsrc, d_bsrc, stride1, wte_e,
                   groups=tuple(range(NG)), sfs=1024, skip_w=False,
                   s1_dbuf=True, prefetch=None, grp=GRP):
        nonlocal g_time
        HB = h * HALF
        g0 = groups[0]
        wsl = s_we[:, pe * WE_COLS: pe * WE_COLS + we_cols]
        if not skip_w:
            emit_wload(pe, we_cols, d_wsrc, d_bsrc, first)
        bsl = s_beb[:, 0:16] if d_bsrc is None else s_beb[:, pe * 8:(pe + 1) * 8]
        tw, tb = kb.wcache[(pe, we_cols)]
        pending_sum = None
        last_smul = None
        t_silu = [None] * NG
        t_s3b = [None] * NG
        t_smul = [None] * NG
        t_sumcp = [None] * NG
        for g in groups:
            pg = g % 2
            pidx = (g % 2) if s1_dbuf else 0
            s1r = ps1_of(g)
            s3r = ps3_of(g)
            sfo = (g - g0) * sfs
            xsl = s_xtb[:, pg * 8 * grp:(pg + 1) * 8 * grp]
            txt = kb.op("Y", lambda y, g=g, xsl=xsl, HB=HB: y.dma_start(
                out=xsl,
                in_=d_xt[:, :, HB + g * grp:HB + (g + 1) * grp]
                .rearrange("k p n -> p k n")),
                after=(g_xt[pg],) + (a_done if first and g < 2 else ()),
                inc=chXL[pg], k=16)
            if g == groups[0] and prefetch is not None:
                prefetch()
            tmm1 = tmm3 = None
            for it in range(nit):
                for k in range(8):
                    af = ((tw, txt) + tuple(g_ps1[pidx] or ())
                          if (it == 0 and k == 0) else ())
                    tmm1 = kb.op("T", lambda t, it=it, k=k, xsl=xsl, s1r=s1r:
                                 t.matmul(
                        s1r[:, it * grp:(it + 1) * grp],
                        wsl[:, w1o + k * stride1 + it * 128:
                            w1o + k * stride1 + it * 128 + 128],
                        xsl[:, k * grp:(k + 1) * grp],
                        start=(k == 0), stop=(k == 7)), after=af,
                        inc=chT if (it == nit - 1 and k == 7) else None)
            for it in range(nit):
                for k in range(8):
                    af = (g_ps3[pidx],) if (it == 0 and k == 0) else ()
                    tmm3 = kb.op("T", lambda t, it=it, k=k, xsl=xsl, s3r=s3r:
                                 t.matmul(
                        s3r[:, it * grp:(it + 1) * grp],
                        wsl[:, w3o + k * stride1 + it * 128:
                            w3o + k * stride1 + it * 128 + 128],
                        xsl[:, k * grp:(k + 1) * grp],
                        start=(k == 0), stop=(k == 7)), after=af,
                        inc=chT if (it == nit - 1 and k == 7) else None)
            g_xt[pg] = tmm3
            ipc = max(1, 1024 // grp)
            for it0 in range(0, nit, ipc):
                nb = min(ipc, nit - it0)
                tcp1 = kb.op("V", lambda v, it0=it0, nb=nb, s1r=s1r:
                             v.tensor_copy(
                    out=s_s1f[:, 0:nb * grp],
                    in_=s1r[:, it0 * grp:(it0 + nb) * grp]),
                    after=(tmm1, g_s1f[0]), inc=chV)
                for it in range(it0, it0 + nb):
                    t_silu[g] = kb.op("S", lambda s, it=it, it0=it0:
                                      s.activation(
                        out=s_a[:, it * grp:(it + 1) * grp],
                        in_=s_s1f[:, (it - it0) * grp:(it - it0 + 1) * grp],
                        func=AF.Silu, bias=bsl[:, it:it + 1]),
                        after=(tcp1, tb) if it == it0 else (), inc=chS)
                g_s1f[0] = t_silu[g]
            g_ps1[pidx] = (tcp1,)
            for it in range(nit):
                t_s3b[g] = kb.op("V", lambda v, it=it, nit=nit, s3r=s3r:
                                 v.tensor_scalar(
                    out=s_s3b[:, it * grp:(it + 1) * grp],
                    in0=s3r[:, it * grp:(it + 1) * grp],
                    scalar1=bsl[:, nit + it:nit + it + 1], scalar2=None,
                    op0=ALU.add),
                    after=(tmm3, tb, last_smul) if it == 0 else (), inc=chV)
            sf_guard = tuple(g_sfgrp) if sfs != 1024 else (g_sfgrp[g],)
            t_smul[g] = kb.op("V", lambda v, sfo=sfo, nit=nit: v.tensor_tensor(
                out=s_sf[:, sfo:sfo + nit * grp],
                in0=s_a[:, 0:nit * grp], in1=s_s3b[:, 0:nit * grp],
                op=ALU.mult),
                after=(t_silu[g], t_s3b[g]) + sf_guard, inc=chV)
            tsq = kb.op("S", lambda s, sfo=sfo, nit=nit, pg=pg: s.activation(
                out=s_sq[:, pg * 2048:pg * 2048 + nit * grp],
                in_=s_sf[:, sfo:sfo + nit * grp], func=AF.Square),
                after=(t_smul[g], g_sq[pg]), inc=chS)
            def emit_sum(g=g, s3r=s3r, tsq=tsq, ts3b=t_s3b[g], pidx=pidx):
                tsum = None
                for it in range(nit):
                    tsum = kb.op("T", lambda t, it=it, s3r=s3r, g=g: t.matmul(
                        s3r[0:1, 0:grp], s_on1[:, :],
                        s_sq[:, (g % 2) * 2048 + it * grp:
                             (g % 2) * 2048 + (it + 1) * grp],
                        start=(it == 0), stop=(it == nit - 1)),
                        after=(tsq, ts3b) if it == 0 else (),
                        inc=chT if it == nit - 1 else None)
                g_sq[g % 2] = tsum
                t_sumcp[g] = kb.op("V", lambda v, g=g, s3r=s3r: v.tensor_copy(
                    out=s_sums[0:1, g * grp:(g + 1) * grp],
                    in_=s3r[0:1, 0:grp]),
                    after=(tsum, g_time), inc=chV)
                g_ps3[pidx] = t_sumcp[g]

            last_smul = t_smul[g]
            if not s1_dbuf:
                emit_sum()
            else:
                if pending_sum is not None:
                    pending_sum()
                pending_sum = emit_sum
        if pending_sum is not None:
            pending_sum()
            pending_sum = None
        glast = groups[-1]
        tq = kb.op("S", lambda s: s.activation(
            out=s_timeb[0:1, g0 * grp:(glast + 1) * grp],
            in_=s_sums[0:1, g0 * grp:(glast + 1) * grp],
            func=AF.Sqrt, bias=c_one[0:1, :]),
            after=(t_sumcp[glast], t_ones_time, g_time, t_cst), inc=chS)
        t_last_k2 = None
        t_last_w2 = None
        for g in groups:
            sfo = (g - g0) * sfs
            ntt = grp // 128
            for tt in range(ntt):
                lt = g * ntt + tt
                gi = h * (HALF // 128) + lt
                po = lt % 4
                cb = 32 * (lt % 2)
                pob = p_o2[po]
                tmm = None
                for kt in range(nkt):
                    for nh in range(2):
                        wnh = 512 if nh == 0 else 511
                        af = ()
                        if kt == 0 and nh == 0:
                            af = (t_smul[g], g_o2[po][0], g_o2[po][1],
                                  t_sumcp[glast])
                        tmm = kb.op("T", lambda t, sfo=sfo, tt=tt, kt=kt, nh=nh,
                                    wnh=wnh, pob=pob: t.matmul(
                            pob(nh * 512, nh * 512 + wnh),
                            s_sf[:, sfo + kt * grp + tt * 128:
                                 sfo + kt * grp + tt * 128 + 128],
                            wsl[:, w2o + kt * NSH + nh * 512:
                                w2o + kt * NSH + nh * 512 + wnh],
                            start=(kt == 0), stop=False), after=af)
                for nh in range(2):
                    wnh = 512 if nh == 0 else 511
                    tmm = kb.op("T", lambda t, g=g, tt=tt, nh=nh, wnh=wnh,
                                pob=pob: t.matmul(
                        pob(nh * 512, nh * 512 + wnh),
                        s_timeb[0:2, g * grp + tt * 128:g * grp + tt * 128 + 128],
                        wsl[0:2, r0o + nh * 512:r0o + nh * 512 + wnh],
                        start=False, stop=True),
                        after=(tq,) if nh == 0 else (),
                        inc=chT if nh == 1 else None)
                t_last_k2 = tmm
                g_sfgrp[g] = tmm
                t_last_w2 = tmm
                pz = lt % 2
                zsl = s_w32[:, pz * D:(pz + 1) * D]
                ysl = s_y[:, lt * D:(lt + 1) * D]
                bscr = s_b16[:, pz * D:pz * D + NSH]
                tob = kb.op("V", lambda v, pob=pob, bscr=bscr:
                            v.tensor_copy(out=bscr, in_=pob(0, NSH)),
                            after=(tmm, g_bscr[pz]), inc=chV)
                tq1 = kb.op("S", lambda s, bscr=bscr, cb=cb:
                            s.activation(out=s_sqs[:, 0:NSH], in_=bscr,
                                         func=AF.Square, accum_out=C(cb)),
                            after=(tob,), inc=chS)
                g_bscr[pz] = tq1
                tt0 = kb.op("S", lambda s, cb=cb: s.activation(
                    out=C(cb + 1), in_=C(cb), func=AF.Sqrt, bias=c_one),
                    after=(tq1, t_cst), inc=chS)
                if wte_e is not None:
                    wcol = s_wte[:, gi * 16 + wte_e:gi * 16 + wte_e + 1]
                    hcol = s_hit[:, gi * 16 + wte_e:gi * 16 + wte_e + 1]
                else:
                    wcol = hcol = None
                tzz = kb.op("V", lambda v, pob=pob, zsl=zsl, ysl=ysl,
                            wcol=wcol: v.scalar_tensor_tensor(
                    out=zsl[:, 1:D], in0=pob(0, NSH),
                    scalar=(wcol if wcol is not None else 1.0),
                    in1=ysl[:, 1:D], op0=ALU.mult, op1=ALU.add),
                    after=(tmm, g_zz[pz], g_y[lt]),
                    inc=chV)
                tz0 = kb.op("V", lambda v, zsl=zsl, ysl=ysl, cb=cb,
                            wcol=wcol: v.scalar_tensor_tensor(
                    out=zsl[:, 0:1], in0=C(cb + 1),
                    scalar=(wcol if wcol is not None else 1.0),
                    in1=ysl[:, 0:1], op0=ALU.mult, op1=ALU.add),
                    after=(tt0,), inc=chV)
                tS1 = kb.op("S", lambda s, zsl=zsl, cb=cb: s.activation(
                    out=s_sqs[:, 0:NSH], in_=zsl[:, 1:D], func=AF.Square,
                    accum_out=C(cb + 2)), after=(tzz,), inc=chS)
                tzq = kb.op("V", lambda v, zsl=zsl, cb=cb: v.tensor_tensor(
                    out=C(cb + 3), in0=zsl[:, 0:1], in1=zsl[:, 0:1],
                    op=ALU.mult), after=(tz0,), inc=chV)
                tdd = kb.op("V", lambda v, cb=cb: v.tensor_tensor(
                    out=C(cb + 4), in0=C(cb + 3), in1=C(cb + 2),
                    op=ALU.subtract), after=(tzq, tS1), inc=chV)
                tdm = kb.op("V", lambda v, cb=cb: v.tensor_scalar(
                    out=C(cb + 5), in0=C(cb + 4), scalar1=EPS, scalar2=None,
                    op0=ALU.max), after=(tdd,), inc=chV)
                trc = kb.op("V", lambda v, cb=cb: v.reciprocal(
                    out=C(cb + 6), in_=C(cb + 5)), after=(tdm,), inc=chV)
                tsq2 = kb.op("S", lambda s, cb=cb: s.activation(
                    out=C(cb + 7), in_=C(cb + 6), func=AF.Sqrt),
                    after=(trc,), inc=chS)
                if hcol is not None:
                    tr1 = kb.op("V", lambda v, cb=cb: v.tensor_scalar(
                        out=C(cb + 8), in0=C(cb + 7), scalar1=1.0,
                        scalar2=None, op0=ALU.subtract), after=(tsq2,),
                        inc=chV)
                    tr2 = kb.op("V", lambda v, cb=cb, hcol=hcol:
                                v.scalar_tensor_tensor(
                        out=C(cb + 9), in0=hcol, scalar=C(cb + 8),
                        in1=s_onef[:, :], op0=ALU.mult, op1=ALU.add),
                        after=(tr1, t_onef), inc=chV)
                    rcol = C(cb + 9)
                    rtok = tr2
                else:
                    rcol = C(cb + 7)
                    rtok = tsq2
                ty = kb.op("S", lambda s, zsl=zsl, ysl=ysl, rcol=rcol:
                           s.activation(out=ysl, in_=zsl, func=AF.Identity,
                                        scale=rcol),
                           after=(rtok, tzz, tz0), inc=chS)
                g_y[lt] = ty
                g_zz[pz] = ty
                g_o2[po] = (tzz, tob)
                if s1_dbuf:
                    if po < 2:
                        g_ps1[po] = (tzz, tob)
                    else:
                        g_ps3[po - 2] = tzz
                else:
                    if po < 2:
                        g_ps1[0] = (tzz, tob)
                    else:
                        g_ps3[0] = tzz
        g_we[pe] = t_last_w2
        if we_cols > WE_COLS:
            g_we[0] = g_we[1] = t_last_w2
        g_beb[pe] = t_s3b[NG - 1]
        g_time = t_last_k2
        return ty

    kb.wcache = {}
    NLT = HALF // 128
    ty_last = None
    for h in range(2):
        yg = tuple(t for t in g_y if t is not None)
        ty0 = kb.op("G", lambda g: g.memset(s_y[:, :], 0.0),
                    after=yg + ((a_done) if h == 0 else ()), inc=chGc)
        ty1 = kb.op("G", lambda g: g.memset(
            bass.AP(s_y, 0, [[NLT * D, 128], [D, NLT], [1, 1]]), 1.0),
            after=(ty0,), inc=chGc)
        for lt in range(NLT):
            g_y[lt] = ty1
        esB = contextlib.ExitStack()
        psA = esB.enter_context(pt("psA_%d" % h, [128, 2048], F32))
        psB = esB.enter_context(pt("psB_%d" % h, [128, 2048], F32))
        ps1_of = lambda g: psA[:, :]
        ps3_of = lambda g: psB[:, :]
        po2 = [lambda lo, hi: psA[:, lo:hi],
               lambda lo, hi: psA[:, 1024 + lo:1024 + hi],
               lambda lo, hi: psB[:, lo:hi],
               lambda lo, hi: psB[:, 1024 + lo:1024 + hi]]
        for e in range(NE):
            if e == 0:
                emit_wload(0, WE_COLS, d_wec[0, :, :], d_bexp[0, :, :],
                           first=(h == 0))
            pf = None
            if e + 1 < NE:
                pf = (lambda e1=e + 1: emit_wload(
                    e1 % 2, WE_COLS, d_wec[e1, :, :], d_bexp[e1, :, :]))
            ty_last = ffn_expert(h, e, e % 2, (h == 0 and e == 0),
                                 ps1_of, ps3_of, po2, WE_COLS,
                                 W1_OFF, W3_OFF, W2_OFF, R0_OFF, 4, 4,
                                 d_wec[e, :, :], d_bexp[e, :, :], 512, e,
                                 skip_w=True, prefetch=pf, s1_dbuf=False,
                                 groups=tuple(range(HALF // 512)), grp=512,
                                 sfs=2048)
        esB.close()
        g_ps1[0] = g_ps1[1] = (ty_last,)
        g_ps3[0] = g_ps3[1] = ty_last
        for _i in range(4):
            g_o2[_i] = (ty_last, ty_last)
        esC = contextlib.ExitStack()
        p_s1C = esC.enter_context(pt("p_s1C_%d" % h, [128, 2048], F32))
        p_s3C = esC.enter_context(pt("p_s3C_%d" % h, [128, 2048], F32))
        po2C = [lambda lo, hi: p_s1C[:, lo:hi],
                lambda lo, hi: p_s1C[:, 1024 + lo:1024 + hi],
                lambda lo, hi: p_s3C[:, lo:hi],
                lambda lo, hi: p_s3C[:, 1024 + lo:1024 + hi]]
        ps1C_of = lambda g: p_s1C[:, :]
        ps3C_of = lambda g: p_s3C[:, :]
        for sb in range(2):
            groups = tuple(range(sb * 4, sb * 4 + 4))
            ty_last = ffn_expert(h, 100 + sb, 0, False,
                                 ps1C_of, ps3C_of, po2C, WS_COLS,
                                 SW1_OFF, SW3_OFF, SW2_OFF, SR0_OFF, 8, 8,
                                 d_wshc, None, 1024, None,
                                 groups=groups, sfs=2048, skip_w=(sb == 1),
                                 s1_dbuf=False)
            g_ps1[0] = g_ps1[1] = (ty_last,)
            g_ps3[0] = g_ps3[1] = ty_last
            for _i in range(4):
                g_o2[_i] = (ty_last, ty_last)
        esC.close()
        for lt in range(NLT):
            gi = h * NLT + lt
            px = lt % 2
            cb = 32 * px + 16
            ysl = s_y[:, lt * D:(lt + 1) * D]
            xsl = s_x[:, px * D:(px + 1) * D]
            wsl2 = s_w32[:, px * D:(px + 1) * D]
            tx2l = kb.op("Y", lambda y, gi=gi, xsl=xsl: y.dma_start(
                out=xsl, in_=d_x2[gi * 128:(gi + 1) * 128, :]),
                after=(g_x[px], tok_x2sp[NT - 2], tok_x2sp[NT - 1]),
                inc=chX[px], k=16)
            rstd, mb, tln = ln_stats(ysl, cb, "sqrt", (g_y[lt],))
            txf = kb.op("S", lambda s, xsl=xsl, wsl2=wsl2, mb=mb: s.activation(
                out=wsl2, in_=xsl, func=AF.Identity, bias=mb),
                after=(tx2l, tln[1], g_zz[px]), inc=chS)
            tst = kb.op("V", lambda v, ysl=ysl, xsl=xsl, wsl2=wsl2, rstd=rstd:
                        v.scalar_tensor_tensor(
                out=xsl, in0=ysl, scalar=rstd, in1=wsl2,
                op0=ALU.mult, op1=ALU.add), after=(txf, tln[0]), inc=chV)
            tout = kb.op("Y", lambda y, gi=gi, xsl=xsl: y.dma_start(
                out=d_out[gi * 128:(gi + 1) * 128, :], in_=xsl),
                after=(tst,), inc=chOut[px], k=16)
            out_toks.append(tout)
            g_x[px] = tout
            g_zz[px] = tst
            g_y[lt] = tst
    esF.close()

    if debug:
        tdbg1 = kb.op("Y", lambda y: y.dma_start(
            out=d_dbg[:, 0:NT * 16], in_=s_wte[:, :]),
            after=(out_toks[-1],), inc=chC, k=16)
        tdbg2 = kb.op("G", lambda gp: gp.dma_start(
            out=d_dbg[:, 512:512 + NT * 16], in_=s_hit[:, :]),
            after=(tdbg1,), inc=chCv[0], k=16)
        out_toks.append(tdbg2)

    kb.op("G", lambda g: g.nop(), after=tuple(out_toks[-3:]))

    with nc.Block() as block:
        kb.emit(block)
    es.close()
    return nc


_NC_CACHE = {}


def _get_nc(debug=False):
    if debug not in _NC_CACHE:
        _NC_CACHE[debug] = build_nc(debug)
    return _NC_CACHE[debug]


def kernel(**inputs):
    g, x = pack_host(inputs)
    nc = _get_nc(False)
    in_maps = []
    for b in range(B):
        m = dict(g)
        m["x"] = np.ascontiguousarray(x[b])
        in_maps.append(m)
    res = run_bass_kernel_spmd(nc, in_maps, core_ids=list(range(B)))
    return np.stack([r["out"] for r in res.results]).astype(np.float32)


# version 11
# speedup vs baseline: 1.0965x; 1.0965x over previous
import contextlib

import numpy as np

import concourse.bass as bass
import concourse.mybir as mybir
from concourse.bass_utils import run_bass_kernel_spmd

F32 = mybir.dt.float32
BF16 = mybir.dt.bfloat16
AF = mybir.ActivationFunctionType
ALU = mybir.AluOpType
AX = mybir.AxisListType

B, T, D, H = 8, 4096, 1024, 64
NE, NI = 16, 511
NSH = 1023
NT = T // 128
HALF = T // 2
GRP = 256
NG = HALF // GRP
EPS_LN = 1e-5
EPS = 1e-6

W1_OFF = 0
W3_OFF = 8 * 512
W2_OFF = 2 * 8 * 512
R0_OFF = W2_OFF + 4 * NSH
WE_COLS = R0_OFF + NSH

SW1_OFF = 0
SW3_OFF = 8 * 1024
SW2_OFF = 2 * 8 * 1024
SR0_OFF = SW2_OFF + 8 * NSH
WS_COLS = SR0_OFF + NSH


class _Chan:
    __slots__ = ("sem", "n", "name", "owner")

    def __init__(self, sem, name, owner):
        self.sem, self.n, self.name, self.owner = sem, 0, name, owner


class KB:
    ENGS = ("Y", "G", "T", "V", "S")

    def __init__(self, nc, es):
        self.nc = nc
        self.es = es
        self.items = []
        self.chans = []

    def chan(self, name, owner=None):
        ch = _Chan(self.es.enter_context(self.nc.semaphore(name)), name, owner)
        self.chans.append(ch)
        return ch

    def op(self, eng, emit, after=(), inc=None, k=1):
        idx = len(self.items)
        tok = None
        if inc is not None:
            tok = (inc, idx)
        after = tuple(t for t in after if t is not None)
        self.items.append([eng, after, emit, inc, k, idx])
        return tok

    def emit(self, block):
        used = set()
        for eng, after, _, _, _, _ in self.items:
            for (ch, idx) in after:
                if ch.owner == eng == "T":
                    continue
                used.add((ch.name, idx))
        val_at = {}
        counts = {}
        for it in self.items:
            eng, after, emitfn, inc, k, idx = it
            if inc is not None and (k == 16 or (inc.name, idx) in used):
                counts[inc.name] = counts.get(inc.name, 0) + k
                val_at[(inc.name, idx)] = counts[inc.name]
            elif inc is not None:
                it[3] = None
        decs = {"T": block.tensor, "V": block.vector, "S": block.scalar,
                "G": block.gpsimd, "Y": block.sync}
        for engname in self.ENGS:
            items = [it for it in self.items if it[0] == engname]
            if not items:
                continue

            def section(eng, items=items, engname=engname):
                waited = {}
                for _, after, emitfn, inc, k, idx in items:
                    for (ch, tidx) in after:
                        if ch.owner == engname == "T":
                            continue
                        val = val_at[(ch.name, tidx)]
                        if waited.get(ch.name, 0) < val:
                            eng.wait_ge(ch.sem, val)
                            waited[ch.name] = val
                    instr = emitfn(eng)
                    if inc is not None:
                        if instr is None:
                            instr = eng.nop()
                        instr.then_inc(inc.sem, k)

            decs[engname](section)


def _pack_T(w, rpad=None):
    R, K = w.shape
    assert K % 128 == 0
    rpad = R if rpad is None else rpad
    wt = np.zeros((K, rpad), np.float32)
    wt[:, :R] = w.T
    wt = wt.reshape(K // 128, 128, rpad)
    return np.ascontiguousarray(wt.transpose(1, 0, 2).reshape(128, (K // 128) * rpad))


def _pack_rows(wt, nrow_pad):
    R, C = wt.shape
    out = np.zeros((nrow_pad * 128, C), np.float32)
    out[:R] = wt
    out = out.reshape(nrow_pad, 128, C).transpose(1, 0, 2)
    return np.ascontiguousarray(out.reshape(128, nrow_pad * C))


def _bias_fm(b, ntile):
    out = np.zeros((ntile * 128,), np.float32)
    out[:b.shape[0]] = b
    return np.ascontiguousarray(out.reshape(ntile, 128).T)


def pack_host(inputs):
    f = np.float32
    g = {}
    g["wq"] = _pack_T(np.ascontiguousarray(inputs["L0_attn_W_q"].T))
    g["wk"] = _pack_T(np.ascontiguousarray(inputs["L0_attn_W_k"].T))
    g["wv"] = _pack_T(np.ascontiguousarray(inputs["L0_attn_W_v"].T))
    g["wo"] = np.ascontiguousarray(inputs["L0_attn_W_o"], dtype=f)
    g["gw"] = _pack_T(np.ascontiguousarray(inputs["L0_ffn_gate_w"]))
    blobs = []
    bias_blobs = []
    for e in range(NE):
        we1 = np.asarray(inputs["L0_ffn_We1"][e], f)
        we3 = np.asarray(inputs["L0_ffn_We3"][e], f)
        we2 = np.asarray(inputs["L0_ffn_We2"][e], f)
        be1 = np.asarray(inputs["L0_ffn_be1"][e], f)
        be3 = np.asarray(inputs["L0_ffn_be3"][e], f)
        be2 = np.asarray(inputs["L0_ffn_be2"][e], f)
        p1 = _pack_T(we1, 512)
        p3 = _pack_T(we3, 512)
        w2t = np.ascontiguousarray(we2.T)
        p2 = _pack_rows(w2t[1:], 4)
        r0 = np.zeros((128, NSH), f)
        r0[0] = w2t[0]
        r0[1] = be2
        blobs.append(np.concatenate([p1, p3, p2, r0], axis=1))
        bias_blobs.append(np.concatenate([_bias_fm(be1, 4), _bias_fm(be3, 4)],
                                         axis=1))
    g["wexp"] = np.stack(blobs)
    g["bexp"] = np.stack(bias_blobs)
    assert g["wexp"].shape == (NE, 128, WE_COLS), g["wexp"].shape
    ws1 = np.asarray(inputs["L0_ffn_Ws1"], f)
    ws3 = np.asarray(inputs["L0_ffn_Ws3"], f)
    ws2 = np.asarray(inputs["L0_ffn_Ws2"], f)
    sp1 = _pack_T(ws1, 1024)
    sp3 = _pack_T(ws3, 1024)
    ws2t = np.ascontiguousarray(ws2.T)
    sp2 = _pack_rows(ws2t[1:], 8)
    sr0 = np.zeros((128, NSH), f)
    sr0[0] = ws2t[0]
    sr0[1] = np.asarray(inputs["L0_ffn_bs2"], f)
    g["wsh"] = np.concatenate([sp1, sp3, sp2, sr0], axis=1)
    g["bsh"] = np.concatenate(
        [_bias_fm(np.asarray(inputs["L0_ffn_bs1"], f), 8),
         _bias_fm(np.asarray(inputs["L0_ffn_bs3"], f), 8)], axis=1)
    assert g["wsh"].shape == (128, WS_COLS), g["wsh"].shape
    import ml_dtypes
    ident = np.eye(128, dtype=f)
    g["identf"] = ident
    g["identb"] = ident.astype(ml_dtypes.bfloat16)
    ut = np.triu(np.ones((128, 128), f))
    g["utf"] = ut
    g["utb"] = ut.astype(ml_dtypes.bfloat16)
    onesb = np.ones((1, 128), f)
    g["onesb"] = onesb.astype(ml_dtypes.bfloat16)
    x = np.asarray(inputs["L0_attn_x"], f)
    return g, x


def build_nc(debug=False, stop_after=None, a3_gate=True, a3_xtspill=True,
             a3_level=9, detect_races=True):
    nc = bass.Bass("TRN2", target_bir_lowering=False,
                   detect_race_conditions=detect_races)
    es = contextlib.ExitStack()
    kb = KB(nc, es)

    dbg_kind = "ExternalOutput" if debug else "Internal"
    d_x = nc.dram_tensor("x", [T, D], F32, kind="ExternalInput").ap()
    d_wq = nc.dram_tensor("wq", [128, 8 * H], F32, kind="ExternalInput").ap()
    d_wk = nc.dram_tensor("wk", [128, 8 * H], F32, kind="ExternalInput").ap()
    d_wv = nc.dram_tensor("wv", [128, 8 * H], F32, kind="ExternalInput").ap()
    d_wo = nc.dram_tensor("wo", [H, D], F32, kind="ExternalInput").ap()
    d_gw = nc.dram_tensor("gw", [128, 8 * 16], F32, kind="ExternalInput").ap()
    d_wexp = nc.dram_tensor("wexp", [NE, 128, WE_COLS], F32, kind="ExternalInput").ap()
    d_wsh = nc.dram_tensor("wsh", [128, WS_COLS], F32, kind="ExternalInput").ap()
    d_bexp = nc.dram_tensor("bexp", [NE, 128, 8], F32, kind="ExternalInput").ap()
    d_bsh = nc.dram_tensor("bsh", [128, 16], F32, kind="ExternalInput").ap()
    d_idf = nc.dram_tensor("identf", [128, 128], F32, kind="ExternalInput").ap()
    d_idb = nc.dram_tensor("identb", [128, 128], BF16, kind="ExternalInput").ap()
    d_utf = nc.dram_tensor("utf", [128, 128], F32, kind="ExternalInput").ap()
    d_utb = nc.dram_tensor("utb", [128, 128], BF16, kind="ExternalInput").ap()
    d_onesb = nc.dram_tensor("onesb", [1, 128], BF16, kind="ExternalInput").ap()
    d_out = nc.dram_tensor("out", [T, D], F32, kind="ExternalOutput").ap()
    d_wec = nc.dram_tensor("wec", [NE, 128, WE_COLS], BF16, kind="Internal").ap()
    d_wshc = nc.dram_tensor("wshc", [128, WS_COLS], BF16, kind="Internal").ap()
    d_x2 = nc.dram_tensor("x2sp", [T, D], F32, kind=dbg_kind).ap()
    d_xt = nc.dram_tensor("xtsp", [8, 128, T], BF16, kind=dbg_kind).ap()
    d_dbg = None
    if debug:
        d_dbg = nc.dram_tensor("dbg", [128, 4096], F32, kind="ExternalOutput").ap()

    chL = [kb.chan("chL0"), kb.chan("chL1")]
    chLg = [kb.chan("chLg0"), kb.chan("chLg1")]
    chCv = [kb.chan("chCv0"), kb.chan("chCv1")]
    chC = kb.chan("chC")
    chX = [kb.chan("chX0"), kb.chan("chX1")]
    chSp = [kb.chan("chSp0"), kb.chan("chSp1")]
    chXt = [kb.chan("chXt0"), kb.chan("chXt1")]
    chXL = [kb.chan("chXL0"), kb.chan("chXL1")]
    chWt = [kb.chan("chWt0"), kb.chan("chWt1")]
    chBe = [kb.chan("chBe0"), kb.chan("chBe1")]
    chOut = [kb.chan("chOut0"), kb.chan("chOut1")]
    chT = kb.chan("chT", "T")
    chV = kb.chan("chV", "V")
    chS = kb.chan("chS", "S")
    chGc = kb.chan("chGc", "G")

    st = nc.sbuf_tensor

    s_x = es.enter_context(st("s_x", [128, 2 * D], F32))
    s_w32 = es.enter_context(st("s_w32", [128, 2 * D], F32))
    s_b16 = es.enter_context(st("s_b16", [128, 2 * D], BF16))
    s_sqs = es.enter_context(st("s_sqs", [128, D], BF16))
    s_wqb = es.enter_context(st("s_wqb", [128, 8 * H], BF16))
    s_wkb = es.enter_context(st("s_wkb", [128, 8 * H], BF16))
    s_wvb = es.enter_context(st("s_wvb", [128, 8 * H], BF16))
    s_wob = es.enter_context(st("s_wob", [H, D], BF16))
    s_gw = es.enter_context(st("s_gw", [128, 8 * 16], F32))
    s_idf = es.enter_context(st("s_idf", [128, 128], F32))
    s_idb = es.enter_context(st("s_idb", [128, 128], BF16))
    s_utf = es.enter_context(st("s_utf", [128, 128], F32))
    s_utb = es.enter_context(st("s_utb", [128, 128], BF16))
    s_onesb = es.enter_context(st("s_onesb", [1, 128], BF16))
    s_wte = es.enter_context(st("s_wte", [128, NT * 16], F32))
    s_hit = es.enter_context(st("s_hit", [128, NT * 16], F32))
    s_y = es.enter_context(st("s_y", [128, (HALF // 128) * D], F32))
    s_c = es.enter_context(st("s_c", [128, 64], F32))
    s_on1 = es.enter_context(st("s_on1", [128, 1], BF16))
    s_cst = es.enter_context(st("s_cst", [128, 3], F32))

    def C(i):
        return s_c[:, i:i + 1]

    c_epsln = s_cst[:, 0:1]
    c_one = s_cst[:, 1:2]
    c_mhalf = s_cst[:, 2:3]
    t_cst = kb.op("G", lambda g: g.memset(s_cst[:, 0:1], EPS_LN), inc=None)
    t_cst = kb.op("G", lambda g: g.memset(s_cst[:, 1:2], 1.0), inc=None)
    t_cst = kb.op("G", lambda g: g.memset(s_cst[:, 2:3], -0.5), inc=chGc)

    def emit_weight_conversions():
        tok = None
        for e in range(NE):
            tok = kb.op("G", lambda gp, e=e: gp.dma_start(
                out=d_wec[e, :, :], in_=d_wexp[e, :, :]), inc=chCv[0], k=16)
        tok = kb.op("G", lambda gp: gp.dma_start(out=d_wshc, in_=d_wsh),
                    inc=chCv[0], k=16)
        return (tok,)

    tok_wcache = emit_weight_conversions()

    def ld(dst, src):
        return kb.op("Y", lambda y: y.dma_start(out=dst, in_=src), inc=chC, k=16)

    for dst, srcc in ((s_idf[:, :], d_idf), (s_idb[:, :], d_idb),
                      (s_utf[:, :], d_utf), (s_utb[:, :], d_utb),
                      (s_onesb[:, :], d_onesb)):
        ld(dst, srcc)
    t_consts = [ld(s_gw[:, :], d_gw)]
    t_aw = []
    for dsrc, dst in ((d_wq, s_wqb), (d_wk, s_wkb), (d_wv, s_wvb)):
        t_aw.append(kb.op("G", lambda gp, s=dsrc, d=dst: gp.dma_start(
            out=d[:, :], in_=s), inc=chCv[1], k=16))
    t_aw.append(kb.op("G", lambda gp: gp.dma_start(
        out=s_wob[:, :], in_=d_wo), inc=chCv[1], k=16))
    t_consts += t_aw[-1:]

    if stop_after == "W":
        kb.op("G", lambda g: g.nop(), after=(t_consts[0],))
        with nc.Block() as block:
            kb.emit(block)
        es.close()
        return nc

    esA = contextlib.ExitStack()
    pt = nc.psum_tensor
    s_qf = esA.enter_context(st("s_qf", [128, T], BF16))
    s_vf = esA.enter_context(st("s_vf", [128, T], BF16))
    s_qt = esA.enter_context(st("s_qt", [128, NT * H], F32))
    s_kt = esA.enter_context(st("s_kt", [128, NT * H], BF16))
    s_vt = esA.enter_context(st("s_vt", [128, NT * H], BF16))
    s_ot = esA.enter_context(st("s_ot", [128, NT * H], BF16))
    s_of = esA.enter_context(st("s_of", [128, T], BF16))
    s_lnf = esA.enter_context(st("s_lnf", [128, 2 * D], BF16))
    s_ph = esA.enter_context(st("s_ph", [128, 3 * 128], F32))
    s_S = esA.enter_context(st("s_S", [64, 2 * H], F32))
    s_Sb = esA.enter_context(st("s_Sb", [64, H], BF16))
    s_z = esA.enter_context(st("s_z", [1, 2 * H], F32))
    s_zb = esA.enter_context(st("s_zb", [1, H], BF16))
    s_fm32 = esA.enter_context(st("s_fm32", [128, D], F32))
    s_g16 = esA.enter_context(st("s_g16", [128, 2 * 16], F32))
    s_g16b = esA.enter_context(st("s_g16b", [128, 2 * 16], F32))
    s_gm = esA.enter_context(st("s_gm", [128, 2 * 128], BF16))

    t_on1 = kb.op("G", lambda g: g.memset(s_on1[:, :], 1.0), inc=chGc)
    t_S0 = kb.op("G", lambda g: g.memset(s_S[:, :], 0.0), inc=chGc)
    t_S0b = kb.op("G", lambda g: g.memset(s_Sb[:, :], 0.0), inc=chGc)
    t_z0 = kb.op("G", lambda g: g.memset(s_z[:, :], 0.0), inc=chGc)
    t_z0b = kb.op("G", lambda g: g.memset(s_zb[:, :], 0.0), inc=chGc)

    def ln_stats(src_ap, cb, variant, after_src):
        t1 = kb.op("V", lambda v: v.tensor_reduce(
            out=C(cb), in_=src_ap, axis=AX.X, op=ALU.add),
            after=after_src, inc=chV)
        t2 = kb.op("S", lambda s: s.activation(
            out=s_sqs[:, :], in_=src_ap, func=AF.Square, accum_out=C(cb + 1)),
            after=after_src, inc=chS)
        t3 = kb.op("V", lambda v: v.tensor_scalar(
            out=C(cb + 2), in0=C(cb), scalar1=-1.0 / D, scalar2=None,
            op0=ALU.mult), after=(t1,), inc=chV)
        t4 = kb.op("V", lambda v: v.tensor_tensor(
            out=C(cb + 3), in0=C(cb + 2), in1=C(cb + 2), op=ALU.mult),
            after=(t3,), inc=chV)
        t5 = kb.op("V", lambda v: v.scalar_tensor_tensor(
            out=C(cb + 4), in0=C(cb + 1), scalar=1.0 / D, in1=C(cb + 3),
            op0=ALU.mult, op1=ALU.subtract), after=(t2, t4), inc=chV)
        if variant == "expln":
            t6 = kb.op("S", lambda s: s.activation(
                out=C(cb + 5), in_=C(cb + 4), func=AF.Ln, bias=c_epsln),
                after=(t5, t_cst), inc=chS)
            t7 = kb.op("S", lambda s: s.activation(
                out=C(cb + 6), in_=C(cb + 5), func=AF.Exp, scale=c_mhalf),
                after=(t6,), inc=chS)
            t8 = kb.op("S", lambda s: s.activation(
                out=C(cb + 7), in_=C(cb + 2), func=AF.Identity,
                scale=C(cb + 6)), after=(t7,), inc=chS)
        else:
            t6 = kb.op("S", lambda s: s.activation(
                out=C(cb + 5), in_=C(cb + 4), func=AF.Sqrt, bias=c_epsln),
                after=(t5, t_cst), inc=chS)
            t7 = kb.op("V", lambda v: v.reciprocal(
                out=C(cb + 6), in_=C(cb + 5)), after=(t6,), inc=chV)
            t8 = kb.op("V", lambda v: v.tensor_scalar(
                out=C(cb + 7), in0=C(cb + 2), scalar1=C(cb + 6), scalar2=None,
                op0=ALU.mult), after=(t7,), inc=chV)
        return C(cb + 6), C(cb + 7), (t7, t8)

    def phi(src_ap, dst_ap, w, after, np_=128):
        ta = kb.op("V", lambda v: v.tensor_scalar(
            out=s_ph[0:np_, 0:w], in0=src_ap, scalar1=0.0, scalar2=None,
            op0=ALU.min), after=after, inc=chV)
        tb = kb.op("S", lambda s: s.activation(
            out=s_ph[0:np_, 128:128 + w], in_=s_ph[0:np_, 0:w], func=AF.Exp),
            after=(ta,), inc=chS)
        tc = kb.op("V", lambda v: v.tensor_scalar(
            out=s_ph[0:np_, 256:256 + w], in0=src_ap, scalar1=0.0,
            scalar2=None, op0=ALU.max), after=after, inc=chV)
        td = kb.op("V", lambda v: v.tensor_tensor(
            out=dst_ap, in0=s_ph[0:np_, 128:128 + w],
            in1=s_ph[0:np_, 256:256 + w],
            op=ALU.add), after=(tb, tc), inc=chV)
        return td

    esA1 = contextlib.ExitStack()
    p_trA = esA1.enter_context(pt("p_trA", [128, D], BF16))
    p_qkv = esA1.enter_context(pt("p_qkv", [128, 512], F32))
    p_G = esA1.enter_context(pt("p_G", [128, 128], F32))
    p_oc = esA1.enter_context(pt("p_oc", [128, H], F32))
    p_zc = esA1.enter_context(pt("p_zc", [128, H], F32))
    p_z = esA1.enter_context(pt("p_z", [1, H], F32))
    p_S = esA1.enter_context(pt("p_S", [64, H], F32))
    p_of = esA1.enter_context(pt("p_of", [64, 128], BF16))
    g_x = [None, None]
    g_trA = None
    g_qkv = None
    tok_qf = [None] * NT
    tok_vf = [None] * NT
    tok_qt = [None] * NT
    tok_kt = [None] * NT
    tok_vt = [None] * NT
    g_lnf = [None, None]
    a1_tr_last = None

    def a1_tile(i):
        nonlocal g_trA, g_qkv, a1_tr_last
        p = i % 2
        cb = 32 * p
        xsl = s_x[:, p * D:(p + 1) * D]
        tx = kb.op("Y", lambda y, i=i, xsl=xsl: y.dma_start(
            out=xsl, in_=d_x[i * 128:(i + 1) * 128, :]),
            after=(g_x[p],) + (tuple(t_consts) if i < 2 else ()),
            inc=chX[p], k=16)
        rstd, mb, tln = ln_stats(xsl, cb, "expln", (tx,))
        lnsl = s_b16[:, p * D:(p + 1) * D]
        tln1 = kb.op("S", lambda s, xsl=xsl, lnsl=lnsl, rstd=rstd, mb=mb:
                     s.activation(out=lnsl, in_=xsl, func=AF.Identity,
                                  scale=rstd, bias=mb),
                     after=tln + (g_lnf[p],), inc=chS)
        g_x[p] = tln1
        ttr = None
        for k in range(8):
            ttr = kb.op("T", lambda t, k=k, lnsl=lnsl: t.transpose(
                p_trA[:, k * 128:(k + 1) * 128], lnsl[:, k * 128:(k + 1) * 128],
                s_idb[:, :]),
                after=(tln1, g_trA) if k == 0 else (),
                inc=chT if k == 7 else None)
        a1_tr_last = ttr
        fmsl = s_lnf[:, p * D:(p + 1) * D]
        tfm = kb.op("V", lambda v, fmsl=fmsl: v.tensor_copy(
            out=fmsl, in_=p_trA[:, :]), after=(ttr,), inc=chV)
        g_trA = tfm
        tmm = None
        forms = [
            lambda t, k, fmsl: t.matmul(
                p_qkv[0:64, 0:128], s_wqb[:, k * H:(k + 1) * H],
                fmsl[:, k * 128:(k + 1) * 128], start=(k == 0), stop=(k == 7)),
            lambda t, k, fmsl: t.matmul(
                p_qkv[0:64, 128:256], s_wvb[:, k * H:(k + 1) * H],
                fmsl[:, k * 128:(k + 1) * 128], start=(k == 0), stop=(k == 7)),
            lambda t, k, fmsl: t.matmul(
                p_qkv[:, 256:320], fmsl[:, k * 128:(k + 1) * 128],
                s_wqb[:, k * H:(k + 1) * H], start=(k == 0), stop=(k == 7)),
            lambda t, k, fmsl: t.matmul(
                p_qkv[:, 320:384], fmsl[:, k * 128:(k + 1) * 128],
                s_wkb[:, k * H:(k + 1) * H], start=(k == 0), stop=(k == 7)),
            lambda t, k, fmsl: t.matmul(
                p_qkv[:, 384:448], fmsl[:, k * 128:(k + 1) * 128],
                s_wvb[:, k * H:(k + 1) * H], start=(k == 0), stop=(k == 7)),
        ]
        for fi, fn in enumerate(forms):
            for k in range(8):
                af = (tfm, g_qkv) if (fi == 0 and k == 0) else ()
                tmm = kb.op("T",
                            lambda t, fn=fn, k=k, fmsl=fmsl: fn(t, k, fmsl),
                            after=af,
                            inc=chT if (fi == 4 and k == 7) else None)
        g_lnf[p] = tmm
        tok_qf[i] = phi(p_qkv[0:64, 0:128], s_qf[0:64, i * 128:(i + 1) * 128],
                        128, (tmm,), np_=64)
        tok_qt[i] = phi(p_qkv[:, 256:320], s_qt[:, i * H:(i + 1) * H],
                        64, (tok_qf[i],))
        tok_kt[i] = phi(p_qkv[:, 320:384], s_kt[:, i * H:(i + 1) * H],
                        64, (tok_qt[i],))
        tok_vf[i] = kb.op("V", lambda v, i=i: v.tensor_copy(
            out=s_vf[0:64, i * 128:(i + 1) * 128], in_=p_qkv[0:64, 128:256]),
            after=(tmm,), inc=chV)
        tok_vt[i] = kb.op("V", lambda v, i=i: v.tensor_copy(
            out=s_vt[:, i * H:(i + 1) * H], in_=p_qkv[:, 384:448]),
            after=(tmm,), inc=chV)
        g_qkv = tok_vt[i]

    g_G = None
    g_gm = [None, None]
    g_oc = None
    g_zc = None
    g_of = None
    t_Scp = t_S0b
    t_zcp = t_z0b
    t_dSmm = None
    t_zsum = None
    tok_o = [None] * NT

    def a2_chunk(c):
        nonlocal g_G, g_oc, g_zc, g_of, t_Scp, t_zcp, t_dSmm, t_zsum
        p = c % 2
        cbase = 40 + 4 * p
        tG = kb.op("T", lambda t, c=c: t.matmul(
            p_G[:, :], s_vf[0:64, c * 128:(c + 1) * 128],
            s_qf[0:64, c * 128:(c + 1) * 128], start=True, stop=True),
            after=(tok_vf[c], tok_qf[c], g_G), inc=chT)
        gm = s_gm[:, p * 128:(p + 1) * 128]
        tGm = kb.op("V", lambda v, gm=gm: v.tensor_tensor(
            out=gm, in0=p_G[:, :], in1=s_utf[:, :], op=ALU.mult),
            after=(tG, g_gm[p]), inc=chV)
        g_G = tGm
        to1 = kb.op("T", lambda t, c=c, gm=gm: t.matmul(
            p_oc[:, :], gm, s_kt[:, c * H:(c + 1) * H], start=True, stop=False),
            after=(tGm, tok_kt[c], g_oc), inc=None)
        to2 = kb.op("T", lambda t, c=c: t.matmul(
            p_oc[:, :], s_qf[0:64, c * 128:(c + 1) * 128], s_Sb[:, :],
            start=False, stop=True), after=(t_Scp,), inc=chT)
        g_gm[p] = to2
        tz1 = kb.op("T", lambda t, c=c: t.matmul(
            p_zc[:, :], s_utb[:, :], s_kt[:, c * H:(c + 1) * H],
            start=True, stop=False), after=(g_zc, tok_kt[c]), inc=None)
        tz2 = kb.op("T", lambda t: t.matmul(
            p_zc[:, :], s_onesb[0:1, :], s_zb[0:1, :],
            start=False, stop=True), after=(t_zcp, t_on1), inc=chT)
        t_zsum = kb.op("T", lambda t, c=c: t.matmul(
            p_z[0:1, :], s_on1[:, :], s_kt[:, c * H:(c + 1) * H],
            start=True, stop=True),
            after=(t_zcp,), inc=chT)
        t_dSmm = kb.op("T", lambda t, c=c: t.matmul(
            p_S[:, :], s_vt[:, c * H:(c + 1) * H], s_kt[:, c * H:(c + 1) * H],
            start=True, stop=True),
            after=(tok_vt[c], t_Scp, to2), inc=chT)
        td1 = kb.op("V", lambda v, c=c: v.tensor_tensor(
            out=s_ph[:, 0:H], in0=p_zc[:, :], in1=s_qt[:, c * H:(c + 1) * H],
            op=ALU.mult), after=(tz2, tok_qt[c]), inc=chV)
        g_zc = td1
        td2 = kb.op("V", lambda v, cb=cbase: v.tensor_reduce(
            out=C(cb), in_=s_ph[:, 0:H], axis=AX.X, op=ALU.add),
            after=(td1,), inc=chV)
        td3 = kb.op("V", lambda v, cb=cbase: v.tensor_scalar(
            out=C(cb + 1), in0=C(cb), scalar1=1e-6, scalar2=None,
            op0=ALU.max), after=(td2,), inc=chV)
        td4 = kb.op("V", lambda v, cb=cbase: v.reciprocal(
            out=C(cb + 2), in_=C(cb + 1)), after=(td3,), inc=chV)
        to3 = kb.op("V", lambda v, c=c, cb=cbase: v.tensor_scalar(
            out=s_ot[:, c * H:(c + 1) * H], in0=p_oc[:, :],
            scalar1=C(cb + 2), scalar2=None, op0=ALU.mult),
            after=(to2, td4), inc=chV)
        g_oc = to3
        pc, pn = c % 2, (c + 1) % 2
        tSa = kb.op("V", lambda v, pc=pc, pn=pn: v.tensor_tensor(
            out=s_S[:, pn * H:(pn + 1) * H], in0=p_S[:, :],
            in1=s_S[:, pc * H:(pc + 1) * H], op=ALU.add),
            after=(t_dSmm, t_S0, t_Scp), inc=chV)
        t_Scp = kb.op("V", lambda v, pn=pn: v.tensor_copy(
            out=s_Sb[:, :], in_=s_S[:, pn * H:(pn + 1) * H]),
            after=(tSa, t_S0b), inc=chV)
        tza = kb.op("V", lambda v, pc=pc, pn=pn: v.tensor_tensor(
            out=s_z[0:1, pn * H:(pn + 1) * H], in0=p_z[0:1, :],
            in1=s_z[0:1, pc * H:(pc + 1) * H], op=ALU.add),
            after=(t_zsum, tz2, t_z0, t_zcp), inc=chV)
        t_zcp = kb.op("V", lambda v, pn=pn: v.tensor_copy(
            out=s_zb[0:1, :], in_=s_z[0:1, pn * H:(pn + 1) * H]),
            after=(tza, t_z0b), inc=chV)
        tof = kb.op("T", lambda t, c=c: t.transpose(
            p_of[:, :], s_ot[:, c * H:(c + 1) * H], s_idb[:, :]),
            after=(to3, g_of), inc=chT)
        tok_o[c] = kb.op("V", lambda v, c=c: v.tensor_copy(
            out=s_of[0:64, c * 128:(c + 1) * 128], in_=p_of[:, :]),
            after=(tof,), inc=chV)
        g_of = tok_o[c]

    for i in range(NT + 1):
        if i < NT:
            a1_tile(i)
        if i >= 1:
            a2_chunk(i - 1)
    esA1.close()

    esA3 = contextlib.ExitStack()
    p_at = [esA3.enter_context(pt("p_at0", [128, D], F32)),
            esA3.enter_context(pt("p_at1", [128, D], F32))]
    p_tr2 = esA3.enter_context(pt("p_tr2", [128, D], F32))
    p_gate = esA3.enter_context(pt("p_gate", [128, 16], F32))
    s_at = esA3.enter_context(st("s_at", [128, 2 * D], F32))
    s_glog = esA3.enter_context(st("s_glog", [128, 2 * 16], F32))
    g_at = [None, None]
    g_atS = [None, None]
    g_x = [g_x[0], g_x[1]]
    g_tr2 = None
    g_fm32 = None
    g_gate = None
    g_b16 = [g_lnf[0], g_lnf[1]]
    tok_x2sp = [None] * NT
    tok_xtsp = [None] * NT
    g_w32 = [None, None]
    for i in range(NT):
        p = i % 2
        cb = 32 * p
        atp = p_at[p]
        t1 = kb.op("T", lambda t, i=i, atp=atp: t.matmul(
            atp[:, 0:512], s_of[0:64, i * 128:(i + 1) * 128], s_wob[:, 0:512],
            start=True, stop=True),
            after=(tok_o[i], g_at[p]) + ((g_of, g_oc) if i < 2 else ()),
            inc=None)
        t2 = kb.op("T", lambda t, i=i, atp=atp: t.matmul(
            atp[:, 512:1024], s_of[0:64, i * 128:(i + 1) * 128],
            s_wob[:, 512:1024], start=True, stop=True), after=(), inc=chT)
        if a3_level < 2:
            tev = kb.op("V", lambda v, atp=atp, p=p: v.tensor_copy(
                out=s_w32[:, p * D:(p + 1) * D], in_=atp[:, :]),
                after=(t2, g_at[p], g_w32[p]), inc=chV)
            g_at[p] = tev
            g_w32[p] = tev
            tok_x2sp[i] = tok_xtsp[i] = tw2 = tev
            continue
        atsl = s_at[:, p * D:(p + 1) * D]
        tat = kb.op("V", lambda v, atp=atp, atsl=atsl: v.tensor_copy(
            out=atsl, in_=atp[:, :]), after=(t2, g_at[p], g_atS[p]), inc=chV)
        g_at[p] = tat
        if 11 <= a3_level <= 15:
            toks = []
            t1 = kb.op("V", lambda v, atsl=atsl, cb=cb: v.tensor_reduce(
                out=C(cb), in_=atsl, axis=AX.X, op=ALU.add),
                after=(tat,), inc=chV)
            toks.append(t1)
            if a3_level >= 12:
                t2s = kb.op("S", lambda s, atsl=atsl, cb=cb: s.activation(
                    out=s_sqs[:, :], in_=atsl, func=AF.Square,
                    accum_out=C(cb + 1)), after=(tat,), inc=chS)
                toks.append(t2s)
            if a3_level >= 13:
                t3 = kb.op("V", lambda v, cb=cb: v.tensor_scalar(
                    out=C(cb + 2), in0=C(cb), scalar1=1.0 / D, scalar2=None,
                    op0=ALU.mult), after=(t1,), inc=chV)
                t4 = kb.op("V", lambda v, cb=cb: v.tensor_tensor(
                    out=C(cb + 3), in0=C(cb + 2), in1=C(cb + 2), op=ALU.mult),
                    after=(t3,), inc=chV)
                t5 = kb.op("V", lambda v, cb=cb: v.scalar_tensor_tensor(
                    out=C(cb + 4), in0=C(cb + 1), scalar=1.0 / D,
                    in1=C(cb + 3), op0=ALU.mult, op1=ALU.subtract),
                    after=(t2s, t4), inc=chV)
                t8 = kb.op("V", lambda v, cb=cb: v.tensor_scalar(
                    out=C(cb + 7), in0=C(cb + 2), scalar1=C(cb + 4),
                    scalar2=-1.0, op0=ALU.mult, op1=ALU.mult),
                    after=(t5,), inc=chV)
                toks += [t8]
            if a3_level >= 14:
                t6 = kb.op("S", lambda s, cb=cb: s.activation(
                    out=C(cb + 5), in_=C(cb + 4), func=AF.Ln, bias=c_epsln),
                    after=(t5, t_cst), inc=chS)
                t7 = kb.op("S", lambda s, cb=cb: s.activation(
                    out=C(cb + 6), in_=C(cb + 5), func=AF.Exp, scale=c_mhalf),
                    after=(t6,), inc=chS)
                toks += [t7]
            tev = kb.op("V", lambda v, atsl=atsl, p=p: v.tensor_copy(
                out=s_w32[:, p * D:(p + 1) * D], in_=atsl),
                after=tuple(toks) + (g_w32[p],), inc=chV)
            g_atS[p] = tev
            g_w32[p] = tev
            tok_x2sp[i] = tok_xtsp[i] = tw2 = tev
            continue
        rstd, mb, tln = ln_stats(atsl, cb, "expln", (tat,))
        xsl = s_x[:, p * D:(p + 1) * D]
        tx = kb.op("Y", lambda y, i=i, xsl=xsl: y.dma_start(
            out=xsl, in_=d_x[i * 128:(i + 1) * 128, :]),
            after=(g_x[p],), inc=chX[p], k=16)
        wsl = s_w32[:, p * D:(p + 1) * D]
        txf = kb.op("S", lambda s, xsl=xsl, wsl=wsl, mb=mb: s.activation(
            out=wsl, in_=xsl, func=AF.Identity, bias=mb),
            after=(tx, tln[1], g_w32[p]), inc=chS)
        tx2 = kb.op("V", lambda v, atsl=atsl, xsl=xsl, wsl=wsl, rstd=rstd:
                    v.scalar_tensor_tensor(
                        out=xsl, in0=atsl, scalar=rstd, in1=wsl,
                        op0=ALU.mult, op1=ALU.add),
                    after=(txf, tln[0]), inc=chV)
        g_atS[p] = tx2
        tok_x2sp[i] = kb.op("Y", lambda y, i=i, xsl=xsl: y.dma_start(
            out=d_x2[i * 128:(i + 1) * 128, :], in_=xsl),
            after=(tx2,), inc=chSp[p], k=16)
        if a3_level < 3:
            tok_xtsp[i] = tw2 = tok_x2sp[i]
            g_x[p] = tok_x2sp[i]
            continue
        rstd2, mb2, tln2 = ln_stats(xsl, cb + 8, "expln", (tx2,))
        tln2f = kb.op("S", lambda s, xsl=xsl, wsl=wsl, rstd2=rstd2, mb2=mb2:
                      s.activation(out=wsl, in_=xsl, func=AF.Identity,
                                   scale=rstd2, bias=mb2),
                      after=tln2 + (txf,), inc=chS)
        g_x[p] = tok_x2sp[i]
        if a3_level < 4:
            tok_xtsp[i] = tw2 = tln2f
            g_w32[p] = tln2f
            continue
        ttr = None
        for k in range(8):
            ttr = kb.op("T", lambda t, k=k, wsl=wsl: t.transpose(
                p_tr2[:, k * 128:(k + 1) * 128], wsl[:, k * 128:(k + 1) * 128],
                s_idf[:, :]),
                after=(tln2f, g_tr2) if k == 0 else (),
                inc=chT if k == 7 else None)
        g_w32[p] = ttr
        tfm32 = kb.op("V", lambda v: v.tensor_copy(
            out=s_fm32[:, :], in_=p_tr2[:, :]), after=(ttr, g_fm32), inc=chV)
        bsl = s_b16[:, p * D:(p + 1) * D]
        txtb = kb.op("V", lambda v, bsl=bsl: v.tensor_copy(
            out=bsl, in_=p_tr2[:, :]),
            after=(ttr, g_b16[p], tfm32), inc=chV)
        g_tr2 = txtb
        if a3_xtspill:
            tok_xtsp[i] = kb.op("Y", lambda y, i=i, bsl=bsl: y.dma_start(
                out=d_xt[:, :, i * 128:(i + 1) * 128].rearrange("k p n -> p k n"),
                in_=bsl.rearrange("p (k n) -> p k n", k=8)),
                after=(txtb,), inc=chXt[p], k=16)
        else:
            tok_xtsp[i] = txtb
        g_b16[p] = tok_xtsp[i]
        if not a3_gate:
            tw2 = txtb
            continue
        tg = None
        for k in range(8):
            tg = kb.op("T", lambda t, k=k: t.matmul(
                p_gate[:, :], s_fm32[:, k * 128:(k + 1) * 128],
                s_gw[:, k * 16:(k + 1) * 16], start=(k == 0), stop=(k == 7)),
                after=(tfm32, g_gate) if k == 0 else (),
                inc=chT if k == 7 else None)
        g_fm32 = tg
        gsl = s_glog[:, p * 16:(p + 1) * 16]
        tgc = kb.op("V", lambda v, gsl=gsl: v.tensor_copy(
            out=gsl, in_=p_gate[:, :]), after=(tg, g_gate), inc=chV)
        tmx = kb.op("V", lambda v, cb=cb, gsl=gsl: v.reduce_max(
            out=C(cb + 16), in_=gsl, axis=AX.X), after=(tgc,), inc=chV)
        tnm = kb.op("V", lambda v, cb=cb: v.tensor_scalar(
            out=C(cb + 17), in0=C(cb + 16), scalar1=-1.0, scalar2=None,
            op0=ALU.mult), after=(tmx,), inc=chV)
        esl = s_g16[:, p * 16:(p + 1) * 16]
        bsl2 = s_g16b[:, p * 16:(p + 1) * 16]
        te = kb.op("S", lambda s, cb=cb, esl=esl, gsl=gsl: s.activation(
            out=esl, in_=gsl, func=AF.Exp, bias=C(cb + 17)),
            after=(tnm,), inc=chS)
        g_gate = tgc
        tse = kb.op("V", lambda v, cb=cb, esl=esl: v.tensor_reduce(
            out=C(cb + 18), in_=esl, axis=AX.X, op=ALU.add),
            after=(te,), inc=chV)
        trs = kb.op("V", lambda v, cb=cb: v.reciprocal(
            out=C(cb + 19), in_=C(cb + 18)), after=(tse,), inc=chV)
        tm1 = kb.op("V", lambda v, cb=cb, esl=esl: v.reduce_max(
            out=C(cb + 20), in_=esl, axis=AX.X), after=(te,), inc=chV)
        teq = kb.op("V", lambda v, cb=cb, esl=esl, bsl2=bsl2: v.tensor_scalar(
            out=bsl2, in0=esl, scalar1=C(cb + 20), scalar2=0.0,
            op0=ALU.subtract, op1=ALU.is_ge), after=(tm1,), inc=chV)
        tmk = kb.op("V", lambda v, esl=esl, bsl2=bsl2: v.scalar_tensor_tensor(
            out=bsl2, in0=bsl2, scalar=-1e30, in1=esl,
            op0=ALU.mult, op1=ALU.add), after=(teq,), inc=chV)
        tm2 = kb.op("V", lambda v, cb=cb, bsl2=bsl2: v.reduce_max(
            out=C(cb + 21), in_=bsl2, axis=AX.X), after=(tmk,), inc=chV)
        th = kb.op("V", lambda v, cb=cb, esl=esl, i=i: v.tensor_scalar(
            out=s_hit[:, i * 16:(i + 1) * 16], in0=esl, scalar1=C(cb + 21),
            scalar2=None, op0=ALU.is_ge), after=(tm2,), inc=chV)
        tw1 = kb.op("V", lambda v, cb=cb, esl=esl, bsl2=bsl2: v.tensor_scalar(
            out=bsl2, in0=esl, scalar1=C(cb + 19), scalar2=None,
            op0=ALU.mult), after=(trs, tm2), inc=chV)
        tw2 = kb.op("V", lambda v, i=i, bsl2=bsl2: v.tensor_tensor(
            out=s_wte[:, i * 16:(i + 1) * 16], in0=bsl2,
            in1=s_hit[:, i * 16:(i + 1) * 16], op=ALU.mult),
            after=(th, tw1), inc=chV)
    esA3.close()
    esA.close()
    a_done = (tok_x2sp[NT - 2], tok_x2sp[NT - 1], tok_xtsp[NT - 2],
              tok_xtsp[NT - 1], tw2) + tok_wcache

    if stop_after == "A":
        kb.op("G", lambda g: g.nop(), after=a_done)
        with nc.Block() as block:
            kb.emit(block)
        es.close()
        return nc

    esF = contextlib.ExitStack()
    s_we = esF.enter_context(st("s_we", [128, 2 * WE_COLS], BF16))
    s_beb = esF.enter_context(st("s_beb", [128, 2 * 8], F32))
    s_xtb = esF.enter_context(st("s_xtb", [128, 2 * 8 * GRP], BF16))
    s_sf = esF.enter_context(st("s_sf", [128, NG * 1024], BF16))
    s_a = esF.enter_context(st("s_a", [128, 2048], BF16))
    s_s3b = esF.enter_context(st("s_s3b", [128, 2048], BF16))
    s_sq = esF.enter_context(st("s_sq", [128, 2 * 2048], BF16))
    s_sums = esF.enter_context(st("s_sums", [1, NG * GRP], F32))
    s_timeb = esF.enter_context(st("s_timeb", [2, NG * GRP], BF16))
    s_onef = esF.enter_context(st("s_onef", [128, 1], F32))
    s_s1f = esF.enter_context(st("s_s1f", [128, 1024], F32))
    s_wshb = None

    t_onef = kb.op("G", lambda g: g.memset(s_onef[:, :], 1.0),
                   after=a_done, inc=chGc)
    t_ones_time = kb.op("G", lambda g: g.memset(s_timeb[:, :], 1.0),
                        after=a_done, inc=chGc)

    g_y = [None] * (HALF // 128)
    g_we = [None, None]
    g_xt = [None, None]
    g_zz = [None, None]
    g_o2 = [(None, None)] * 4
    g_sfgrp = [None] * NG
    g_time = None
    g_beb = [None, None]
    g_ps1 = [None, None]
    g_ps3 = [None, None]
    g_s1f = [None]
    g_bscr = [None, None]
    g_sq = [None, None]

    def emit_wload(pe, we_cols, d_wsrc, d_bsrc, first=False):
        wsl = s_we[:, pe * WE_COLS: pe * WE_COLS + we_cols]
        w_after = ((g_we[0], g_we[1]) if we_cols > WE_COLS else (g_we[pe],))
        tw = kb.op("Y", lambda y: y.dma_start(out=wsl, in_=d_wsrc),
                   after=w_after + (a_done if first else ()),
                   inc=chWt[pe], k=16)
        if d_bsrc is not None:
            bsl = s_beb[:, pe * 8:(pe + 1) * 8]
            tb = kb.op("Y", lambda y, bsl=bsl: y.dma_start(out=bsl, in_=d_bsrc),
                       after=(g_beb[pe],), inc=chBe[pe], k=16)
        else:
            bsl = s_beb[:, 0:16]
            tb = kb.op("Y", lambda y, bsl=bsl: y.dma_start(out=bsl, in_=d_bsh),
                       after=(g_beb[0], g_beb[1]), inc=chBe[0], k=16)
        kb.wcache[(pe, we_cols)] = (tw, tb)
    out_toks = []

    def ffn_expert(h, e, pe, first, ps1_of, ps3_of, p_o2, we_cols, w1o, w3o,
                   w2o, r0o, nit, nkt, d_wsrc, d_bsrc, stride1, wte_e,
                   groups=tuple(range(NG)), sfs=1024, skip_w=False,
                   s1_dbuf=True, prefetch=None):
        nonlocal g_time
        HB = h * HALF
        g0 = groups[0]
        wsl = s_we[:, pe * WE_COLS: pe * WE_COLS + we_cols]
        if not skip_w:
            emit_wload(pe, we_cols, d_wsrc, d_bsrc, first)
        bsl = s_beb[:, 0:16] if d_bsrc is None else s_beb[:, pe * 8:(pe + 1) * 8]
        tw, tb = kb.wcache[(pe, we_cols)]
        pending_sum = None
        last_smul = None
        t_silu = [None] * NG
        t_s3b = [None] * NG
        t_smul = [None] * NG
        t_sumcp = [None] * NG
        for g in groups:
            pg = g % 2
            pidx = (g % 2) if s1_dbuf else 0
            s1r = ps1_of(g)
            s3r = ps3_of(g)
            sfo = (g - g0) * sfs
            xsl = s_xtb[:, pg * 8 * GRP:(pg + 1) * 8 * GRP]
            txt = kb.op("Y", lambda y, g=g, xsl=xsl, HB=HB: y.dma_start(
                out=xsl,
                in_=d_xt[:, :, HB + g * GRP:HB + (g + 1) * GRP]
                .rearrange("k p n -> p k n")),
                after=(g_xt[pg],) + (a_done if first and g < 2 else ()),
                inc=chXL[pg], k=16)
            if g == groups[0] and prefetch is not None:
                prefetch()
            tmm1 = tmm3 = None
            for it in range(nit):
                for k in range(8):
                    af = ((tw, txt) + tuple(g_ps1[pidx] or ())
                          if (it == 0 and k == 0) else ())
                    tmm1 = kb.op("T", lambda t, it=it, k=k, xsl=xsl, s1r=s1r:
                                 t.matmul(
                        s1r[:, it * GRP:(it + 1) * GRP],
                        wsl[:, w1o + k * stride1 + it * 128:
                            w1o + k * stride1 + it * 128 + 128],
                        xsl[:, k * GRP:(k + 1) * GRP],
                        start=(k == 0), stop=(k == 7)), after=af,
                        inc=chT if (it == nit - 1 and k == 7) else None)
            for it in range(nit):
                for k in range(8):
                    af = (g_ps3[pidx],) if (it == 0 and k == 0) else ()
                    tmm3 = kb.op("T", lambda t, it=it, k=k, xsl=xsl, s3r=s3r:
                                 t.matmul(
                        s3r[:, it * GRP:(it + 1) * GRP],
                        wsl[:, w3o + k * stride1 + it * 128:
                            w3o + k * stride1 + it * 128 + 128],
                        xsl[:, k * GRP:(k + 1) * GRP],
                        start=(k == 0), stop=(k == 7)), after=af,
                        inc=chT if (it == nit - 1 and k == 7) else None)
            g_xt[pg] = tmm3
            for it0 in range(0, nit, 4):
                nb = min(4, nit - it0)
                tcp1 = kb.op("V", lambda v, it0=it0, nb=nb, s1r=s1r:
                             v.tensor_copy(
                    out=s_s1f[:, 0:nb * GRP],
                    in_=s1r[:, it0 * GRP:(it0 + nb) * GRP]),
                    after=(tmm1, g_s1f[0]), inc=chV)
                for it in range(it0, it0 + nb):
                    t_silu[g] = kb.op("S", lambda s, it=it, it0=it0:
                                      s.activation(
                        out=s_a[:, it * GRP:(it + 1) * GRP],
                        in_=s_s1f[:, (it - it0) * GRP:(it - it0 + 1) * GRP],
                        func=AF.Silu, bias=bsl[:, it:it + 1]),
                        after=(tcp1, tb) if it == it0 else (), inc=chS)
                g_s1f[0] = t_silu[g]
            g_ps1[pidx] = (tcp1,)
            for it in range(nit):
                t_s3b[g] = kb.op("V", lambda v, it=it, nit=nit, s3r=s3r:
                                 v.tensor_scalar(
                    out=s_s3b[:, it * GRP:(it + 1) * GRP],
                    in0=s3r[:, it * GRP:(it + 1) * GRP],
                    scalar1=bsl[:, nit + it:nit + it + 1], scalar2=None,
                    op0=ALU.add),
                    after=(tmm3, tb, last_smul) if it == 0 else (), inc=chV)
            sf_guard = tuple(g_sfgrp) if sfs != 1024 else (g_sfgrp[g],)
            t_smul[g] = kb.op("G", lambda gp, sfo=sfo, nit=nit:
                              gp.tensor_tensor(
                out=s_sf[:, sfo:sfo + nit * GRP],
                in0=s_a[:, 0:nit * GRP], in1=s_s3b[:, 0:nit * GRP],
                op=ALU.mult),
                after=(t_silu[g], t_s3b[g]) + sf_guard, inc=chGc)
            tsq = kb.op("S", lambda s, sfo=sfo, nit=nit, pg=pg: s.activation(
                out=s_sq[:, pg * 2048:pg * 2048 + nit * GRP],
                in_=s_sf[:, sfo:sfo + nit * GRP], func=AF.Square),
                after=(t_smul[g], g_sq[pg]), inc=chS)
            def emit_sum(g=g, s3r=s3r, tsq=tsq, ts3b=t_s3b[g], pidx=pidx):
                tsum = None
                for it in range(nit):
                    tsum = kb.op("T", lambda t, it=it, s3r=s3r, g=g: t.matmul(
                        s3r[0:1, 0:GRP], s_on1[:, :],
                        s_sq[:, (g % 2) * 2048 + it * GRP:
                             (g % 2) * 2048 + (it + 1) * GRP],
                        start=(it == 0), stop=(it == nit - 1)),
                        after=(tsq, ts3b) if it == 0 else (),
                        inc=chT if it == nit - 1 else None)
                g_sq[g % 2] = tsum
                t_sumcp[g] = kb.op("V", lambda v, g=g, s3r=s3r: v.tensor_copy(
                    out=s_sums[0:1, g * GRP:(g + 1) * GRP],
                    in_=s3r[0:1, 0:GRP]),
                    after=(tsum, g_time), inc=chV)
                g_ps3[pidx] = t_sumcp[g]

            last_smul = t_smul[g]
            if not s1_dbuf:
                emit_sum()
            else:
                if pending_sum is not None:
                    pending_sum()
                pending_sum = emit_sum
        if pending_sum is not None:
            pending_sum()
            pending_sum = None
        glast = groups[-1]
        tq = kb.op("S", lambda s: s.activation(
            out=s_timeb[0:1, g0 * GRP:(glast + 1) * GRP],
            in_=s_sums[0:1, g0 * GRP:(glast + 1) * GRP],
            func=AF.Sqrt, bias=c_one[0:1, :]),
            after=(t_sumcp[glast], t_ones_time, g_time, t_cst), inc=chS)
        t_last_k2 = None
        t_last_w2 = None
        for g in groups:
            sfo = (g - g0) * sfs
            for tt in range(2):
                lt = g * 2 + tt
                gi = h * (HALF // 128) + lt
                po = lt % 4
                cb = 32 * (lt % 2)
                pob = p_o2[po]
                tmm = None
                for kt in range(nkt):
                    for nh in range(2):
                        wnh = 512 if nh == 0 else 511
                        af = ()
                        if kt == 0 and nh == 0:
                            af = (t_smul[g], g_o2[po][0], g_o2[po][1],
                                  t_sumcp[glast])
                        tmm = kb.op("T", lambda t, sfo=sfo, tt=tt, kt=kt, nh=nh,
                                    wnh=wnh, pob=pob: t.matmul(
                            pob(nh * 512, nh * 512 + wnh),
                            s_sf[:, sfo + kt * GRP + tt * 128:
                                 sfo + kt * GRP + tt * 128 + 128],
                            wsl[:, w2o + kt * NSH + nh * 512:
                                w2o + kt * NSH + nh * 512 + wnh],
                            start=(kt == 0), stop=False), after=af)
                for nh in range(2):
                    wnh = 512 if nh == 0 else 511
                    tmm = kb.op("T", lambda t, g=g, tt=tt, nh=nh, wnh=wnh,
                                pob=pob: t.matmul(
                        pob(nh * 512, nh * 512 + wnh),
                        s_timeb[0:2, g * GRP + tt * 128:g * GRP + tt * 128 + 128],
                        wsl[0:2, r0o + nh * 512:r0o + nh * 512 + wnh],
                        start=False, stop=True),
                        after=(tq,) if nh == 0 else (),
                        inc=chT if nh == 1 else None)
                t_last_k2 = tmm
                g_sfgrp[g] = tmm
                t_last_w2 = tmm
                pz = lt % 2
                zsl = s_w32[:, pz * D:(pz + 1) * D]
                ysl = s_y[:, lt * D:(lt + 1) * D]
                bscr = s_b16[:, pz * D:pz * D + NSH]
                tob = kb.op("V", lambda v, pob=pob, bscr=bscr:
                            v.tensor_copy(out=bscr, in_=pob(0, NSH)),
                            after=(tmm, g_bscr[pz]), inc=chV)
                tq1 = kb.op("S", lambda s, bscr=bscr, cb=cb:
                            s.activation(out=s_sqs[:, 0:NSH], in_=bscr,
                                         func=AF.Square, accum_out=C(cb)),
                            after=(tob,), inc=chS)
                g_bscr[pz] = tq1
                tt0 = kb.op("S", lambda s, cb=cb: s.activation(
                    out=C(cb + 1), in_=C(cb), func=AF.Sqrt, bias=c_one),
                    after=(tq1, t_cst), inc=chS)
                if wte_e is not None:
                    wcol = s_wte[:, gi * 16 + wte_e:gi * 16 + wte_e + 1]
                    hcol = s_hit[:, gi * 16 + wte_e:gi * 16 + wte_e + 1]
                else:
                    wcol = hcol = None
                tzz = kb.op("V", lambda v, pob=pob, zsl=zsl, ysl=ysl,
                            wcol=wcol: v.scalar_tensor_tensor(
                    out=zsl[:, 1:D], in0=pob(0, NSH),
                    scalar=(wcol if wcol is not None else 1.0),
                    in1=ysl[:, 1:D], op0=ALU.mult, op1=ALU.add),
                    after=(tmm, g_zz[pz], g_y[lt]),
                    inc=chV)
                tz0 = kb.op("V", lambda v, zsl=zsl, ysl=ysl, cb=cb,
                            wcol=wcol: v.scalar_tensor_tensor(
                    out=zsl[:, 0:1], in0=C(cb + 1),
                    scalar=(wcol if wcol is not None else 1.0),
                    in1=ysl[:, 0:1], op0=ALU.mult, op1=ALU.add),
                    after=(tt0,), inc=chV)
                tS1 = kb.op("S", lambda s, zsl=zsl, cb=cb: s.activation(
                    out=s_sqs[:, 0:NSH], in_=zsl[:, 1:D], func=AF.Square,
                    accum_out=C(cb + 2)), after=(tzz,), inc=chS)
                tdd = kb.op("V", lambda v, zsl=zsl, cb=cb:
                            v.scalar_tensor_tensor(
                    out=C(cb + 4), in0=zsl[:, 0:1], scalar=zsl[:, 0:1],
                    in1=C(cb + 2), op0=ALU.mult, op1=ALU.subtract),
                    after=(tz0, tS1), inc=chV)
                tdm = kb.op("V", lambda v, cb=cb: v.tensor_scalar(
                    out=C(cb + 5), in0=C(cb + 4), scalar1=EPS, scalar2=None,
                    op0=ALU.max), after=(tdd,), inc=chV)
                trc = kb.op("V", lambda v, cb=cb: v.reciprocal(
                    out=C(cb + 6), in_=C(cb + 5)), after=(tdm,), inc=chV)
                tsq2 = kb.op("S", lambda s, cb=cb: s.activation(
                    out=C(cb + 7), in_=C(cb + 6), func=AF.Sqrt),
                    after=(trc,), inc=chS)
                if hcol is not None:
                    tr1 = kb.op("V", lambda v, cb=cb: v.tensor_scalar(
                        out=C(cb + 8), in0=C(cb + 7), scalar1=1.0,
                        scalar2=None, op0=ALU.subtract), after=(tsq2,),
                        inc=chV)
                    tr2 = kb.op("V", lambda v, cb=cb, hcol=hcol:
                                v.scalar_tensor_tensor(
                        out=C(cb + 9), in0=hcol, scalar=C(cb + 8),
                        in1=s_onef[:, :], op0=ALU.mult, op1=ALU.add),
                        after=(tr1, t_onef), inc=chV)
                    rcol = C(cb + 9)
                    rtok = tr2
                else:
                    rcol = C(cb + 7)
                    rtok = tsq2
                ty = kb.op("S", lambda s, zsl=zsl, ysl=ysl, rcol=rcol:
                           s.activation(out=ysl, in_=zsl, func=AF.Identity,
                                        scale=rcol),
                           after=(rtok, tzz, tz0), inc=chS)
                g_y[lt] = ty
                g_zz[pz] = ty
                g_o2[po] = (tzz, tob)
                if s1_dbuf:
                    if po < 2:
                        g_ps1[po] = (tzz, tob)
                    else:
                        g_ps3[po - 2] = tzz
                else:
                    g_ps1[0] = (tzz, tob)
        g_we[pe] = t_last_w2
        if we_cols > WE_COLS:
            g_we[0] = g_we[1] = t_last_w2
        g_beb[pe] = t_s3b[NG - 1]
        g_time = t_last_k2
        return ty

    kb.wcache = {}
    NLT = HALF // 128
    ty_last = None
    for h in range(2):
        yg = tuple(t for t in g_y if t is not None)
        ty0 = kb.op("G", lambda g: g.memset(s_y[:, :], 0.0),
                    after=yg + ((a_done) if h == 0 else ()), inc=chGc)
        ty1 = kb.op("G", lambda g: g.memset(
            bass.AP(s_y, 0, [[NLT * D, 128], [D, NLT], [1, 1]]), 1.0),
            after=(ty0,), inc=chGc)
        for lt in range(NLT):
            g_y[lt] = ty1
        esB = contextlib.ExitStack()
        psA = esB.enter_context(pt("psA_%d" % h, [128, 2048], F32))
        psB = esB.enter_context(pt("psB_%d" % h, [128, 2048], F32))
        psab = [psA, psB]
        ps1_of = lambda g: psab[g % 2][:, 0:1024]
        ps3_of = lambda g: psab[g % 2][:, 1024:2048]
        po2 = [lambda lo, hi: psA[:, lo:hi],
               lambda lo, hi: psB[:, lo:hi],
               lambda lo, hi: psA[:, 1024 + lo:1024 + hi],
               lambda lo, hi: psB[:, 1024 + lo:1024 + hi]]
        for e in range(NE):
            if e == 0:
                emit_wload(0, WE_COLS, d_wec[0, :, :], d_bexp[0, :, :],
                           first=(h == 0))
            pf = None
            if e + 1 < NE:
                pf = (lambda e1=e + 1: emit_wload(
                    e1 % 2, WE_COLS, d_wec[e1, :, :], d_bexp[e1, :, :]))
            ty_last = ffn_expert(h, e, e % 2, (h == 0 and e == 0),
                                 ps1_of, ps3_of, po2, WE_COLS,
                                 W1_OFF, W3_OFF, W2_OFF, R0_OFF, 4, 4,
                                 d_wec[e, :, :], d_bexp[e, :, :], 512, e,
                                 skip_w=True, prefetch=pf)
        esB.close()
        g_ps1[0] = g_ps1[1] = (ty_last,)
        g_ps3[0] = g_ps3[1] = ty_last
        for _i in range(4):
            g_o2[_i] = (ty_last, ty_last)
        esC = contextlib.ExitStack()
        p_s1C = esC.enter_context(pt("p_s1C_%d" % h, [128, 2048], F32))
        p_s3C = esC.enter_context(pt("p_s3C_%d" % h, [128, 2048], F32))
        po2C = [lambda lo, hi: p_s1C[:, lo:hi],
                lambda lo, hi: p_s1C[:, 1024 + lo:1024 + hi],
                lambda lo, hi: p_s3C[:, lo:hi],
                lambda lo, hi: p_s3C[:, 1024 + lo:1024 + hi]]
        ps1C_of = lambda g: p_s1C[:, :]
        ps3C_of = lambda g: p_s3C[:, :]
        for sb in range(2):
            groups = tuple(range(sb * 4, sb * 4 + 4))
            ty_last = ffn_expert(h, 100 + sb, 0, False,
                                 ps1C_of, ps3C_of, po2C, WS_COLS,
                                 SW1_OFF, SW3_OFF, SW2_OFF, SR0_OFF, 8, 8,
                                 d_wshc, None, 1024, None,
                                 groups=groups, sfs=2048, skip_w=(sb == 1),
                                 s1_dbuf=False)
            g_ps1[0] = g_ps1[1] = (ty_last,)
            g_ps3[0] = g_ps3[1] = ty_last
            for _i in range(4):
                g_o2[_i] = (ty_last, ty_last)
        esC.close()
        for lt in range(NLT):
            gi = h * NLT + lt
            px = lt % 2
            cb = 32 * px + 16
            ysl = s_y[:, lt * D:(lt + 1) * D]
            xsl = s_x[:, px * D:(px + 1) * D]
            wsl2 = s_w32[:, px * D:(px + 1) * D]
            tx2l = kb.op("Y", lambda y, gi=gi, xsl=xsl: y.dma_start(
                out=xsl, in_=d_x2[gi * 128:(gi + 1) * 128, :]),
                after=(g_x[px], tok_x2sp[NT - 2], tok_x2sp[NT - 1]),
                inc=chX[px], k=16)
            rstd, mb, tln = ln_stats(ysl, cb, "sqrt", (g_y[lt],))
            txf = kb.op("S", lambda s, xsl=xsl, wsl2=wsl2, mb=mb: s.activation(
                out=wsl2, in_=xsl, func=AF.Identity, bias=mb),
                after=(tx2l, tln[1], g_zz[px]), inc=chS)
            tst = kb.op("V", lambda v, ysl=ysl, xsl=xsl, wsl2=wsl2, rstd=rstd:
                        v.scalar_tensor_tensor(
                out=xsl, in0=ysl, scalar=rstd, in1=wsl2,
                op0=ALU.mult, op1=ALU.add), after=(txf, tln[0]), inc=chV)
            tout = kb.op("Y", lambda y, gi=gi, xsl=xsl: y.dma_start(
                out=d_out[gi * 128:(gi + 1) * 128, :], in_=xsl),
                after=(tst,), inc=chOut[px], k=16)
            out_toks.append(tout)
            g_x[px] = tout
            g_zz[px] = tst
            g_y[lt] = tst
    esF.close()

    if debug:
        tdbg1 = kb.op("Y", lambda y: y.dma_start(
            out=d_dbg[:, 0:NT * 16], in_=s_wte[:, :]),
            after=(out_toks[-1],), inc=chC, k=16)
        tdbg2 = kb.op("Y", lambda y: y.dma_start(
            out=d_dbg[:, 512:512 + NT * 16], in_=s_hit[:, :]),
            after=(tdbg1,), inc=chC, k=16)
        out_toks.append(tdbg2)

    kb.op("G", lambda g: g.nop(), after=tuple(out_toks[-3:]))

    with nc.Block() as block:
        kb.emit(block)
    es.close()
    return nc


_NC_CACHE = {}


def _get_nc(debug=False):
    if debug not in _NC_CACHE:
        _NC_CACHE[debug] = build_nc(debug)
    return _NC_CACHE[debug]


def kernel(**inputs):
    g, x = pack_host(inputs)
    nc = _get_nc(False)
    in_maps = []
    for b in range(B):
        m = dict(g)
        m["x"] = np.ascontiguousarray(x[b])
        in_maps.append(m)
    res = run_bass_kernel_spmd(nc, in_maps, core_ids=list(range(B)))
    return np.stack([r["out"] for r in res.results]).astype(np.float32)


# version 12
# speedup vs baseline: 1.1335x; 1.0338x over previous
import contextlib

import numpy as np

import concourse.bass as bass
import concourse.mybir as mybir
from concourse.bass_utils import run_bass_kernel_spmd

F32 = mybir.dt.float32
BF16 = mybir.dt.bfloat16
AF = mybir.ActivationFunctionType
ALU = mybir.AluOpType
AX = mybir.AxisListType

B, T, D, H = 8, 4096, 1024, 64
NE, NI = 16, 511
NSH = 1023
NT = T // 128
HALF = T // 2
GRP = 256
NG = HALF // GRP
EPS_LN = 1e-5
EPS = 1e-6

W1_OFF = 0
W3_OFF = 8 * 512
W2_OFF = 2 * 8 * 512
R0_OFF = W2_OFF + 4 * NSH
WE_COLS = R0_OFF + NSH

SW1_OFF = 0
SW3_OFF = 8 * 1024
SW2_OFF = 2 * 8 * 1024
SR0_OFF = SW2_OFF + 8 * NSH
WS_COLS = SR0_OFF + NSH


class _Chan:
    __slots__ = ("sem", "n", "name", "owner")

    def __init__(self, sem, name, owner):
        self.sem, self.n, self.name, self.owner = sem, 0, name, owner


class KB:
    ENGS = ("Y", "G", "T", "V", "S")

    def __init__(self, nc, es):
        self.nc = nc
        self.es = es
        self.items = []
        self.chans = []

    def chan(self, name, owner=None):
        ch = _Chan(self.es.enter_context(self.nc.semaphore(name)), name, owner)
        self.chans.append(ch)
        return ch

    def op(self, eng, emit, after=(), inc=None, k=1):
        idx = len(self.items)
        tok = None
        if inc is not None:
            tok = (inc, idx)
        after = tuple(t for t in after if t is not None)
        self.items.append([eng, after, emit, inc, k, idx])
        return tok

    def emit(self, block):
        used = set()
        for eng, after, _, _, _, _ in self.items:
            for (ch, idx) in after:
                if ch.owner == eng == "T":
                    continue
                used.add((ch.name, idx))
        val_at = {}
        counts = {}
        for it in self.items:
            eng, after, emitfn, inc, k, idx = it
            if inc is not None and (k == 16 or (inc.name, idx) in used):
                counts[inc.name] = counts.get(inc.name, 0) + k
                val_at[(inc.name, idx)] = counts[inc.name]
            elif inc is not None:
                it[3] = None
        decs = {"T": block.tensor, "V": block.vector, "S": block.scalar,
                "G": block.gpsimd, "Y": block.sync}
        for engname in self.ENGS:
            items = [it for it in self.items if it[0] == engname]
            if not items:
                continue

            def section(eng, items=items, engname=engname):
                waited = {}
                for _, after, emitfn, inc, k, idx in items:
                    for (ch, tidx) in after:
                        if ch.owner == engname == "T":
                            continue
                        val = val_at[(ch.name, tidx)]
                        if waited.get(ch.name, 0) < val:
                            eng.wait_ge(ch.sem, val)
                            waited[ch.name] = val
                    instr = emitfn(eng)
                    if inc is not None:
                        if instr is None:
                            instr = eng.nop()
                        instr.then_inc(inc.sem, k)

            decs[engname](section)


def _pack_T(w, rpad=None):
    R, K = w.shape
    assert K % 128 == 0
    rpad = R if rpad is None else rpad
    wt = np.zeros((K, rpad), np.float32)
    wt[:, :R] = w.T
    wt = wt.reshape(K // 128, 128, rpad)
    return np.ascontiguousarray(wt.transpose(1, 0, 2).reshape(128, (K // 128) * rpad))


def _pack_rows(wt, nrow_pad):
    R, C = wt.shape
    out = np.zeros((nrow_pad * 128, C), np.float32)
    out[:R] = wt
    out = out.reshape(nrow_pad, 128, C).transpose(1, 0, 2)
    return np.ascontiguousarray(out.reshape(128, nrow_pad * C))


def _bias_fm(b, ntile):
    out = np.zeros((ntile * 128,), np.float32)
    out[:b.shape[0]] = b
    return np.ascontiguousarray(out.reshape(ntile, 128).T)


def pack_host(inputs):
    f = np.float32
    g = {}
    g["wq"] = _pack_T(np.ascontiguousarray(inputs["L0_attn_W_q"].T))
    g["wk"] = _pack_T(np.ascontiguousarray(inputs["L0_attn_W_k"].T))
    g["wv"] = _pack_T(np.ascontiguousarray(inputs["L0_attn_W_v"].T))
    g["wo"] = np.ascontiguousarray(inputs["L0_attn_W_o"], dtype=f)
    g["gw"] = _pack_T(np.ascontiguousarray(inputs["L0_ffn_gate_w"]))
    blobs = []
    bias_blobs = []
    for e in range(NE):
        we1 = np.asarray(inputs["L0_ffn_We1"][e], f)
        we3 = np.asarray(inputs["L0_ffn_We3"][e], f)
        we2 = np.asarray(inputs["L0_ffn_We2"][e], f)
        be1 = np.asarray(inputs["L0_ffn_be1"][e], f)
        be3 = np.asarray(inputs["L0_ffn_be3"][e], f)
        be2 = np.asarray(inputs["L0_ffn_be2"][e], f)
        p1 = _pack_T(we1, 512)
        p3 = _pack_T(we3, 512)
        w2t = np.ascontiguousarray(we2.T)
        p2 = _pack_rows(w2t[1:], 4)
        r0 = np.zeros((128, NSH), f)
        r0[0] = w2t[0]
        r0[1] = be2
        blobs.append(np.concatenate([p1, p3, p2, r0], axis=1))
        bias_blobs.append(np.concatenate([_bias_fm(be1, 4), _bias_fm(be3, 4)],
                                         axis=1))
    g["wexp"] = np.stack(blobs)
    g["bexp"] = np.stack(bias_blobs)
    assert g["wexp"].shape == (NE, 128, WE_COLS), g["wexp"].shape
    ws1 = np.asarray(inputs["L0_ffn_Ws1"], f)
    ws3 = np.asarray(inputs["L0_ffn_Ws3"], f)
    ws2 = np.asarray(inputs["L0_ffn_Ws2"], f)
    sp1 = _pack_T(ws1, 1024)
    sp3 = _pack_T(ws3, 1024)
    ws2t = np.ascontiguousarray(ws2.T)
    sp2 = _pack_rows(ws2t[1:], 8)
    sr0 = np.zeros((128, NSH), f)
    sr0[0] = ws2t[0]
    sr0[1] = np.asarray(inputs["L0_ffn_bs2"], f)
    g["wsh"] = np.concatenate([sp1, sp3, sp2, sr0], axis=1)
    g["bsh"] = np.concatenate(
        [_bias_fm(np.asarray(inputs["L0_ffn_bs1"], f), 8),
         _bias_fm(np.asarray(inputs["L0_ffn_bs3"], f), 8)], axis=1)
    assert g["wsh"].shape == (128, WS_COLS), g["wsh"].shape
    import ml_dtypes
    ident = np.eye(128, dtype=f)
    g["identf"] = ident
    g["identb"] = ident.astype(ml_dtypes.bfloat16)
    ut = np.triu(np.ones((128, 128), f))
    g["utf"] = ut
    g["utb"] = ut.astype(ml_dtypes.bfloat16)
    onesb = np.ones((1, 128), f)
    g["onesb"] = onesb.astype(ml_dtypes.bfloat16)
    x = np.asarray(inputs["L0_attn_x"], f)
    return g, x


def build_nc(debug=False, stop_after=None, a3_gate=True, a3_xtspill=True,
             a3_level=9, detect_races=True):
    nc = bass.Bass("TRN2", target_bir_lowering=False,
                   detect_race_conditions=detect_races)
    es = contextlib.ExitStack()
    kb = KB(nc, es)

    dbg_kind = "ExternalOutput" if debug else "Internal"
    d_x = nc.dram_tensor("x", [T, D], F32, kind="ExternalInput").ap()
    d_wq = nc.dram_tensor("wq", [128, 8 * H], F32, kind="ExternalInput").ap()
    d_wk = nc.dram_tensor("wk", [128, 8 * H], F32, kind="ExternalInput").ap()
    d_wv = nc.dram_tensor("wv", [128, 8 * H], F32, kind="ExternalInput").ap()
    d_wo = nc.dram_tensor("wo", [H, D], F32, kind="ExternalInput").ap()
    d_gw = nc.dram_tensor("gw", [128, 8 * 16], F32, kind="ExternalInput").ap()
    d_wexp = nc.dram_tensor("wexp", [NE, 128, WE_COLS], F32, kind="ExternalInput").ap()
    d_wsh = nc.dram_tensor("wsh", [128, WS_COLS], F32, kind="ExternalInput").ap()
    d_bexp = nc.dram_tensor("bexp", [NE, 128, 8], F32, kind="ExternalInput").ap()
    d_bsh = nc.dram_tensor("bsh", [128, 16], F32, kind="ExternalInput").ap()
    d_idf = nc.dram_tensor("identf", [128, 128], F32, kind="ExternalInput").ap()
    d_idb = nc.dram_tensor("identb", [128, 128], BF16, kind="ExternalInput").ap()
    d_utf = nc.dram_tensor("utf", [128, 128], F32, kind="ExternalInput").ap()
    d_utb = nc.dram_tensor("utb", [128, 128], BF16, kind="ExternalInput").ap()
    d_onesb = nc.dram_tensor("onesb", [1, 128], BF16, kind="ExternalInput").ap()
    d_out = nc.dram_tensor("out", [T, D], F32, kind="ExternalOutput").ap()
    d_wec = nc.dram_tensor("wec", [NE, 128, WE_COLS], BF16, kind="Internal").ap()
    d_wshc = nc.dram_tensor("wshc", [128, WS_COLS], BF16, kind="Internal").ap()
    d_x2 = nc.dram_tensor("x2sp", [T, D], F32, kind=dbg_kind).ap()
    d_xt = nc.dram_tensor("xtsp", [8, 128, T], BF16, kind=dbg_kind).ap()
    d_dbg = None
    if debug:
        d_dbg = nc.dram_tensor("dbg", [128, 4096], F32, kind="ExternalOutput").ap()

    chL = [kb.chan("chL0"), kb.chan("chL1")]
    chLg = [kb.chan("chLg0"), kb.chan("chLg1")]
    chCv = [kb.chan("chCv0"), kb.chan("chCv1")]
    chC = kb.chan("chC")
    chX = [kb.chan("chX0"), kb.chan("chX1")]
    chSp = [kb.chan("chSp0"), kb.chan("chSp1")]
    chXt = [kb.chan("chXt0"), kb.chan("chXt1")]
    chXL = [kb.chan("chXL0"), kb.chan("chXL1")]
    chWt = [kb.chan("chWt0"), kb.chan("chWt1")]
    chBe = [kb.chan("chBe0"), kb.chan("chBe1")]
    chOut = [kb.chan("chOut0"), kb.chan("chOut1")]
    chT = kb.chan("chT", "T")
    chV = kb.chan("chV", "V")
    chS = kb.chan("chS", "S")
    chGc = kb.chan("chGc", "G")

    st = nc.sbuf_tensor

    s_x = es.enter_context(st("s_x", [128, 2 * D], F32))
    s_w32 = es.enter_context(st("s_w32", [128, 2 * D], F32))
    s_b16 = es.enter_context(st("s_b16", [128, 2 * D], BF16))
    s_sqs = es.enter_context(st("s_sqs", [128, D], BF16))
    s_wqb = es.enter_context(st("s_wqb", [128, 8 * H], BF16))
    s_wkb = es.enter_context(st("s_wkb", [128, 8 * H], BF16))
    s_wvb = es.enter_context(st("s_wvb", [128, 8 * H], BF16))
    s_wob = es.enter_context(st("s_wob", [H, D], BF16))
    s_gw = es.enter_context(st("s_gw", [128, 8 * 16], F32))
    s_idf = es.enter_context(st("s_idf", [128, 128], F32))
    s_idb = es.enter_context(st("s_idb", [128, 128], BF16))
    s_utf = es.enter_context(st("s_utf", [128, 128], F32))
    s_utb = es.enter_context(st("s_utb", [128, 128], BF16))
    s_onesb = es.enter_context(st("s_onesb", [1, 128], BF16))
    s_wte = es.enter_context(st("s_wte", [128, NT * 16], F32))
    s_hit = es.enter_context(st("s_hit", [128, NT * 16], F32))
    s_y = es.enter_context(st("s_y", [128, (HALF // 128) * D], F32))
    s_c = es.enter_context(st("s_c", [128, 64], F32))
    s_on1 = es.enter_context(st("s_on1", [128, 1], BF16))
    s_cst = es.enter_context(st("s_cst", [128, 3], F32))

    def C(i):
        return s_c[:, i:i + 1]

    c_epsln = s_cst[:, 0:1]
    c_one = s_cst[:, 1:2]
    c_mhalf = s_cst[:, 2:3]
    t_cst = kb.op("G", lambda g: g.memset(s_cst[:, 0:1], EPS_LN), inc=None)
    t_cst = kb.op("G", lambda g: g.memset(s_cst[:, 1:2], 1.0), inc=None)
    t_cst = kb.op("G", lambda g: g.memset(s_cst[:, 2:3], -0.5), inc=chGc)

    def emit_weight_conversions():
        tok = None
        for e in range(NE):
            tok = kb.op("G", lambda gp, e=e: gp.dma_start(
                out=d_wec[e, :, :], in_=d_wexp[e, :, :]), inc=chCv[0], k=16)
        tok = kb.op("G", lambda gp: gp.dma_start(out=d_wshc, in_=d_wsh),
                    inc=chCv[0], k=16)
        return (tok,)

    tok_wcache = emit_weight_conversions()

    def ld(dst, src):
        return kb.op("Y", lambda y: y.dma_start(out=dst, in_=src), inc=chC, k=16)

    for dst, srcc in ((s_idf[:, :], d_idf), (s_idb[:, :], d_idb),
                      (s_utf[:, :], d_utf), (s_utb[:, :], d_utb),
                      (s_onesb[:, :], d_onesb)):
        ld(dst, srcc)
    t_consts = [ld(s_gw[:, :], d_gw)]
    t_aw = []
    for dsrc, dst in ((d_wq, s_wqb), (d_wk, s_wkb), (d_wv, s_wvb)):
        t_aw.append(kb.op("G", lambda gp, s=dsrc, d=dst: gp.dma_start(
            out=d[:, :], in_=s), inc=chCv[1], k=16))
    t_aw.append(kb.op("G", lambda gp: gp.dma_start(
        out=s_wob[:, :], in_=d_wo), inc=chCv[1], k=16))
    t_consts += t_aw[-1:]

    if stop_after == "W":
        kb.op("G", lambda g: g.nop(), after=(t_consts[0],))
        with nc.Block() as block:
            kb.emit(block)
        es.close()
        return nc

    esA = contextlib.ExitStack()
    pt = nc.psum_tensor
    s_qf = esA.enter_context(st("s_qf", [128, T], BF16))
    s_vf = esA.enter_context(st("s_vf", [128, T], BF16))
    s_qt = esA.enter_context(st("s_qt", [128, NT * H], F32))
    s_kt = esA.enter_context(st("s_kt", [128, NT * H], BF16))
    s_vt = esA.enter_context(st("s_vt", [128, NT * H], BF16))
    s_ot = esA.enter_context(st("s_ot", [128, NT * H], BF16))
    s_of = esA.enter_context(st("s_of", [128, T], BF16))
    s_lnf = esA.enter_context(st("s_lnf", [128, 2 * D], BF16))
    s_ph = esA.enter_context(st("s_ph", [128, 3 * 128], F32))
    s_S = esA.enter_context(st("s_S", [64, 2 * H], F32))
    s_Sb = esA.enter_context(st("s_Sb", [64, H], BF16))
    s_z = esA.enter_context(st("s_z", [1, 2 * H], F32))
    s_zb = esA.enter_context(st("s_zb", [1, H], BF16))
    s_fm32 = esA.enter_context(st("s_fm32", [128, D], F32))
    s_g16 = esA.enter_context(st("s_g16", [128, 2 * 16], F32))
    s_g16b = esA.enter_context(st("s_g16b", [128, 2 * 16], F32))
    s_gm = esA.enter_context(st("s_gm", [128, 2 * 128], BF16))

    t_on1 = kb.op("G", lambda g: g.memset(s_on1[:, :], 1.0), inc=chGc)
    t_S0 = kb.op("G", lambda g: g.memset(s_S[:, :], 0.0), inc=chGc)
    t_S0b = kb.op("G", lambda g: g.memset(s_Sb[:, :], 0.0), inc=chGc)
    t_z0 = kb.op("G", lambda g: g.memset(s_z[:, :], 0.0), inc=chGc)
    t_z0b = kb.op("G", lambda g: g.memset(s_zb[:, :], 0.0), inc=chGc)

    def ln_stats(src_ap, cb, variant, after_src):
        t1 = kb.op("V", lambda v: v.tensor_reduce(
            out=C(cb), in_=src_ap, axis=AX.X, op=ALU.add),
            after=after_src, inc=chV)
        t2 = kb.op("S", lambda s: s.activation(
            out=s_sqs[:, :], in_=src_ap, func=AF.Square, accum_out=C(cb + 1)),
            after=after_src, inc=chS)
        t3 = kb.op("V", lambda v: v.tensor_scalar(
            out=C(cb + 2), in0=C(cb), scalar1=-1.0 / D, scalar2=None,
            op0=ALU.mult), after=(t1,), inc=chV)
        t4 = kb.op("V", lambda v: v.tensor_tensor(
            out=C(cb + 3), in0=C(cb + 2), in1=C(cb + 2), op=ALU.mult),
            after=(t3,), inc=chV)
        t5 = kb.op("V", lambda v: v.scalar_tensor_tensor(
            out=C(cb + 4), in0=C(cb + 1), scalar=1.0 / D, in1=C(cb + 3),
            op0=ALU.mult, op1=ALU.subtract), after=(t2, t4), inc=chV)
        if variant == "expln":
            t6 = kb.op("S", lambda s: s.activation(
                out=C(cb + 5), in_=C(cb + 4), func=AF.Ln, bias=c_epsln),
                after=(t5, t_cst), inc=chS)
            t7 = kb.op("S", lambda s: s.activation(
                out=C(cb + 6), in_=C(cb + 5), func=AF.Exp, scale=c_mhalf),
                after=(t6,), inc=chS)
            t8 = kb.op("S", lambda s: s.activation(
                out=C(cb + 7), in_=C(cb + 2), func=AF.Identity,
                scale=C(cb + 6)), after=(t7,), inc=chS)
        else:
            t6 = kb.op("S", lambda s: s.activation(
                out=C(cb + 5), in_=C(cb + 4), func=AF.Sqrt, bias=c_epsln),
                after=(t5, t_cst), inc=chS)
            t7 = kb.op("V", lambda v: v.reciprocal(
                out=C(cb + 6), in_=C(cb + 5)), after=(t6,), inc=chV)
            t8 = kb.op("V", lambda v: v.tensor_scalar(
                out=C(cb + 7), in0=C(cb + 2), scalar1=C(cb + 6), scalar2=None,
                op0=ALU.mult), after=(t7,), inc=chV)
        return C(cb + 6), C(cb + 7), (t7, t8)

    def phi(src_ap, dst_ap, w, after, np_=128):
        ta = kb.op("V", lambda v: v.tensor_scalar(
            out=s_ph[0:np_, 0:w], in0=src_ap, scalar1=0.0, scalar2=None,
            op0=ALU.min), after=after, inc=chV)
        tb = kb.op("S", lambda s: s.activation(
            out=s_ph[0:np_, 128:128 + w], in_=s_ph[0:np_, 0:w], func=AF.Exp),
            after=(ta,), inc=chS)
        tc = kb.op("V", lambda v: v.tensor_scalar(
            out=s_ph[0:np_, 256:256 + w], in0=src_ap, scalar1=0.0,
            scalar2=None, op0=ALU.max), after=after, inc=chV)
        td = kb.op("V", lambda v: v.tensor_tensor(
            out=dst_ap, in0=s_ph[0:np_, 128:128 + w],
            in1=s_ph[0:np_, 256:256 + w],
            op=ALU.add), after=(tb, tc), inc=chV)
        return td

    esA1 = contextlib.ExitStack()
    p_trA = esA1.enter_context(pt("p_trA", [128, D], BF16))
    p_qkv = esA1.enter_context(pt("p_qkv", [128, 512], F32))
    p_G = esA1.enter_context(pt("p_G", [128, 128], F32))
    p_oc = esA1.enter_context(pt("p_oc", [128, H], F32))
    p_zc = esA1.enter_context(pt("p_zc", [128, H], F32))
    p_z = esA1.enter_context(pt("p_z", [1, H], F32))
    p_S = esA1.enter_context(pt("p_S", [64, H], F32))
    p_of = esA1.enter_context(pt("p_of", [64, 128], BF16))
    g_x = [None, None]
    g_trA = None
    g_qkv = None
    tok_qf = [None] * NT
    tok_vf = [None] * NT
    tok_qt = [None] * NT
    tok_kt = [None] * NT
    tok_vt = [None] * NT
    g_lnf = [None, None]
    a1_tr_last = None

    def a1_tile(i):
        nonlocal g_trA, g_qkv, a1_tr_last
        p = i % 2
        cb = 32 * p
        xsl = s_x[:, p * D:(p + 1) * D]
        tx = kb.op("Y", lambda y, i=i, xsl=xsl: y.dma_start(
            out=xsl, in_=d_x[i * 128:(i + 1) * 128, :]),
            after=(g_x[p],) + (tuple(t_consts) if i < 2 else ()),
            inc=chX[p], k=16)
        rstd, mb, tln = ln_stats(xsl, cb, "expln", (tx,))
        lnsl = s_b16[:, p * D:(p + 1) * D]
        tln1 = kb.op("S", lambda s, xsl=xsl, lnsl=lnsl, rstd=rstd, mb=mb:
                     s.activation(out=lnsl, in_=xsl, func=AF.Identity,
                                  scale=rstd, bias=mb),
                     after=tln + (g_lnf[p],), inc=chS)
        g_x[p] = tln1
        ttr = None
        for k in range(8):
            ttr = kb.op("T", lambda t, k=k, lnsl=lnsl: t.transpose(
                p_trA[:, k * 128:(k + 1) * 128], lnsl[:, k * 128:(k + 1) * 128],
                s_idb[:, :]),
                after=(tln1, g_trA) if k == 0 else (),
                inc=chT if k == 7 else None)
        a1_tr_last = ttr
        fmsl = s_lnf[:, p * D:(p + 1) * D]
        tfm = kb.op("V", lambda v, fmsl=fmsl: v.tensor_copy(
            out=fmsl, in_=p_trA[:, :]), after=(ttr,), inc=chV)
        g_trA = tfm
        tmm = None
        forms = [
            lambda t, k, fmsl: t.matmul(
                p_qkv[0:64, 0:128], s_wqb[:, k * H:(k + 1) * H],
                fmsl[:, k * 128:(k + 1) * 128], start=(k == 0), stop=(k == 7)),
            lambda t, k, fmsl: t.matmul(
                p_qkv[0:64, 128:256], s_wvb[:, k * H:(k + 1) * H],
                fmsl[:, k * 128:(k + 1) * 128], start=(k == 0), stop=(k == 7)),
            lambda t, k, fmsl: t.matmul(
                p_qkv[:, 256:320], fmsl[:, k * 128:(k + 1) * 128],
                s_wqb[:, k * H:(k + 1) * H], start=(k == 0), stop=(k == 7)),
            lambda t, k, fmsl: t.matmul(
                p_qkv[:, 320:384], fmsl[:, k * 128:(k + 1) * 128],
                s_wkb[:, k * H:(k + 1) * H], start=(k == 0), stop=(k == 7)),
            lambda t, k, fmsl: t.matmul(
                p_qkv[:, 384:448], fmsl[:, k * 128:(k + 1) * 128],
                s_wvb[:, k * H:(k + 1) * H], start=(k == 0), stop=(k == 7)),
        ]
        for fi, fn in enumerate(forms):
            for k in range(8):
                af = (tfm, g_qkv) if (fi == 0 and k == 0) else ()
                tmm = kb.op("T",
                            lambda t, fn=fn, k=k, fmsl=fmsl: fn(t, k, fmsl),
                            after=af,
                            inc=chT if (fi == 4 and k == 7) else None)
        g_lnf[p] = tmm
        tok_qf[i] = phi(p_qkv[0:64, 0:128], s_qf[0:64, i * 128:(i + 1) * 128],
                        128, (tmm,), np_=64)
        tok_qt[i] = phi(p_qkv[:, 256:320], s_qt[:, i * H:(i + 1) * H],
                        64, (tok_qf[i],))
        tok_kt[i] = phi(p_qkv[:, 320:384], s_kt[:, i * H:(i + 1) * H],
                        64, (tok_qt[i],))
        tok_vf[i] = kb.op("V", lambda v, i=i: v.tensor_copy(
            out=s_vf[0:64, i * 128:(i + 1) * 128], in_=p_qkv[0:64, 128:256]),
            after=(tmm,), inc=chV)
        tok_vt[i] = kb.op("V", lambda v, i=i: v.tensor_copy(
            out=s_vt[:, i * H:(i + 1) * H], in_=p_qkv[:, 384:448]),
            after=(tmm,), inc=chV)
        g_qkv = tok_vt[i]

    g_G = None
    g_gm = [None, None]
    g_oc = None
    g_zc = None
    g_of = None
    t_Scp = t_S0b
    t_zcp = t_z0b
    t_dSmm = None
    t_zsum = None
    tok_o = [None] * NT

    def a2_chunk(c):
        nonlocal g_G, g_oc, g_zc, g_of, t_Scp, t_zcp, t_dSmm, t_zsum
        p = c % 2
        cbase = 40 + 4 * p
        tG = kb.op("T", lambda t, c=c: t.matmul(
            p_G[:, :], s_vf[0:64, c * 128:(c + 1) * 128],
            s_qf[0:64, c * 128:(c + 1) * 128], start=True, stop=True),
            after=(tok_vf[c], tok_qf[c], g_G), inc=chT)
        gm = s_gm[:, p * 128:(p + 1) * 128]
        tGm = kb.op("V", lambda v, gm=gm: v.tensor_tensor(
            out=gm, in0=p_G[:, :], in1=s_utf[:, :], op=ALU.mult),
            after=(tG, g_gm[p]), inc=chV)
        g_G = tGm
        to1 = kb.op("T", lambda t, c=c, gm=gm: t.matmul(
            p_oc[:, :], gm, s_kt[:, c * H:(c + 1) * H], start=True, stop=False),
            after=(tGm, tok_kt[c], g_oc), inc=None)
        to2 = kb.op("T", lambda t, c=c: t.matmul(
            p_oc[:, :], s_qf[0:64, c * 128:(c + 1) * 128], s_Sb[:, :],
            start=False, stop=True), after=(t_Scp,), inc=chT)
        g_gm[p] = to2
        tz1 = kb.op("T", lambda t, c=c: t.matmul(
            p_zc[:, :], s_utb[:, :], s_kt[:, c * H:(c + 1) * H],
            start=True, stop=False), after=(g_zc, tok_kt[c]), inc=None)
        tz2 = kb.op("T", lambda t: t.matmul(
            p_zc[:, :], s_onesb[0:1, :], s_zb[0:1, :],
            start=False, stop=True), after=(t_zcp, t_on1), inc=chT)
        t_zsum = kb.op("T", lambda t, c=c: t.matmul(
            p_z[0:1, :], s_on1[:, :], s_kt[:, c * H:(c + 1) * H],
            start=True, stop=True),
            after=(t_zcp,), inc=chT)
        t_dSmm = kb.op("T", lambda t, c=c: t.matmul(
            p_S[:, :], s_vt[:, c * H:(c + 1) * H], s_kt[:, c * H:(c + 1) * H],
            start=True, stop=True),
            after=(tok_vt[c], t_Scp, to2), inc=chT)
        td1 = kb.op("V", lambda v, c=c: v.tensor_tensor(
            out=s_ph[:, 0:H], in0=p_zc[:, :], in1=s_qt[:, c * H:(c + 1) * H],
            op=ALU.mult), after=(tz2, tok_qt[c]), inc=chV)
        g_zc = td1
        td2 = kb.op("V", lambda v, cb=cbase: v.tensor_reduce(
            out=C(cb), in_=s_ph[:, 0:H], axis=AX.X, op=ALU.add),
            after=(td1,), inc=chV)
        td3 = kb.op("V", lambda v, cb=cbase: v.tensor_scalar(
            out=C(cb + 1), in0=C(cb), scalar1=1e-6, scalar2=None,
            op0=ALU.max), after=(td2,), inc=chV)
        td4 = kb.op("V", lambda v, cb=cbase: v.reciprocal(
            out=C(cb + 2), in_=C(cb + 1)), after=(td3,), inc=chV)
        to3 = kb.op("V", lambda v, c=c, cb=cbase: v.tensor_scalar(
            out=s_ot[:, c * H:(c + 1) * H], in0=p_oc[:, :],
            scalar1=C(cb + 2), scalar2=None, op0=ALU.mult),
            after=(to2, td4), inc=chV)
        g_oc = to3
        pc, pn = c % 2, (c + 1) % 2
        tSa = kb.op("V", lambda v, pc=pc, pn=pn: v.tensor_tensor(
            out=s_S[:, pn * H:(pn + 1) * H], in0=p_S[:, :],
            in1=s_S[:, pc * H:(pc + 1) * H], op=ALU.add),
            after=(t_dSmm, t_S0, t_Scp), inc=chV)
        t_Scp = kb.op("V", lambda v, pn=pn: v.tensor_copy(
            out=s_Sb[:, :], in_=s_S[:, pn * H:(pn + 1) * H]),
            after=(tSa, t_S0b), inc=chV)
        tza = kb.op("V", lambda v, pc=pc, pn=pn: v.tensor_tensor(
            out=s_z[0:1, pn * H:(pn + 1) * H], in0=p_z[0:1, :],
            in1=s_z[0:1, pc * H:(pc + 1) * H], op=ALU.add),
            after=(t_zsum, tz2, t_z0, t_zcp), inc=chV)
        t_zcp = kb.op("V", lambda v, pn=pn: v.tensor_copy(
            out=s_zb[0:1, :], in_=s_z[0:1, pn * H:(pn + 1) * H]),
            after=(tza, t_z0b), inc=chV)
        tof = kb.op("T", lambda t, c=c: t.transpose(
            p_of[:, :], s_ot[:, c * H:(c + 1) * H], s_idb[:, :]),
            after=(to3, g_of), inc=chT)
        tok_o[c] = kb.op("V", lambda v, c=c: v.tensor_copy(
            out=s_of[0:64, c * 128:(c + 1) * 128], in_=p_of[:, :]),
            after=(tof,), inc=chV)
        g_of = tok_o[c]

    for i in range(NT + 1):
        if i < NT:
            a1_tile(i)
        if i >= 1:
            a2_chunk(i - 1)
    esA1.close()

    esA3 = contextlib.ExitStack()
    p_at = [esA3.enter_context(pt("p_at0", [128, D], F32)),
            esA3.enter_context(pt("p_at1", [128, D], F32))]
    p_tr2 = esA3.enter_context(pt("p_tr2", [128, D], F32))
    p_gate = esA3.enter_context(pt("p_gate", [128, 16], F32))
    s_at = esA3.enter_context(st("s_at", [128, 2 * D], F32))
    s_glog = esA3.enter_context(st("s_glog", [128, 2 * 16], F32))
    g_at = [None, None]
    g_atS = [None, None]
    g_x = [g_x[0], g_x[1]]
    g_tr2 = None
    g_fm32 = None
    g_gate = None
    g_b16 = [g_lnf[0], g_lnf[1]]
    tok_x2sp = [None] * NT
    tok_xtsp = [None] * NT
    g_w32 = [None, None]
    for i in range(NT):
        p = i % 2
        cb = 32 * p
        atp = p_at[p]
        t1 = kb.op("T", lambda t, i=i, atp=atp: t.matmul(
            atp[:, 0:512], s_of[0:64, i * 128:(i + 1) * 128], s_wob[:, 0:512],
            start=True, stop=True),
            after=(tok_o[i], g_at[p]) + ((g_of, g_oc) if i < 2 else ()),
            inc=None)
        t2 = kb.op("T", lambda t, i=i, atp=atp: t.matmul(
            atp[:, 512:1024], s_of[0:64, i * 128:(i + 1) * 128],
            s_wob[:, 512:1024], start=True, stop=True), after=(), inc=chT)
        if a3_level < 2:
            tev = kb.op("V", lambda v, atp=atp, p=p: v.tensor_copy(
                out=s_w32[:, p * D:(p + 1) * D], in_=atp[:, :]),
                after=(t2, g_at[p], g_w32[p]), inc=chV)
            g_at[p] = tev
            g_w32[p] = tev
            tok_x2sp[i] = tok_xtsp[i] = tw2 = tev
            continue
        atsl = s_at[:, p * D:(p + 1) * D]
        tat = kb.op("V", lambda v, atp=atp, atsl=atsl: v.tensor_copy(
            out=atsl, in_=atp[:, :]), after=(t2, g_at[p], g_atS[p]), inc=chV)
        g_at[p] = tat
        if 11 <= a3_level <= 15:
            toks = []
            t1 = kb.op("V", lambda v, atsl=atsl, cb=cb: v.tensor_reduce(
                out=C(cb), in_=atsl, axis=AX.X, op=ALU.add),
                after=(tat,), inc=chV)
            toks.append(t1)
            if a3_level >= 12:
                t2s = kb.op("S", lambda s, atsl=atsl, cb=cb: s.activation(
                    out=s_sqs[:, :], in_=atsl, func=AF.Square,
                    accum_out=C(cb + 1)), after=(tat,), inc=chS)
                toks.append(t2s)
            if a3_level >= 13:
                t3 = kb.op("V", lambda v, cb=cb: v.tensor_scalar(
                    out=C(cb + 2), in0=C(cb), scalar1=1.0 / D, scalar2=None,
                    op0=ALU.mult), after=(t1,), inc=chV)
                t4 = kb.op("V", lambda v, cb=cb: v.tensor_tensor(
                    out=C(cb + 3), in0=C(cb + 2), in1=C(cb + 2), op=ALU.mult),
                    after=(t3,), inc=chV)
                t5 = kb.op("V", lambda v, cb=cb: v.scalar_tensor_tensor(
                    out=C(cb + 4), in0=C(cb + 1), scalar=1.0 / D,
                    in1=C(cb + 3), op0=ALU.mult, op1=ALU.subtract),
                    after=(t2s, t4), inc=chV)
                t8 = kb.op("V", lambda v, cb=cb: v.tensor_scalar(
                    out=C(cb + 7), in0=C(cb + 2), scalar1=C(cb + 4),
                    scalar2=-1.0, op0=ALU.mult, op1=ALU.mult),
                    after=(t5,), inc=chV)
                toks += [t8]
            if a3_level >= 14:
                t6 = kb.op("S", lambda s, cb=cb: s.activation(
                    out=C(cb + 5), in_=C(cb + 4), func=AF.Ln, bias=c_epsln),
                    after=(t5, t_cst), inc=chS)
                t7 = kb.op("S", lambda s, cb=cb: s.activation(
                    out=C(cb + 6), in_=C(cb + 5), func=AF.Exp, scale=c_mhalf),
                    after=(t6,), inc=chS)
                toks += [t7]
            tev = kb.op("V", lambda v, atsl=atsl, p=p: v.tensor_copy(
                out=s_w32[:, p * D:(p + 1) * D], in_=atsl),
                after=tuple(toks) + (g_w32[p],), inc=chV)
            g_atS[p] = tev
            g_w32[p] = tev
            tok_x2sp[i] = tok_xtsp[i] = tw2 = tev
            continue
        rstd, mb, tln = ln_stats(atsl, cb, "expln", (tat,))
        xsl = s_x[:, p * D:(p + 1) * D]
        tx = kb.op("Y", lambda y, i=i, xsl=xsl: y.dma_start(
            out=xsl, in_=d_x[i * 128:(i + 1) * 128, :]),
            after=(g_x[p],), inc=chX[p], k=16)
        wsl = s_w32[:, p * D:(p + 1) * D]
        txf = kb.op("S", lambda s, xsl=xsl, wsl=wsl, mb=mb: s.activation(
            out=wsl, in_=xsl, func=AF.Identity, bias=mb),
            after=(tx, tln[1], g_w32[p]), inc=chS)
        tx2 = kb.op("V", lambda v, atsl=atsl, xsl=xsl, wsl=wsl, rstd=rstd:
                    v.scalar_tensor_tensor(
                        out=xsl, in0=atsl, scalar=rstd, in1=wsl,
                        op0=ALU.mult, op1=ALU.add),
                    after=(txf, tln[0]), inc=chV)
        g_atS[p] = tx2
        tok_x2sp[i] = kb.op("Y", lambda y, i=i, xsl=xsl: y.dma_start(
            out=d_x2[i * 128:(i + 1) * 128, :], in_=xsl),
            after=(tx2,), inc=chSp[p], k=16)
        if a3_level < 3:
            tok_xtsp[i] = tw2 = tok_x2sp[i]
            g_x[p] = tok_x2sp[i]
            continue
        rstd2, mb2, tln2 = ln_stats(xsl, cb + 8, "expln", (tx2,))
        tln2f = kb.op("S", lambda s, xsl=xsl, wsl=wsl, rstd2=rstd2, mb2=mb2:
                      s.activation(out=wsl, in_=xsl, func=AF.Identity,
                                   scale=rstd2, bias=mb2),
                      after=tln2 + (txf,), inc=chS)
        g_x[p] = tok_x2sp[i]
        if a3_level < 4:
            tok_xtsp[i] = tw2 = tln2f
            g_w32[p] = tln2f
            continue
        ttr = None
        for k in range(8):
            ttr = kb.op("T", lambda t, k=k, wsl=wsl: t.transpose(
                p_tr2[:, k * 128:(k + 1) * 128], wsl[:, k * 128:(k + 1) * 128],
                s_idf[:, :]),
                after=(tln2f, g_tr2) if k == 0 else (),
                inc=chT if k == 7 else None)
        g_w32[p] = ttr
        tfm32 = kb.op("V", lambda v: v.tensor_copy(
            out=s_fm32[:, :], in_=p_tr2[:, :]), after=(ttr, g_fm32), inc=chV)
        bsl = s_b16[:, p * D:(p + 1) * D]
        txtb = kb.op("V", lambda v, bsl=bsl: v.tensor_copy(
            out=bsl, in_=p_tr2[:, :]),
            after=(ttr, g_b16[p], tfm32), inc=chV)
        g_tr2 = txtb
        if a3_xtspill:
            tok_xtsp[i] = kb.op("Y", lambda y, i=i, bsl=bsl: y.dma_start(
                out=d_xt[:, :, i * 128:(i + 1) * 128].rearrange("k p n -> p k n"),
                in_=bsl.rearrange("p (k n) -> p k n", k=8)),
                after=(txtb,), inc=chXt[p], k=16)
        else:
            tok_xtsp[i] = txtb
        g_b16[p] = tok_xtsp[i]
        if not a3_gate:
            tw2 = txtb
            continue
        tg = None
        for k in range(8):
            tg = kb.op("T", lambda t, k=k: t.matmul(
                p_gate[:, :], s_fm32[:, k * 128:(k + 1) * 128],
                s_gw[:, k * 16:(k + 1) * 16], start=(k == 0), stop=(k == 7)),
                after=(tfm32, g_gate) if k == 0 else (),
                inc=chT if k == 7 else None)
        g_fm32 = tg
        gsl = s_glog[:, p * 16:(p + 1) * 16]
        tgc = kb.op("V", lambda v, gsl=gsl: v.tensor_copy(
            out=gsl, in_=p_gate[:, :]), after=(tg, g_gate), inc=chV)
        tmx = kb.op("V", lambda v, cb=cb, gsl=gsl: v.reduce_max(
            out=C(cb + 16), in_=gsl, axis=AX.X), after=(tgc,), inc=chV)
        tnm = kb.op("V", lambda v, cb=cb: v.tensor_scalar(
            out=C(cb + 17), in0=C(cb + 16), scalar1=-1.0, scalar2=None,
            op0=ALU.mult), after=(tmx,), inc=chV)
        esl = s_g16[:, p * 16:(p + 1) * 16]
        bsl2 = s_g16b[:, p * 16:(p + 1) * 16]
        te = kb.op("S", lambda s, cb=cb, esl=esl, gsl=gsl: s.activation(
            out=esl, in_=gsl, func=AF.Exp, bias=C(cb + 17)),
            after=(tnm,), inc=chS)
        g_gate = tgc
        tse = kb.op("V", lambda v, cb=cb, esl=esl: v.tensor_reduce(
            out=C(cb + 18), in_=esl, axis=AX.X, op=ALU.add),
            after=(te,), inc=chV)
        trs = kb.op("V", lambda v, cb=cb: v.reciprocal(
            out=C(cb + 19), in_=C(cb + 18)), after=(tse,), inc=chV)
        tm1 = kb.op("V", lambda v, cb=cb, esl=esl: v.reduce_max(
            out=C(cb + 20), in_=esl, axis=AX.X), after=(te,), inc=chV)
        teq = kb.op("V", lambda v, cb=cb, esl=esl, bsl2=bsl2: v.tensor_scalar(
            out=bsl2, in0=esl, scalar1=C(cb + 20), scalar2=0.0,
            op0=ALU.subtract, op1=ALU.is_ge), after=(tm1,), inc=chV)
        tmk = kb.op("V", lambda v, esl=esl, bsl2=bsl2: v.scalar_tensor_tensor(
            out=bsl2, in0=bsl2, scalar=-1e30, in1=esl,
            op0=ALU.mult, op1=ALU.add), after=(teq,), inc=chV)
        tm2 = kb.op("V", lambda v, cb=cb, bsl2=bsl2: v.reduce_max(
            out=C(cb + 21), in_=bsl2, axis=AX.X), after=(tmk,), inc=chV)
        th = kb.op("V", lambda v, cb=cb, esl=esl, i=i: v.tensor_scalar(
            out=s_hit[:, i * 16:(i + 1) * 16], in0=esl, scalar1=C(cb + 21),
            scalar2=None, op0=ALU.is_ge), after=(tm2,), inc=chV)
        tw1 = kb.op("V", lambda v, cb=cb, esl=esl, bsl2=bsl2: v.tensor_scalar(
            out=bsl2, in0=esl, scalar1=C(cb + 19), scalar2=None,
            op0=ALU.mult), after=(trs, tm2), inc=chV)
        tw2 = kb.op("V", lambda v, i=i, bsl2=bsl2: v.tensor_tensor(
            out=s_wte[:, i * 16:(i + 1) * 16], in0=bsl2,
            in1=s_hit[:, i * 16:(i + 1) * 16], op=ALU.mult),
            after=(th, tw1), inc=chV)
    esA3.close()
    esA.close()
    a_done = (tok_x2sp[NT - 2], tok_x2sp[NT - 1], tok_xtsp[NT - 2],
              tok_xtsp[NT - 1], tw2) + tok_wcache

    if stop_after == "A":
        kb.op("G", lambda g: g.nop(), after=a_done)
        with nc.Block() as block:
            kb.emit(block)
        es.close()
        return nc

    esF = contextlib.ExitStack()
    s_we = esF.enter_context(st("s_we", [128, 2 * WE_COLS], BF16))
    s_beb = esF.enter_context(st("s_beb", [128, 2 * 8], F32))
    s_xtb = esF.enter_context(st("s_xtb", [128, 2 * 8 * GRP], BF16))
    s_sf = esF.enter_context(st("s_sf", [128, NG * 1024], BF16))
    s_a = esF.enter_context(st("s_a", [128, 2048], BF16))
    s_s3b = esF.enter_context(st("s_s3b", [128, 2048], BF16))
    s_sq = esF.enter_context(st("s_sq", [128, 2 * 2048], BF16))
    s_sums = esF.enter_context(st("s_sums", [1, NG * GRP], F32))
    s_timeb = esF.enter_context(st("s_timeb", [2, NG * GRP], BF16))
    s_onef = esF.enter_context(st("s_onef", [128, 1], F32))
    s_s1f = esF.enter_context(st("s_s1f", [128, 1024], F32))
    s_wshb = None

    t_onef = kb.op("G", lambda g: g.memset(s_onef[:, :], 1.0),
                   after=a_done, inc=chGc)
    t_ones_time = kb.op("G", lambda g: g.memset(s_timeb[:, :], 1.0),
                        after=a_done, inc=chGc)

    g_y = [None] * (HALF // 128)
    g_we = [None, None]
    g_xt = [None, None]
    g_zz = [None, None]
    g_o2 = [(None, None)] * 4
    g_sfgrp = [None] * NG
    g_time = None
    g_beb = [None, None]
    g_ps1 = [None, None]
    g_ps3 = [None, None]
    g_s1f = [None]
    g_bscr = [None, None]
    g_sq = [None, None]

    def emit_wload(pe, we_cols, d_wsrc, d_bsrc, first=False):
        wsl = s_we[:, pe * WE_COLS: pe * WE_COLS + we_cols]
        w_after = ((g_we[0], g_we[1]) if we_cols > WE_COLS else (g_we[pe],))
        tw = kb.op("Y", lambda y: y.dma_start(out=wsl, in_=d_wsrc),
                   after=w_after + (a_done if first else ()),
                   inc=chWt[pe], k=16)
        if d_bsrc is not None:
            bsl = s_beb[:, pe * 8:(pe + 1) * 8]
            tb = kb.op("Y", lambda y, bsl=bsl: y.dma_start(out=bsl, in_=d_bsrc),
                       after=(g_beb[pe],), inc=chBe[pe], k=16)
        else:
            bsl = s_beb[:, 0:16]
            tb = kb.op("Y", lambda y, bsl=bsl: y.dma_start(out=bsl, in_=d_bsh),
                       after=(g_beb[0], g_beb[1]), inc=chBe[0], k=16)
        kb.wcache[(pe, we_cols)] = (tw, tb)
    out_toks = []

    def ffn_expert(h, e, pe, first, ps1_of, ps3_of, p_o2, we_cols, w1o, w3o,
                   w2o, r0o, nit, nkt, d_wsrc, d_bsrc, stride1, wte_e,
                   groups=tuple(range(NG)), sfs=1024, skip_w=False,
                   s1_dbuf=True, prefetch=None):
        nonlocal g_time
        HB = h * HALF
        g0 = groups[0]
        wsl = s_we[:, pe * WE_COLS: pe * WE_COLS + we_cols]
        if not skip_w:
            emit_wload(pe, we_cols, d_wsrc, d_bsrc, first)
        bsl = s_beb[:, 0:16] if d_bsrc is None else s_beb[:, pe * 8:(pe + 1) * 8]
        tw, tb = kb.wcache[(pe, we_cols)]
        pending_sum = None
        last_smul = None
        t_silu = [None] * NG
        t_s3b = [None] * NG
        t_smul = [None] * NG
        t_sumcp = [None] * NG
        for g in groups:
            pg = g % 2
            pidx = (g % 2) if s1_dbuf else 0
            s1r = ps1_of(g)
            s3r = ps3_of(g)
            sfo = (g - g0) * sfs
            xsl = s_xtb[:, pg * 8 * GRP:(pg + 1) * 8 * GRP]
            txt = kb.op("Y", lambda y, g=g, xsl=xsl, HB=HB: y.dma_start(
                out=xsl,
                in_=d_xt[:, :, HB + g * GRP:HB + (g + 1) * GRP]
                .rearrange("k p n -> p k n")),
                after=(g_xt[pg],) + (a_done if first and g < 2 else ()),
                inc=chXL[pg], k=16)
            if g == groups[0] and prefetch is not None:
                prefetch()
            tmm1 = tmm3 = None
            for it in range(nit):
                for k in range(8):
                    af = ((tw, txt) + tuple(g_ps1[pidx] or ())
                          if (it == 0 and k == 0) else ())
                    tmm1 = kb.op("T", lambda t, it=it, k=k, xsl=xsl, s1r=s1r:
                                 t.matmul(
                        s1r[:, it * GRP:(it + 1) * GRP],
                        wsl[:, w1o + k * stride1 + it * 128:
                            w1o + k * stride1 + it * 128 + 128],
                        xsl[:, k * GRP:(k + 1) * GRP],
                        start=(k == 0), stop=(k == 7)), after=af,
                        inc=chT if (it == nit - 1 and k == 7) else None)
            for it in range(nit):
                for k in range(8):
                    af = (g_ps3[pidx],) if (it == 0 and k == 0) else ()
                    tmm3 = kb.op("T", lambda t, it=it, k=k, xsl=xsl, s3r=s3r:
                                 t.matmul(
                        s3r[:, it * GRP:(it + 1) * GRP],
                        wsl[:, w3o + k * stride1 + it * 128:
                            w3o + k * stride1 + it * 128 + 128],
                        xsl[:, k * GRP:(k + 1) * GRP],
                        start=(k == 0), stop=(k == 7)), after=af,
                        inc=chT if (it == nit - 1 and k == 7) else None)
            g_xt[pg] = tmm3
            for it0 in range(0, nit, 4):
                nb = min(4, nit - it0)
                tcp1 = kb.op("V", lambda v, it0=it0, nb=nb, s1r=s1r:
                             v.tensor_copy(
                    out=s_s1f[:, 0:nb * GRP],
                    in_=s1r[:, it0 * GRP:(it0 + nb) * GRP]),
                    after=(tmm1, g_s1f[0]), inc=chV)
                for it in range(it0, it0 + nb):
                    t_silu[g] = kb.op("S", lambda s, it=it, it0=it0:
                                      s.activation(
                        out=s_a[:, it * GRP:(it + 1) * GRP],
                        in_=s_s1f[:, (it - it0) * GRP:(it - it0 + 1) * GRP],
                        func=AF.Silu, bias=bsl[:, it:it + 1]),
                        after=(tcp1, tb) if it == it0 else (), inc=chS)
                g_s1f[0] = t_silu[g]
            g_ps1[pidx] = (tcp1,)
            for it in range(nit):
                t_s3b[g] = kb.op("V", lambda v, it=it, nit=nit, s3r=s3r:
                                 v.tensor_scalar(
                    out=s_s3b[:, it * GRP:(it + 1) * GRP],
                    in0=s3r[:, it * GRP:(it + 1) * GRP],
                    scalar1=bsl[:, nit + it:nit + it + 1], scalar2=None,
                    op0=ALU.add),
                    after=(tmm3, tb, last_smul) if it == 0 else (), inc=chV)
            sf_guard = tuple(g_sfgrp) if sfs != 1024 else (g_sfgrp[g],)
            t_smul[g] = kb.op("V", lambda v, sfo=sfo, nit=nit: v.tensor_tensor(
                out=s_sf[:, sfo:sfo + nit * GRP],
                in0=s_a[:, 0:nit * GRP], in1=s_s3b[:, 0:nit * GRP],
                op=ALU.mult),
                after=(t_silu[g], t_s3b[g]) + sf_guard, inc=chV)
            tsq = kb.op("S", lambda s, sfo=sfo, nit=nit, pg=pg: s.activation(
                out=s_sq[:, pg * 2048:pg * 2048 + nit * GRP],
                in_=s_sf[:, sfo:sfo + nit * GRP], func=AF.Square),
                after=(t_smul[g], g_sq[pg]), inc=chS)
            def emit_sum(g=g, s3r=s3r, tsq=tsq, ts3b=t_s3b[g], pidx=pidx):
                tsum = None
                for it in range(nit):
                    tsum = kb.op("T", lambda t, it=it, s3r=s3r, g=g: t.matmul(
                        s3r[0:1, 0:GRP], s_on1[:, :],
                        s_sq[:, (g % 2) * 2048 + it * GRP:
                             (g % 2) * 2048 + (it + 1) * GRP],
                        start=(it == 0), stop=(it == nit - 1)),
                        after=(tsq, ts3b) if it == 0 else (),
                        inc=chT if it == nit - 1 else None)
                g_sq[g % 2] = tsum
                t_sumcp[g] = kb.op("V", lambda v, g=g, s3r=s3r: v.tensor_copy(
                    out=s_sums[0:1, g * GRP:(g + 1) * GRP],
                    in_=s3r[0:1, 0:GRP]),
                    after=(tsum, g_time), inc=chV)
                g_ps3[pidx] = t_sumcp[g]

            last_smul = t_smul[g]
            if not s1_dbuf:
                emit_sum()
            else:
                if pending_sum is not None:
                    pending_sum()
                pending_sum = emit_sum
        if pending_sum is not None:
            pending_sum()
            pending_sum = None
        glast = groups[-1]
        tq = kb.op("S", lambda s: s.activation(
            out=s_timeb[0:1, g0 * GRP:(glast + 1) * GRP],
            in_=s_sums[0:1, g0 * GRP:(glast + 1) * GRP],
            func=AF.Sqrt, bias=c_one[0:1, :]),
            after=(t_sumcp[glast], t_ones_time, g_time, t_cst), inc=chS)
        t_last_k2 = None
        t_last_w2 = None
        for g in groups:
            sfo = (g - g0) * sfs
            for tt in range(2):
                lt = g * 2 + tt
                gi = h * (HALF // 128) + lt
                po = lt % 4
                cb = 32 * (lt % 2)
                pob = p_o2[po]
                tmm = None
                for kt in range(nkt):
                    for nh in range(2):
                        wnh = 512 if nh == 0 else 511
                        af = ()
                        if kt == 0 and nh == 0:
                            af = (t_smul[g], g_o2[po][0], g_o2[po][1],
                                  t_sumcp[glast])
                        tmm = kb.op("T", lambda t, sfo=sfo, tt=tt, kt=kt, nh=nh,
                                    wnh=wnh, pob=pob: t.matmul(
                            pob(nh * 512, nh * 512 + wnh),
                            s_sf[:, sfo + kt * GRP + tt * 128:
                                 sfo + kt * GRP + tt * 128 + 128],
                            wsl[:, w2o + kt * NSH + nh * 512:
                                w2o + kt * NSH + nh * 512 + wnh],
                            start=(kt == 0), stop=False), after=af)
                for nh in range(2):
                    wnh = 512 if nh == 0 else 511
                    tmm = kb.op("T", lambda t, g=g, tt=tt, nh=nh, wnh=wnh,
                                pob=pob: t.matmul(
                        pob(nh * 512, nh * 512 + wnh),
                        s_timeb[0:2, g * GRP + tt * 128:g * GRP + tt * 128 + 128],
                        wsl[0:2, r0o + nh * 512:r0o + nh * 512 + wnh],
                        start=False, stop=True),
                        after=(tq,) if nh == 0 else (),
                        inc=chT if nh == 1 else None)
                t_last_k2 = tmm
                g_sfgrp[g] = tmm
                t_last_w2 = tmm
                pz = lt % 2
                zsl = s_w32[:, pz * D:(pz + 1) * D]
                ysl = s_y[:, lt * D:(lt + 1) * D]
                bscr = s_b16[:, pz * D:pz * D + NSH]
                tob = kb.op("V", lambda v, pob=pob, bscr=bscr:
                            v.tensor_copy(out=bscr, in_=pob(0, NSH)),
                            after=(tmm, g_bscr[pz]), inc=chV)
                tq1 = kb.op("S", lambda s, bscr=bscr, cb=cb:
                            s.activation(out=s_sqs[:, 0:NSH], in_=bscr,
                                         func=AF.Square, accum_out=C(cb)),
                            after=(tob,), inc=chS)
                g_bscr[pz] = tq1
                tt0 = kb.op("S", lambda s, cb=cb: s.activation(
                    out=C(cb + 1), in_=C(cb), func=AF.Sqrt, bias=c_one),
                    after=(tq1, t_cst), inc=chS)
                if wte_e is not None:
                    wcol = s_wte[:, gi * 16 + wte_e:gi * 16 + wte_e + 1]
                    hcol = s_hit[:, gi * 16 + wte_e:gi * 16 + wte_e + 1]
                else:
                    wcol = hcol = None
                tzz = kb.op("V", lambda v, pob=pob, zsl=zsl, ysl=ysl,
                            wcol=wcol: v.scalar_tensor_tensor(
                    out=zsl[:, 1:D], in0=pob(0, NSH),
                    scalar=(wcol if wcol is not None else 1.0),
                    in1=ysl[:, 1:D], op0=ALU.mult, op1=ALU.add),
                    after=(tmm, g_zz[pz], g_y[lt]),
                    inc=chV)
                tz0 = kb.op("V", lambda v, zsl=zsl, ysl=ysl, cb=cb,
                            wcol=wcol: v.scalar_tensor_tensor(
                    out=zsl[:, 0:1], in0=C(cb + 1),
                    scalar=(wcol if wcol is not None else 1.0),
                    in1=ysl[:, 0:1], op0=ALU.mult, op1=ALU.add),
                    after=(tt0,), inc=chV)
                tS1 = kb.op("S", lambda s, zsl=zsl, cb=cb: s.activation(
                    out=s_sqs[:, 0:NSH], in_=zsl[:, 1:D], func=AF.Square,
                    accum_out=C(cb + 2)), after=(tzz,), inc=chS)
                tdd = kb.op("V", lambda v, zsl=zsl, cb=cb:
                            v.scalar_tensor_tensor(
                    out=C(cb + 4), in0=zsl[:, 0:1], scalar=zsl[:, 0:1],
                    in1=C(cb + 2), op0=ALU.mult, op1=ALU.subtract),
                    after=(tz0, tS1), inc=chV)
                tdm = kb.op("V", lambda v, cb=cb: v.tensor_scalar(
                    out=C(cb + 5), in0=C(cb + 4), scalar1=EPS, scalar2=None,
                    op0=ALU.max), after=(tdd,), inc=chV)
                trc = kb.op("V", lambda v, cb=cb: v.reciprocal(
                    out=C(cb + 6), in_=C(cb + 5)), after=(tdm,), inc=chV)
                tsq2 = kb.op("S", lambda s, cb=cb: s.activation(
                    out=C(cb + 7), in_=C(cb + 6), func=AF.Sqrt),
                    after=(trc,), inc=chS)
                if hcol is not None:
                    tr1 = kb.op("V", lambda v, cb=cb: v.tensor_scalar(
                        out=C(cb + 8), in0=C(cb + 7), scalar1=1.0,
                        scalar2=None, op0=ALU.subtract), after=(tsq2,),
                        inc=chV)
                    tr2 = kb.op("V", lambda v, cb=cb, hcol=hcol:
                                v.scalar_tensor_tensor(
                        out=C(cb + 9), in0=hcol, scalar=C(cb + 8),
                        in1=s_onef[:, :], op0=ALU.mult, op1=ALU.add),
                        after=(tr1, t_onef), inc=chV)
                    rcol = C(cb + 9)
                    rtok = tr2
                else:
                    rcol = C(cb + 7)
                    rtok = tsq2
                ty = kb.op("S", lambda s, zsl=zsl, ysl=ysl, rcol=rcol:
                           s.activation(out=ysl, in_=zsl, func=AF.Identity,
                                        scale=rcol),
                           after=(rtok, tzz, tz0), inc=chS)
                g_y[lt] = ty
                g_zz[pz] = ty
                g_o2[po] = (tzz, tob)
                if s1_dbuf:
                    if po < 2:
                        g_ps1[po] = (tzz, tob)
                    else:
                        g_ps3[po - 2] = tzz
                else:
                    g_ps1[0] = (tzz, tob)
        g_we[pe] = t_last_w2
        if we_cols > WE_COLS:
            g_we[0] = g_we[1] = t_last_w2
        g_beb[pe] = t_s3b[NG - 1]
        g_time = t_last_k2
        return ty

    kb.wcache = {}
    NLT = HALF // 128
    ty_last = None
    for h in range(2):
        yg = tuple(t for t in g_y if t is not None)
        ty0 = kb.op("G", lambda g: g.memset(s_y[:, :], 0.0),
                    after=yg + ((a_done) if h == 0 else ()), inc=chGc)
        ty1 = kb.op("G", lambda g: g.memset(
            bass.AP(s_y, 0, [[NLT * D, 128], [D, NLT], [1, 1]]), 1.0),
            after=(ty0,), inc=chGc)
        for lt in range(NLT):
            g_y[lt] = ty1
        esB = contextlib.ExitStack()
        psA = esB.enter_context(pt("psA_%d" % h, [128, 2048], F32))
        psB = esB.enter_context(pt("psB_%d" % h, [128, 2048], F32))
        psab = [psA, psB]
        ps1_of = lambda g: psab[g % 2][:, 0:1024]
        ps3_of = lambda g: psab[g % 2][:, 1024:2048]
        po2 = [lambda lo, hi: psA[:, lo:hi],
               lambda lo, hi: psB[:, lo:hi],
               lambda lo, hi: psA[:, 1024 + lo:1024 + hi],
               lambda lo, hi: psB[:, 1024 + lo:1024 + hi]]
        for e in range(NE):
            if e == 0:
                emit_wload(0, WE_COLS, d_wec[0, :, :], d_bexp[0, :, :],
                           first=(h == 0))
            pf = None
            if e + 1 < NE:
                pf = (lambda e1=e + 1: emit_wload(
                    e1 % 2, WE_COLS, d_wec[e1, :, :], d_bexp[e1, :, :]))
            ty_last = ffn_expert(h, e, e % 2, (h == 0 and e == 0),
                                 ps1_of, ps3_of, po2, WE_COLS,
                                 W1_OFF, W3_OFF, W2_OFF, R0_OFF, 4, 4,
                                 d_wec[e, :, :], d_bexp[e, :, :], 512, e,
                                 skip_w=True, prefetch=pf)
        esB.close()
        g_ps1[0] = g_ps1[1] = (ty_last,)
        g_ps3[0] = g_ps3[1] = ty_last
        for _i in range(4):
            g_o2[_i] = (ty_last, ty_last)
        esC = contextlib.ExitStack()
        p_s1C = esC.enter_context(pt("p_s1C_%d" % h, [128, 2048], F32))
        p_s3C = esC.enter_context(pt("p_s3C_%d" % h, [128, 2048], F32))
        po2C = [lambda lo, hi: p_s1C[:, lo:hi],
                lambda lo, hi: p_s1C[:, 1024 + lo:1024 + hi],
                lambda lo, hi: p_s3C[:, lo:hi],
                lambda lo, hi: p_s3C[:, 1024 + lo:1024 + hi]]
        ps1C_of = lambda g: p_s1C[:, :]
        ps3C_of = lambda g: p_s3C[:, :]
        for sb in range(2):
            groups = tuple(range(sb * 4, sb * 4 + 4))
            ty_last = ffn_expert(h, 100 + sb, 0, False,
                                 ps1C_of, ps3C_of, po2C, WS_COLS,
                                 SW1_OFF, SW3_OFF, SW2_OFF, SR0_OFF, 8, 8,
                                 d_wshc, None, 1024, None,
                                 groups=groups, sfs=2048, skip_w=(sb == 1),
                                 s1_dbuf=False)
            g_ps1[0] = g_ps1[1] = (ty_last,)
            g_ps3[0] = g_ps3[1] = ty_last
            for _i in range(4):
                g_o2[_i] = (ty_last, ty_last)
        esC.close()
        for lt in range(NLT):
            gi = h * NLT + lt
            px = lt % 2
            cb = 32 * px + 16
            ysl = s_y[:, lt * D:(lt + 1) * D]
            xsl = s_x[:, px * D:(px + 1) * D]
            wsl2 = s_w32[:, px * D:(px + 1) * D]
            tx2l = kb.op("Y", lambda y, gi=gi, xsl=xsl: y.dma_start(
                out=xsl, in_=d_x2[gi * 128:(gi + 1) * 128, :]),
                after=(g_x[px], tok_x2sp[NT - 2], tok_x2sp[NT - 1]),
                inc=chX[px], k=16)
            rstd, mb, tln = ln_stats(ysl, cb, "sqrt", (g_y[lt],))
            txf = kb.op("S", lambda s, xsl=xsl, wsl2=wsl2, mb=mb: s.activation(
                out=wsl2, in_=xsl, func=AF.Identity, bias=mb),
                after=(tx2l, tln[1], g_zz[px]), inc=chS)
            tst = kb.op("V", lambda v, ysl=ysl, xsl=xsl, wsl2=wsl2, rstd=rstd:
                        v.scalar_tensor_tensor(
                out=xsl, in0=ysl, scalar=rstd, in1=wsl2,
                op0=ALU.mult, op1=ALU.add), after=(txf, tln[0]), inc=chV)
            tout = kb.op("Y", lambda y, gi=gi, xsl=xsl: y.dma_start(
                out=d_out[gi * 128:(gi + 1) * 128, :], in_=xsl),
                after=(tst,), inc=chOut[px], k=16)
            out_toks.append(tout)
            g_x[px] = tout
            g_zz[px] = tst
            g_y[lt] = tst
    esF.close()

    if debug:
        tdbg1 = kb.op("Y", lambda y: y.dma_start(
            out=d_dbg[:, 0:NT * 16], in_=s_wte[:, :]),
            after=(out_toks[-1],), inc=chC, k=16)
        tdbg2 = kb.op("Y", lambda y: y.dma_start(
            out=d_dbg[:, 512:512 + NT * 16], in_=s_hit[:, :]),
            after=(tdbg1,), inc=chC, k=16)
        out_toks.append(tdbg2)

    kb.op("G", lambda g: g.nop(), after=tuple(out_toks[-3:]))

    with nc.Block() as block:
        kb.emit(block)
    es.close()
    return nc


_NC_CACHE = {}


def _get_nc(debug=False):
    if debug not in _NC_CACHE:
        _NC_CACHE[debug] = build_nc(debug)
    return _NC_CACHE[debug]


def kernel(**inputs):
    g, x = pack_host(inputs)
    nc = _get_nc(False)
    in_maps = []
    for b in range(B):
        m = dict(g)
        m["x"] = np.ascontiguousarray(x[b])
        in_maps.append(m)
    res = run_bass_kernel_spmd(nc, in_maps, core_ids=list(range(B)))
    return np.stack([r["out"] for r in res.results]).astype(np.float32)


# version 13 (final)
# speedup vs baseline: 1.1496x; 1.0142x over previous
import contextlib

import numpy as np

import concourse.bass as bass
import concourse.mybir as mybir
from concourse.bass_utils import run_bass_kernel_spmd

F32 = mybir.dt.float32
BF16 = mybir.dt.bfloat16
AF = mybir.ActivationFunctionType
ALU = mybir.AluOpType
AX = mybir.AxisListType

B, T, D, H = 8, 4096, 1024, 64
NE, NI = 16, 511
NSH = 1023
NT = T // 128
HALF = T // 2
GRP = 256
NG = HALF // GRP
EPS_LN = 1e-5
EPS = 1e-6

W1_OFF = 0
W3_OFF = 8 * 512
W2_OFF = 2 * 8 * 512
R0_OFF = W2_OFF + 4 * NSH
WE_COLS = R0_OFF + NSH

SW1_OFF = 0
SW3_OFF = 8 * 1024
SW2_OFF = 2 * 8 * 1024
SR0_OFF = SW2_OFF + 8 * NSH
WS_COLS = SR0_OFF + NSH


class _Chan:
    __slots__ = ("sem", "n", "name", "owner")

    def __init__(self, sem, name, owner):
        self.sem, self.n, self.name, self.owner = sem, 0, name, owner


class KB:
    ENGS = ("Y", "G", "T", "V", "S")

    def __init__(self, nc, es):
        self.nc = nc
        self.es = es
        self.items = []
        self.chans = []

    def chan(self, name, owner=None):
        ch = _Chan(self.es.enter_context(self.nc.semaphore(name)), name, owner)
        self.chans.append(ch)
        return ch

    def op(self, eng, emit, after=(), inc=None, k=1):
        idx = len(self.items)
        tok = None
        if inc is not None:
            tok = (inc, idx)
        after = tuple(t for t in after if t is not None)
        self.items.append([eng, after, emit, inc, k, idx])
        return tok

    def emit(self, block):
        used = set()
        for eng, after, _, _, _, _ in self.items:
            for (ch, idx) in after:
                if ch.owner == eng == "T":
                    continue
                used.add((ch.name, idx))
        val_at = {}
        counts = {}
        for it in self.items:
            eng, after, emitfn, inc, k, idx = it
            if inc is not None and (k == 16 or (inc.name, idx) in used):
                counts[inc.name] = counts.get(inc.name, 0) + k
                val_at[(inc.name, idx)] = counts[inc.name]
            elif inc is not None:
                it[3] = None
        decs = {"T": block.tensor, "V": block.vector, "S": block.scalar,
                "G": block.gpsimd, "Y": block.sync}
        for engname in self.ENGS:
            items = [it for it in self.items if it[0] == engname]
            if not items:
                continue

            def section(eng, items=items, engname=engname):
                waited = {}
                for _, after, emitfn, inc, k, idx in items:
                    for (ch, tidx) in after:
                        if ch.owner == engname == "T":
                            continue
                        val = val_at[(ch.name, tidx)]
                        if waited.get(ch.name, 0) < val:
                            eng.wait_ge(ch.sem, val)
                            waited[ch.name] = val
                    instr = emitfn(eng)
                    if inc is not None:
                        if instr is None:
                            instr = eng.nop()
                        instr.then_inc(inc.sem, k)

            decs[engname](section)


def _pack_T(w, rpad=None):
    R, K = w.shape
    assert K % 128 == 0
    rpad = R if rpad is None else rpad
    wt = np.zeros((K, rpad), np.float32)
    wt[:, :R] = w.T
    wt = wt.reshape(K // 128, 128, rpad)
    return np.ascontiguousarray(wt.transpose(1, 0, 2).reshape(128, (K // 128) * rpad))


def _pack_rows(wt, nrow_pad):
    R, C = wt.shape
    out = np.zeros((nrow_pad * 128, C), np.float32)
    out[:R] = wt
    out = out.reshape(nrow_pad, 128, C).transpose(1, 0, 2)
    return np.ascontiguousarray(out.reshape(128, nrow_pad * C))


def _bias_fm(b, ntile):
    out = np.zeros((ntile * 128,), np.float32)
    out[:b.shape[0]] = b
    return np.ascontiguousarray(out.reshape(ntile, 128).T)


def pack_host(inputs):
    f = np.float32
    g = {}
    g["wq"] = _pack_T(np.ascontiguousarray(inputs["L0_attn_W_q"].T))
    g["wk"] = _pack_T(np.ascontiguousarray(inputs["L0_attn_W_k"].T))
    g["wv"] = _pack_T(np.ascontiguousarray(inputs["L0_attn_W_v"].T))
    g["wo"] = np.ascontiguousarray(inputs["L0_attn_W_o"], dtype=f)
    g["gw"] = _pack_T(np.ascontiguousarray(inputs["L0_ffn_gate_w"]))
    blobs = []
    bias_blobs = []
    for e in range(NE):
        we1 = np.asarray(inputs["L0_ffn_We1"][e], f)
        we3 = np.asarray(inputs["L0_ffn_We3"][e], f)
        we2 = np.asarray(inputs["L0_ffn_We2"][e], f)
        be1 = np.asarray(inputs["L0_ffn_be1"][e], f)
        be3 = np.asarray(inputs["L0_ffn_be3"][e], f)
        be2 = np.asarray(inputs["L0_ffn_be2"][e], f)
        p1 = _pack_T(we1, 512)
        p3 = _pack_T(we3, 512)
        w2t = np.ascontiguousarray(we2.T)
        p2 = _pack_rows(w2t[1:], 4)
        r0 = np.zeros((128, NSH), f)
        r0[0] = w2t[0]
        r0[1] = be2
        blobs.append(np.concatenate([p1, p3, p2, r0], axis=1))
        bias_blobs.append(np.concatenate([_bias_fm(be1, 4), _bias_fm(be3, 4)],
                                         axis=1))
    g["wexp"] = np.stack(blobs)
    g["bexp"] = np.stack(bias_blobs)
    assert g["wexp"].shape == (NE, 128, WE_COLS), g["wexp"].shape
    ws1 = np.asarray(inputs["L0_ffn_Ws1"], f)
    ws3 = np.asarray(inputs["L0_ffn_Ws3"], f)
    ws2 = np.asarray(inputs["L0_ffn_Ws2"], f)
    sp1 = _pack_T(ws1, 1024)
    sp3 = _pack_T(ws3, 1024)
    ws2t = np.ascontiguousarray(ws2.T)
    sp2 = _pack_rows(ws2t[1:], 8)
    sr0 = np.zeros((128, NSH), f)
    sr0[0] = ws2t[0]
    sr0[1] = np.asarray(inputs["L0_ffn_bs2"], f)
    g["wsh"] = np.concatenate([sp1, sp3, sp2, sr0], axis=1)
    g["bsh"] = np.concatenate(
        [_bias_fm(np.asarray(inputs["L0_ffn_bs1"], f), 8),
         _bias_fm(np.asarray(inputs["L0_ffn_bs3"], f), 8)], axis=1)
    assert g["wsh"].shape == (128, WS_COLS), g["wsh"].shape
    import ml_dtypes
    ident = np.eye(128, dtype=f)
    g["identf"] = ident
    g["identb"] = ident.astype(ml_dtypes.bfloat16)
    ut = np.triu(np.ones((128, 128), f))
    g["utf"] = ut
    g["utb"] = ut.astype(ml_dtypes.bfloat16)
    onesb = np.ones((1, 128), f)
    g["onesb"] = onesb.astype(ml_dtypes.bfloat16)
    x = np.asarray(inputs["L0_attn_x"], f)
    return g, x


def build_nc(debug=False, stop_after=None, a3_gate=True, a3_xtspill=True,
             a3_level=9, detect_races=True):
    nc = bass.Bass("TRN2", target_bir_lowering=False,
                   detect_race_conditions=detect_races)
    es = contextlib.ExitStack()
    kb = KB(nc, es)

    dbg_kind = "ExternalOutput" if debug else "Internal"
    d_x = nc.dram_tensor("x", [T, D], F32, kind="ExternalInput").ap()
    d_wq = nc.dram_tensor("wq", [128, 8 * H], F32, kind="ExternalInput").ap()
    d_wk = nc.dram_tensor("wk", [128, 8 * H], F32, kind="ExternalInput").ap()
    d_wv = nc.dram_tensor("wv", [128, 8 * H], F32, kind="ExternalInput").ap()
    d_wo = nc.dram_tensor("wo", [H, D], F32, kind="ExternalInput").ap()
    d_gw = nc.dram_tensor("gw", [128, 8 * 16], F32, kind="ExternalInput").ap()
    d_wexp = nc.dram_tensor("wexp", [NE, 128, WE_COLS], F32, kind="ExternalInput").ap()
    d_wsh = nc.dram_tensor("wsh", [128, WS_COLS], F32, kind="ExternalInput").ap()
    d_bexp = nc.dram_tensor("bexp", [NE, 128, 8], F32, kind="ExternalInput").ap()
    d_bsh = nc.dram_tensor("bsh", [128, 16], F32, kind="ExternalInput").ap()
    d_idf = nc.dram_tensor("identf", [128, 128], F32, kind="ExternalInput").ap()
    d_idb = nc.dram_tensor("identb", [128, 128], BF16, kind="ExternalInput").ap()
    d_utf = nc.dram_tensor("utf", [128, 128], F32, kind="ExternalInput").ap()
    d_utb = nc.dram_tensor("utb", [128, 128], BF16, kind="ExternalInput").ap()
    d_onesb = nc.dram_tensor("onesb", [1, 128], BF16, kind="ExternalInput").ap()
    d_out = nc.dram_tensor("out", [T, D], F32, kind="ExternalOutput").ap()
    d_wec = nc.dram_tensor("wec", [NE, 128, WE_COLS], BF16, kind="Internal").ap()
    d_wshc = nc.dram_tensor("wshc", [128, WS_COLS], BF16, kind="Internal").ap()
    d_x2 = nc.dram_tensor("x2sp", [T, D], F32, kind=dbg_kind).ap()
    d_xt = nc.dram_tensor("xtsp", [8, 128, T], BF16, kind=dbg_kind).ap()
    d_dbg = None
    if debug:
        d_dbg = nc.dram_tensor("dbg", [128, 4096], F32, kind="ExternalOutput").ap()

    chL = [kb.chan("chL0"), kb.chan("chL1")]
    chLg = [kb.chan("chLg0"), kb.chan("chLg1")]
    chCv = [kb.chan("chCv0"), kb.chan("chCv1")]
    chC = kb.chan("chC")
    chX = [kb.chan("chX0"), kb.chan("chX1")]
    chSp = [kb.chan("chSp0"), kb.chan("chSp1")]
    chXt = [kb.chan("chXt0"), kb.chan("chXt1")]
    chXL = [kb.chan("chXL0"), kb.chan("chXL1")]
    chWt = [kb.chan("chWt0"), kb.chan("chWt1")]
    chBe = [kb.chan("chBe0"), kb.chan("chBe1")]
    chOut = [kb.chan("chOut0"), kb.chan("chOut1")]
    chT = kb.chan("chT", "T")
    chV = kb.chan("chV", "V")
    chS = kb.chan("chS", "S")
    chGc = kb.chan("chGc", "G")

    st = nc.sbuf_tensor

    s_x = es.enter_context(st("s_x", [128, 2 * D], F32))
    s_w32 = es.enter_context(st("s_w32", [128, 2 * D], F32))
    s_b16 = es.enter_context(st("s_b16", [128, 2 * D], BF16))
    s_sqs = es.enter_context(st("s_sqs", [128, D], BF16))
    s_wqb = es.enter_context(st("s_wqb", [128, 8 * H], BF16))
    s_wkb = es.enter_context(st("s_wkb", [128, 8 * H], BF16))
    s_wvb = es.enter_context(st("s_wvb", [128, 8 * H], BF16))
    s_wob = es.enter_context(st("s_wob", [H, D], BF16))
    s_gw = es.enter_context(st("s_gw", [128, 8 * 16], F32))
    s_idf = es.enter_context(st("s_idf", [128, 128], F32))
    s_idb = es.enter_context(st("s_idb", [128, 128], BF16))
    s_utf = es.enter_context(st("s_utf", [128, 128], F32))
    s_utb = es.enter_context(st("s_utb", [128, 128], BF16))
    s_onesb = es.enter_context(st("s_onesb", [1, 128], BF16))
    s_wte = es.enter_context(st("s_wte", [128, NT * 16], F32))
    s_hit = es.enter_context(st("s_hit", [128, NT * 16], F32))
    s_y = es.enter_context(st("s_y", [128, (HALF // 128) * D], F32))
    s_c = es.enter_context(st("s_c", [128, 64], F32))
    s_on1 = es.enter_context(st("s_on1", [128, 1], BF16))
    s_cst = es.enter_context(st("s_cst", [128, 3], F32))

    def C(i):
        return s_c[:, i:i + 1]

    c_epsln = s_cst[:, 0:1]
    c_one = s_cst[:, 1:2]
    c_mhalf = s_cst[:, 2:3]
    t_cst = kb.op("G", lambda g: g.memset(s_cst[:, 0:1], EPS_LN), inc=None)
    t_cst = kb.op("G", lambda g: g.memset(s_cst[:, 1:2], 1.0), inc=None)
    t_cst = kb.op("G", lambda g: g.memset(s_cst[:, 2:3], -0.5), inc=chGc)

    def emit_weight_conversions():
        tok = None
        for e in range(NE):
            tok = kb.op("G", lambda gp, e=e: gp.dma_start(
                out=d_wec[e, :, :], in_=d_wexp[e, :, :]), inc=chCv[0], k=16)
        tok = kb.op("G", lambda gp: gp.dma_start(out=d_wshc, in_=d_wsh),
                    inc=chCv[0], k=16)
        return (tok,)

    tok_wcache = emit_weight_conversions()

    def ld(dst, src):
        return kb.op("Y", lambda y: y.dma_start(out=dst, in_=src), inc=chC, k=16)

    for dst, srcc in ((s_idf[:, :], d_idf), (s_idb[:, :], d_idb),
                      (s_utf[:, :], d_utf), (s_utb[:, :], d_utb),
                      (s_onesb[:, :], d_onesb)):
        ld(dst, srcc)
    t_consts = [ld(s_gw[:, :], d_gw)]
    t_aw = []
    for dsrc, dst in ((d_wq, s_wqb), (d_wk, s_wkb), (d_wv, s_wvb)):
        t_aw.append(kb.op("G", lambda gp, s=dsrc, d=dst: gp.dma_start(
            out=d[:, :], in_=s), inc=chCv[1], k=16))
    t_aw.append(kb.op("G", lambda gp: gp.dma_start(
        out=s_wob[:, :], in_=d_wo), inc=chCv[1], k=16))
    t_consts += t_aw[-1:]

    if stop_after == "W":
        kb.op("G", lambda g: g.nop(), after=(t_consts[0],))
        with nc.Block() as block:
            kb.emit(block)
        es.close()
        return nc

    esA = contextlib.ExitStack()
    pt = nc.psum_tensor
    s_qf = esA.enter_context(st("s_qf", [128, T], BF16))
    s_vf = esA.enter_context(st("s_vf", [128, T], BF16))
    s_qt = esA.enter_context(st("s_qt", [128, NT * H], F32))
    s_kt = esA.enter_context(st("s_kt", [128, NT * H], BF16))
    s_vt = esA.enter_context(st("s_vt", [128, NT * H], BF16))
    s_ot = esA.enter_context(st("s_ot", [128, NT * H], BF16))
    s_of = esA.enter_context(st("s_of", [128, T], BF16))
    s_lnf = esA.enter_context(st("s_lnf", [128, 2 * D], BF16))
    s_ph = esA.enter_context(st("s_ph", [128, 3 * 128], F32))
    s_S = esA.enter_context(st("s_S", [64, 2 * H], F32))
    s_Sb = esA.enter_context(st("s_Sb", [64, H], BF16))
    s_z = esA.enter_context(st("s_z", [1, 2 * H], F32))
    s_zb = esA.enter_context(st("s_zb", [1, H], BF16))
    s_fm32 = esA.enter_context(st("s_fm32", [128, D], F32))
    s_g16 = esA.enter_context(st("s_g16", [128, 2 * 16], F32))
    s_g16b = esA.enter_context(st("s_g16b", [128, 2 * 16], F32))
    s_gm = esA.enter_context(st("s_gm", [128, 2 * 128], BF16))

    t_on1 = kb.op("G", lambda g: g.memset(s_on1[:, :], 1.0), inc=chGc)
    t_S0 = kb.op("G", lambda g: g.memset(s_S[:, :], 0.0), inc=chGc)
    t_S0b = kb.op("G", lambda g: g.memset(s_Sb[:, :], 0.0), inc=chGc)
    t_z0 = kb.op("G", lambda g: g.memset(s_z[:, :], 0.0), inc=chGc)
    t_z0b = kb.op("G", lambda g: g.memset(s_zb[:, :], 0.0), inc=chGc)

    def ln_stats(src_ap, cb, variant, after_src):
        t1 = kb.op("V", lambda v: v.tensor_reduce(
            out=C(cb), in_=src_ap, axis=AX.X, op=ALU.add),
            after=after_src, inc=chV)
        t2 = kb.op("S", lambda s: s.activation(
            out=s_sqs[:, :], in_=src_ap, func=AF.Square, accum_out=C(cb + 1)),
            after=after_src, inc=chS)
        t3 = kb.op("V", lambda v: v.tensor_scalar(
            out=C(cb + 2), in0=C(cb), scalar1=-1.0 / D, scalar2=None,
            op0=ALU.mult), after=(t1,), inc=chV)
        t4 = kb.op("V", lambda v: v.tensor_tensor(
            out=C(cb + 3), in0=C(cb + 2), in1=C(cb + 2), op=ALU.mult),
            after=(t3,), inc=chV)
        t5 = kb.op("V", lambda v: v.scalar_tensor_tensor(
            out=C(cb + 4), in0=C(cb + 1), scalar=1.0 / D, in1=C(cb + 3),
            op0=ALU.mult, op1=ALU.subtract), after=(t2, t4), inc=chV)
        if variant == "expln":
            t6 = kb.op("S", lambda s: s.activation(
                out=C(cb + 5), in_=C(cb + 4), func=AF.Ln, bias=c_epsln),
                after=(t5, t_cst), inc=chS)
            t7 = kb.op("S", lambda s: s.activation(
                out=C(cb + 6), in_=C(cb + 5), func=AF.Exp, scale=c_mhalf),
                after=(t6,), inc=chS)
            t8 = kb.op("S", lambda s: s.activation(
                out=C(cb + 7), in_=C(cb + 2), func=AF.Identity,
                scale=C(cb + 6)), after=(t7,), inc=chS)
        else:
            t6 = kb.op("S", lambda s: s.activation(
                out=C(cb + 5), in_=C(cb + 4), func=AF.Sqrt, bias=c_epsln),
                after=(t5, t_cst), inc=chS)
            t7 = kb.op("V", lambda v: v.reciprocal(
                out=C(cb + 6), in_=C(cb + 5)), after=(t6,), inc=chV)
            t8 = kb.op("V", lambda v: v.tensor_scalar(
                out=C(cb + 7), in0=C(cb + 2), scalar1=C(cb + 6), scalar2=None,
                op0=ALU.mult), after=(t7,), inc=chV)
        return C(cb + 6), C(cb + 7), (t7, t8)

    def phi(src_ap, dst_ap, w, after, np_=128):
        ta = kb.op("V", lambda v: v.tensor_scalar(
            out=s_ph[0:np_, 0:w], in0=src_ap, scalar1=0.0, scalar2=None,
            op0=ALU.min), after=after, inc=chV)
        tb = kb.op("S", lambda s: s.activation(
            out=s_ph[0:np_, 128:128 + w], in_=s_ph[0:np_, 0:w], func=AF.Exp),
            after=(ta,), inc=chS)
        tc = kb.op("V", lambda v: v.tensor_scalar(
            out=s_ph[0:np_, 256:256 + w], in0=src_ap, scalar1=0.0,
            scalar2=None, op0=ALU.max), after=after, inc=chV)
        td = kb.op("V", lambda v: v.tensor_tensor(
            out=dst_ap, in0=s_ph[0:np_, 128:128 + w],
            in1=s_ph[0:np_, 256:256 + w],
            op=ALU.add), after=(tb, tc), inc=chV)
        return td

    esA1 = contextlib.ExitStack()
    p_trA = esA1.enter_context(pt("p_trA", [128, D], BF16))
    p_qkv = esA1.enter_context(pt("p_qkv", [128, 512], F32))
    p_G = esA1.enter_context(pt("p_G", [128, 128], F32))
    p_oc = esA1.enter_context(pt("p_oc", [128, H], F32))
    p_zc = esA1.enter_context(pt("p_zc", [128, H], F32))
    p_z = esA1.enter_context(pt("p_z", [1, H], F32))
    p_S = esA1.enter_context(pt("p_S", [64, H], F32))
    p_of = esA1.enter_context(pt("p_of", [64, 128], BF16))
    g_x = [None, None]
    g_trA = None
    g_qkv = None
    tok_qf = [None] * NT
    tok_vf = [None] * NT
    tok_qt = [None] * NT
    tok_kt = [None] * NT
    tok_vt = [None] * NT
    g_lnf = [None, None]
    a1_tr_last = None

    def a1_tile(i):
        nonlocal g_trA, g_qkv, a1_tr_last
        p = i % 2
        cb = 32 * p
        xsl = s_x[:, p * D:(p + 1) * D]
        tx = kb.op("Y", lambda y, i=i, xsl=xsl: y.dma_start(
            out=xsl, in_=d_x[i * 128:(i + 1) * 128, :]),
            after=(g_x[p],) + (tuple(t_consts) if i < 2 else ()),
            inc=chX[p], k=16)
        rstd, mb, tln = ln_stats(xsl, cb, "expln", (tx,))
        lnsl = s_b16[:, p * D:(p + 1) * D]
        tln1 = kb.op("S", lambda s, xsl=xsl, lnsl=lnsl, rstd=rstd, mb=mb:
                     s.activation(out=lnsl, in_=xsl, func=AF.Identity,
                                  scale=rstd, bias=mb),
                     after=tln + (g_lnf[p],), inc=chS)
        g_x[p] = tln1
        ttr = None
        for k in range(8):
            ttr = kb.op("T", lambda t, k=k, lnsl=lnsl: t.transpose(
                p_trA[:, k * 128:(k + 1) * 128], lnsl[:, k * 128:(k + 1) * 128],
                s_idb[:, :]),
                after=(tln1, g_trA) if k == 0 else (),
                inc=chT if k == 7 else None)
        a1_tr_last = ttr
        fmsl = s_lnf[:, p * D:(p + 1) * D]
        tfm = kb.op("V", lambda v, fmsl=fmsl: v.tensor_copy(
            out=fmsl, in_=p_trA[:, :]), after=(ttr,), inc=chV)
        g_trA = tfm
        tmm = None
        forms = [
            lambda t, k, fmsl: t.matmul(
                p_qkv[0:64, 0:128], s_wqb[:, k * H:(k + 1) * H],
                fmsl[:, k * 128:(k + 1) * 128], start=(k == 0), stop=(k == 7)),
            lambda t, k, fmsl: t.matmul(
                p_qkv[0:64, 128:256], s_wvb[:, k * H:(k + 1) * H],
                fmsl[:, k * 128:(k + 1) * 128], start=(k == 0), stop=(k == 7)),
            lambda t, k, fmsl: t.matmul(
                p_qkv[:, 256:320], fmsl[:, k * 128:(k + 1) * 128],
                s_wqb[:, k * H:(k + 1) * H], start=(k == 0), stop=(k == 7)),
            lambda t, k, fmsl: t.matmul(
                p_qkv[:, 320:384], fmsl[:, k * 128:(k + 1) * 128],
                s_wkb[:, k * H:(k + 1) * H], start=(k == 0), stop=(k == 7)),
            lambda t, k, fmsl: t.matmul(
                p_qkv[:, 384:448], fmsl[:, k * 128:(k + 1) * 128],
                s_wvb[:, k * H:(k + 1) * H], start=(k == 0), stop=(k == 7)),
        ]
        for fi, fn in enumerate(forms):
            for k in range(8):
                af = (tfm, g_qkv) if (fi == 0 and k == 0) else ()
                tmm = kb.op("T",
                            lambda t, fn=fn, k=k, fmsl=fmsl: fn(t, k, fmsl),
                            after=af,
                            inc=chT if (fi == 4 and k == 7) else None)
        g_lnf[p] = tmm
        tok_qf[i] = phi(p_qkv[0:64, 0:128], s_qf[0:64, i * 128:(i + 1) * 128],
                        128, (tmm,), np_=64)
        tok_qt[i] = phi(p_qkv[:, 256:320], s_qt[:, i * H:(i + 1) * H],
                        64, (tok_qf[i],))
        tok_kt[i] = phi(p_qkv[:, 320:384], s_kt[:, i * H:(i + 1) * H],
                        64, (tok_qt[i],))
        tok_vf[i] = kb.op("V", lambda v, i=i: v.tensor_copy(
            out=s_vf[0:64, i * 128:(i + 1) * 128], in_=p_qkv[0:64, 128:256]),
            after=(tmm,), inc=chV)
        tok_vt[i] = kb.op("V", lambda v, i=i: v.tensor_copy(
            out=s_vt[:, i * H:(i + 1) * H], in_=p_qkv[:, 384:448]),
            after=(tmm,), inc=chV)
        g_qkv = tok_vt[i]

    g_G = None
    g_gm = [None, None]
    g_oc = None
    g_zc = None
    g_of = None
    t_Scp = t_S0b
    t_zcp = t_z0b
    t_dSmm = None
    t_zsum = None
    tok_o = [None] * NT

    def a2_chunk(c):
        nonlocal g_G, g_oc, g_zc, g_of, t_Scp, t_zcp, t_dSmm, t_zsum
        p = c % 2
        cbase = 40 + 4 * p
        tG = kb.op("T", lambda t, c=c: t.matmul(
            p_G[:, :], s_vf[0:64, c * 128:(c + 1) * 128],
            s_qf[0:64, c * 128:(c + 1) * 128], start=True, stop=True),
            after=(tok_vf[c], tok_qf[c], g_G), inc=chT)
        gm = s_gm[:, p * 128:(p + 1) * 128]
        tGm = kb.op("V", lambda v, gm=gm: v.tensor_tensor(
            out=gm, in0=p_G[:, :], in1=s_utf[:, :], op=ALU.mult),
            after=(tG, g_gm[p]), inc=chV)
        g_G = tGm
        to1 = kb.op("T", lambda t, c=c, gm=gm: t.matmul(
            p_oc[:, :], gm, s_kt[:, c * H:(c + 1) * H], start=True, stop=False),
            after=(tGm, tok_kt[c], g_oc), inc=None)
        to2 = kb.op("T", lambda t, c=c: t.matmul(
            p_oc[:, :], s_qf[0:64, c * 128:(c + 1) * 128], s_Sb[:, :],
            start=False, stop=True), after=(t_Scp,), inc=chT)
        g_gm[p] = to2
        tz1 = kb.op("T", lambda t, c=c: t.matmul(
            p_zc[:, :], s_utb[:, :], s_kt[:, c * H:(c + 1) * H],
            start=True, stop=False), after=(g_zc, tok_kt[c]), inc=None)
        tz2 = kb.op("T", lambda t: t.matmul(
            p_zc[:, :], s_onesb[0:1, :], s_zb[0:1, :],
            start=False, stop=True), after=(t_zcp, t_on1), inc=chT)
        t_zsum = kb.op("T", lambda t, c=c: t.matmul(
            p_z[0:1, :], s_on1[:, :], s_kt[:, c * H:(c + 1) * H],
            start=True, stop=True),
            after=(t_zcp,), inc=chT)
        t_dSmm = kb.op("T", lambda t, c=c: t.matmul(
            p_S[:, :], s_vt[:, c * H:(c + 1) * H], s_kt[:, c * H:(c + 1) * H],
            start=True, stop=True),
            after=(tok_vt[c], t_Scp, to2), inc=chT)
        td1 = kb.op("V", lambda v, c=c: v.tensor_tensor(
            out=s_ph[:, 0:H], in0=p_zc[:, :], in1=s_qt[:, c * H:(c + 1) * H],
            op=ALU.mult), after=(tz2, tok_qt[c]), inc=chV)
        g_zc = td1
        td2 = kb.op("V", lambda v, cb=cbase: v.tensor_reduce(
            out=C(cb), in_=s_ph[:, 0:H], axis=AX.X, op=ALU.add),
            after=(td1,), inc=chV)
        td3 = kb.op("V", lambda v, cb=cbase: v.tensor_scalar(
            out=C(cb + 1), in0=C(cb), scalar1=1e-6, scalar2=None,
            op0=ALU.max), after=(td2,), inc=chV)
        td4 = kb.op("V", lambda v, cb=cbase: v.reciprocal(
            out=C(cb + 2), in_=C(cb + 1)), after=(td3,), inc=chV)
        to3 = kb.op("V", lambda v, c=c, cb=cbase: v.tensor_scalar(
            out=s_ot[:, c * H:(c + 1) * H], in0=p_oc[:, :],
            scalar1=C(cb + 2), scalar2=None, op0=ALU.mult),
            after=(to2, td4), inc=chV)
        g_oc = to3
        pc, pn = c % 2, (c + 1) % 2
        tSa = kb.op("V", lambda v, pc=pc, pn=pn: v.tensor_tensor(
            out=s_S[:, pn * H:(pn + 1) * H], in0=p_S[:, :],
            in1=s_S[:, pc * H:(pc + 1) * H], op=ALU.add),
            after=(t_dSmm, t_S0, t_Scp), inc=chV)
        t_Scp = kb.op("V", lambda v, pn=pn: v.tensor_copy(
            out=s_Sb[:, :], in_=s_S[:, pn * H:(pn + 1) * H]),
            after=(tSa, t_S0b), inc=chV)
        tza = kb.op("V", lambda v, pc=pc, pn=pn: v.tensor_tensor(
            out=s_z[0:1, pn * H:(pn + 1) * H], in0=p_z[0:1, :],
            in1=s_z[0:1, pc * H:(pc + 1) * H], op=ALU.add),
            after=(t_zsum, tz2, t_z0, t_zcp), inc=chV)
        t_zcp = kb.op("V", lambda v, pn=pn: v.tensor_copy(
            out=s_zb[0:1, :], in_=s_z[0:1, pn * H:(pn + 1) * H]),
            after=(tza, t_z0b), inc=chV)
        tof = kb.op("T", lambda t, c=c: t.transpose(
            p_of[:, :], s_ot[:, c * H:(c + 1) * H], s_idb[:, :]),
            after=(to3, g_of), inc=chT)
        tok_o[c] = kb.op("V", lambda v, c=c: v.tensor_copy(
            out=s_of[0:64, c * 128:(c + 1) * 128], in_=p_of[:, :]),
            after=(tof,), inc=chV)
        g_of = tok_o[c]

    for i in range(NT + 1):
        if i < NT:
            a1_tile(i)
        if i >= 1:
            a2_chunk(i - 1)
    esA1.close()

    esA3 = contextlib.ExitStack()
    p_at = [esA3.enter_context(pt("p_at0", [128, D], F32)),
            esA3.enter_context(pt("p_at1", [128, D], F32))]
    p_tr2 = esA3.enter_context(pt("p_tr2", [128, D], F32))
    p_gate = esA3.enter_context(pt("p_gate", [128, 16], F32))
    s_at = esA3.enter_context(st("s_at", [128, 2 * D], F32))
    s_glog = esA3.enter_context(st("s_glog", [128, 2 * 16], F32))
    g_at = [None, None]
    g_atS = [None, None]
    g_x = [g_x[0], g_x[1]]
    g_tr2 = None
    g_fm32 = None
    g_gate = None
    g_b16 = [g_lnf[0], g_lnf[1]]
    tok_x2sp = [None] * NT
    tok_xtsp = [None] * NT
    g_w32 = [None, None]
    for i in range(NT):
        p = i % 2
        cb = 32 * p
        atp = p_at[p]
        t1 = kb.op("T", lambda t, i=i, atp=atp: t.matmul(
            atp[:, 0:512], s_of[0:64, i * 128:(i + 1) * 128], s_wob[:, 0:512],
            start=True, stop=True),
            after=(tok_o[i], g_at[p]) + ((g_of, g_oc) if i < 2 else ()),
            inc=None)
        t2 = kb.op("T", lambda t, i=i, atp=atp: t.matmul(
            atp[:, 512:1024], s_of[0:64, i * 128:(i + 1) * 128],
            s_wob[:, 512:1024], start=True, stop=True), after=(), inc=chT)
        if a3_level < 2:
            tev = kb.op("V", lambda v, atp=atp, p=p: v.tensor_copy(
                out=s_w32[:, p * D:(p + 1) * D], in_=atp[:, :]),
                after=(t2, g_at[p], g_w32[p]), inc=chV)
            g_at[p] = tev
            g_w32[p] = tev
            tok_x2sp[i] = tok_xtsp[i] = tw2 = tev
            continue
        atsl = s_at[:, p * D:(p + 1) * D]
        tat = kb.op("V", lambda v, atp=atp, atsl=atsl: v.tensor_copy(
            out=atsl, in_=atp[:, :]), after=(t2, g_at[p], g_atS[p]), inc=chV)
        g_at[p] = tat
        if 11 <= a3_level <= 15:
            toks = []
            t1 = kb.op("V", lambda v, atsl=atsl, cb=cb: v.tensor_reduce(
                out=C(cb), in_=atsl, axis=AX.X, op=ALU.add),
                after=(tat,), inc=chV)
            toks.append(t1)
            if a3_level >= 12:
                t2s = kb.op("S", lambda s, atsl=atsl, cb=cb: s.activation(
                    out=s_sqs[:, :], in_=atsl, func=AF.Square,
                    accum_out=C(cb + 1)), after=(tat,), inc=chS)
                toks.append(t2s)
            if a3_level >= 13:
                t3 = kb.op("V", lambda v, cb=cb: v.tensor_scalar(
                    out=C(cb + 2), in0=C(cb), scalar1=1.0 / D, scalar2=None,
                    op0=ALU.mult), after=(t1,), inc=chV)
                t4 = kb.op("V", lambda v, cb=cb: v.tensor_tensor(
                    out=C(cb + 3), in0=C(cb + 2), in1=C(cb + 2), op=ALU.mult),
                    after=(t3,), inc=chV)
                t5 = kb.op("V", lambda v, cb=cb: v.scalar_tensor_tensor(
                    out=C(cb + 4), in0=C(cb + 1), scalar=1.0 / D,
                    in1=C(cb + 3), op0=ALU.mult, op1=ALU.subtract),
                    after=(t2s, t4), inc=chV)
                t8 = kb.op("V", lambda v, cb=cb: v.tensor_scalar(
                    out=C(cb + 7), in0=C(cb + 2), scalar1=C(cb + 4),
                    scalar2=-1.0, op0=ALU.mult, op1=ALU.mult),
                    after=(t5,), inc=chV)
                toks += [t8]
            if a3_level >= 14:
                t6 = kb.op("S", lambda s, cb=cb: s.activation(
                    out=C(cb + 5), in_=C(cb + 4), func=AF.Ln, bias=c_epsln),
                    after=(t5, t_cst), inc=chS)
                t7 = kb.op("S", lambda s, cb=cb: s.activation(
                    out=C(cb + 6), in_=C(cb + 5), func=AF.Exp, scale=c_mhalf),
                    after=(t6,), inc=chS)
                toks += [t7]
            tev = kb.op("V", lambda v, atsl=atsl, p=p: v.tensor_copy(
                out=s_w32[:, p * D:(p + 1) * D], in_=atsl),
                after=tuple(toks) + (g_w32[p],), inc=chV)
            g_atS[p] = tev
            g_w32[p] = tev
            tok_x2sp[i] = tok_xtsp[i] = tw2 = tev
            continue
        rstd, mb, tln = ln_stats(atsl, cb, "expln", (tat,))
        xsl = s_x[:, p * D:(p + 1) * D]
        tx = kb.op("Y", lambda y, i=i, xsl=xsl: y.dma_start(
            out=xsl, in_=d_x[i * 128:(i + 1) * 128, :]),
            after=(g_x[p],), inc=chX[p], k=16)
        wsl = s_w32[:, p * D:(p + 1) * D]
        txf = kb.op("S", lambda s, xsl=xsl, wsl=wsl, mb=mb: s.activation(
            out=wsl, in_=xsl, func=AF.Identity, bias=mb),
            after=(tx, tln[1], g_w32[p]), inc=chS)
        tx2 = kb.op("V", lambda v, atsl=atsl, xsl=xsl, wsl=wsl, rstd=rstd:
                    v.scalar_tensor_tensor(
                        out=xsl, in0=atsl, scalar=rstd, in1=wsl,
                        op0=ALU.mult, op1=ALU.add),
                    after=(txf, tln[0]), inc=chV)
        g_atS[p] = tx2
        tok_x2sp[i] = kb.op("Y", lambda y, i=i, xsl=xsl: y.dma_start(
            out=d_x2[i * 128:(i + 1) * 128, :], in_=xsl),
            after=(tx2,), inc=chSp[p], k=16)
        if a3_level < 3:
            tok_xtsp[i] = tw2 = tok_x2sp[i]
            g_x[p] = tok_x2sp[i]
            continue
        rstd2, mb2, tln2 = ln_stats(xsl, cb + 8, "expln", (tx2,))
        tln2f = kb.op("S", lambda s, xsl=xsl, wsl=wsl, rstd2=rstd2, mb2=mb2:
                      s.activation(out=wsl, in_=xsl, func=AF.Identity,
                                   scale=rstd2, bias=mb2),
                      after=tln2 + (txf,), inc=chS)
        g_x[p] = tok_x2sp[i]
        if a3_level < 4:
            tok_xtsp[i] = tw2 = tln2f
            g_w32[p] = tln2f
            continue
        ttr = None
        for k in range(8):
            ttr = kb.op("T", lambda t, k=k, wsl=wsl: t.transpose(
                p_tr2[:, k * 128:(k + 1) * 128], wsl[:, k * 128:(k + 1) * 128],
                s_idf[:, :]),
                after=(tln2f, g_tr2) if k == 0 else (),
                inc=chT if k == 7 else None)
        g_w32[p] = ttr
        tfm32 = kb.op("V", lambda v: v.tensor_copy(
            out=s_fm32[:, :], in_=p_tr2[:, :]), after=(ttr, g_fm32), inc=chV)
        bsl = s_b16[:, p * D:(p + 1) * D]
        txtb = kb.op("V", lambda v, bsl=bsl: v.tensor_copy(
            out=bsl, in_=p_tr2[:, :]),
            after=(ttr, g_b16[p], tfm32), inc=chV)
        g_tr2 = txtb
        if a3_xtspill:
            tok_xtsp[i] = kb.op("Y", lambda y, i=i, bsl=bsl: y.dma_start(
                out=d_xt[:, :, i * 128:(i + 1) * 128].rearrange("k p n -> p k n"),
                in_=bsl.rearrange("p (k n) -> p k n", k=8)),
                after=(txtb,), inc=chXt[p], k=16)
        else:
            tok_xtsp[i] = txtb
        g_b16[p] = tok_xtsp[i]
        if not a3_gate:
            tw2 = txtb
            continue
        tg = None
        for k in range(8):
            tg = kb.op("T", lambda t, k=k: t.matmul(
                p_gate[:, :], s_fm32[:, k * 128:(k + 1) * 128],
                s_gw[:, k * 16:(k + 1) * 16], start=(k == 0), stop=(k == 7)),
                after=(tfm32, g_gate) if k == 0 else (),
                inc=chT if k == 7 else None)
        g_fm32 = tg
        gsl = s_glog[:, p * 16:(p + 1) * 16]
        tgc = kb.op("V", lambda v, gsl=gsl: v.tensor_copy(
            out=gsl, in_=p_gate[:, :]), after=(tg, g_gate), inc=chV)
        tmx = kb.op("V", lambda v, cb=cb, gsl=gsl: v.reduce_max(
            out=C(cb + 16), in_=gsl, axis=AX.X), after=(tgc,), inc=chV)
        tnm = kb.op("V", lambda v, cb=cb: v.tensor_scalar(
            out=C(cb + 17), in0=C(cb + 16), scalar1=-1.0, scalar2=None,
            op0=ALU.mult), after=(tmx,), inc=chV)
        esl = s_g16[:, p * 16:(p + 1) * 16]
        bsl2 = s_g16b[:, p * 16:(p + 1) * 16]
        te = kb.op("S", lambda s, cb=cb, esl=esl, gsl=gsl: s.activation(
            out=esl, in_=gsl, func=AF.Exp, bias=C(cb + 17)),
            after=(tnm,), inc=chS)
        g_gate = tgc
        tse = kb.op("V", lambda v, cb=cb, esl=esl: v.tensor_reduce(
            out=C(cb + 18), in_=esl, axis=AX.X, op=ALU.add),
            after=(te,), inc=chV)
        trs = kb.op("V", lambda v, cb=cb: v.reciprocal(
            out=C(cb + 19), in_=C(cb + 18)), after=(tse,), inc=chV)
        tm1 = kb.op("V", lambda v, cb=cb, esl=esl: v.reduce_max(
            out=C(cb + 20), in_=esl, axis=AX.X), after=(te,), inc=chV)
        teq = kb.op("V", lambda v, cb=cb, esl=esl, bsl2=bsl2: v.tensor_scalar(
            out=bsl2, in0=esl, scalar1=C(cb + 20), scalar2=0.0,
            op0=ALU.subtract, op1=ALU.is_ge), after=(tm1,), inc=chV)
        tmk = kb.op("V", lambda v, esl=esl, bsl2=bsl2: v.scalar_tensor_tensor(
            out=bsl2, in0=bsl2, scalar=-1e30, in1=esl,
            op0=ALU.mult, op1=ALU.add), after=(teq,), inc=chV)
        tm2 = kb.op("V", lambda v, cb=cb, bsl2=bsl2: v.reduce_max(
            out=C(cb + 21), in_=bsl2, axis=AX.X), after=(tmk,), inc=chV)
        th = kb.op("V", lambda v, cb=cb, esl=esl, i=i: v.tensor_scalar(
            out=s_hit[:, i * 16:(i + 1) * 16], in0=esl, scalar1=C(cb + 21),
            scalar2=None, op0=ALU.is_ge), after=(tm2,), inc=chV)
        tw1 = kb.op("V", lambda v, cb=cb, esl=esl, bsl2=bsl2: v.tensor_scalar(
            out=bsl2, in0=esl, scalar1=C(cb + 19), scalar2=None,
            op0=ALU.mult), after=(trs, tm2), inc=chV)
        tw2 = kb.op("V", lambda v, i=i, bsl2=bsl2: v.tensor_tensor(
            out=s_wte[:, i * 16:(i + 1) * 16], in0=bsl2,
            in1=s_hit[:, i * 16:(i + 1) * 16], op=ALU.mult),
            after=(th, tw1), inc=chV)
    esA3.close()
    esA.close()
    a_done = (tok_x2sp[NT - 2], tok_x2sp[NT - 1], tok_xtsp[NT - 2],
              tok_xtsp[NT - 1], tw2) + tok_wcache

    if stop_after == "A":
        kb.op("G", lambda g: g.nop(), after=a_done)
        with nc.Block() as block:
            kb.emit(block)
        es.close()
        return nc

    esF = contextlib.ExitStack()
    s_we = esF.enter_context(st("s_we", [128, 2 * WE_COLS], BF16))
    s_beb = esF.enter_context(st("s_beb", [128, 2 * 8], F32))
    s_xtb = esF.enter_context(st("s_xtb", [128, 2 * 8 * GRP], BF16))
    s_sf = esF.enter_context(st("s_sf", [128, NG * 1024], BF16))
    s_a = esF.enter_context(st("s_a", [128, 2048], BF16))
    s_s3b = esF.enter_context(st("s_s3b", [128, 2048], BF16))
    s_sq = esF.enter_context(st("s_sq", [128, 2 * 2048], BF16))
    s_sums = esF.enter_context(st("s_sums", [1, NG * GRP], F32))
    s_timeb = esF.enter_context(st("s_timeb", [2, NG * GRP], BF16))
    s_onef = esF.enter_context(st("s_onef", [128, 1], F32))
    s_s1f = esF.enter_context(st("s_s1f", [128, 1024], F32))
    s_wshb = None

    t_onef = kb.op("G", lambda g: g.memset(s_onef[:, :], 1.0),
                   after=a_done, inc=chGc)
    t_ones_time = kb.op("G", lambda g: g.memset(s_timeb[:, :], 1.0),
                        after=a_done, inc=chGc)

    g_y = [None] * (HALF // 128)
    g_we = [None, None]
    g_xt = [None, None]
    g_zz = [None, None]
    g_o2 = [(None, None)] * 4
    g_sfgrp = [None] * NG
    g_time = None
    g_beb = [None, None]
    g_ps1 = [None, None]
    g_ps3 = [None, None]
    g_s1f = [None]
    g_bscr = [None, None]
    g_sq = [None, None]

    def emit_wload(pe, we_cols, d_wsrc, d_bsrc, first=False):
        wsl = s_we[:, pe * WE_COLS: pe * WE_COLS + we_cols]
        w_after = ((g_we[0], g_we[1]) if we_cols > WE_COLS else (g_we[pe],))
        tw = kb.op("Y", lambda y: y.dma_start(out=wsl, in_=d_wsrc),
                   after=w_after + (a_done if first else ()),
                   inc=chWt[pe], k=16)
        if d_bsrc is not None:
            bsl = s_beb[:, pe * 8:(pe + 1) * 8]
            tb = kb.op("Y", lambda y, bsl=bsl: y.dma_start(out=bsl, in_=d_bsrc),
                       after=(g_beb[pe],), inc=chBe[pe], k=16)
        else:
            bsl = s_beb[:, 0:16]
            tb = kb.op("Y", lambda y, bsl=bsl: y.dma_start(out=bsl, in_=d_bsh),
                       after=(g_beb[0], g_beb[1]), inc=chBe[0], k=16)
        kb.wcache[(pe, we_cols)] = (tw, tb)
    out_toks = []

    def ffn_expert(h, e, pe, first, ps1_of, ps3_of, p_o2, we_cols, w1o, w3o,
                   w2o, r0o, nit, nkt, d_wsrc, d_bsrc, stride1, wte_e,
                   groups=tuple(range(NG)), sfs=1024, skip_w=False,
                   s1_dbuf=True, prefetch=None):
        nonlocal g_time
        HB = h * HALF
        g0 = groups[0]
        wsl = s_we[:, pe * WE_COLS: pe * WE_COLS + we_cols]
        if not skip_w:
            emit_wload(pe, we_cols, d_wsrc, d_bsrc, first)
        bsl = s_beb[:, 0:16] if d_bsrc is None else s_beb[:, pe * 8:(pe + 1) * 8]
        tw, tb = kb.wcache[(pe, we_cols)]
        pending_sum = None
        last_smul = None
        t_silu = [None] * NG
        t_s3b = [None] * NG
        t_smul = [None] * NG
        t_sumcp = [None] * NG
        for g in groups:
            pg = g % 2
            pidx = (g % 2) if s1_dbuf else 0
            s1r = ps1_of(g)
            s3r = ps3_of(g)
            sfo = (g - g0) * sfs
            xsl = s_xtb[:, pg * 8 * GRP:(pg + 1) * 8 * GRP]
            txt = kb.op("Y", lambda y, g=g, xsl=xsl, HB=HB: y.dma_start(
                out=xsl,
                in_=d_xt[:, :, HB + g * GRP:HB + (g + 1) * GRP]
                .rearrange("k p n -> p k n")),
                after=(g_xt[pg],) + (a_done if first and g < 2 else ()),
                inc=chXL[pg], k=16)
            if g == groups[0] and prefetch is not None:
                prefetch()
            tmm1 = tmm3 = None
            for it in range(nit):
                for k in range(8):
                    af = ((tw, txt) + tuple(g_ps1[pidx] or ())
                          if (it == 0 and k == 0) else ())
                    tmm1 = kb.op("T", lambda t, it=it, k=k, xsl=xsl, s1r=s1r:
                                 t.matmul(
                        s1r[:, it * GRP:(it + 1) * GRP],
                        wsl[:, w1o + k * stride1 + it * 128:
                            w1o + k * stride1 + it * 128 + 128],
                        xsl[:, k * GRP:(k + 1) * GRP],
                        start=(k == 0), stop=(k == 7)), after=af,
                        inc=chT if (it == nit - 1 and k == 7) else None)
            for it in range(nit):
                for k in range(8):
                    af = (g_ps3[pidx],) if (it == 0 and k == 0) else ()
                    tmm3 = kb.op("T", lambda t, it=it, k=k, xsl=xsl, s3r=s3r:
                                 t.matmul(
                        s3r[:, it * GRP:(it + 1) * GRP],
                        wsl[:, w3o + k * stride1 + it * 128:
                            w3o + k * stride1 + it * 128 + 128],
                        xsl[:, k * GRP:(k + 1) * GRP],
                        start=(k == 0), stop=(k == 7)), after=af,
                        inc=chT if (it == nit - 1 and k == 7) else None)
            g_xt[pg] = tmm3
            for it0 in range(0, nit, 4):
                nb = min(4, nit - it0)
                tcp1 = kb.op("V", lambda v, it0=it0, nb=nb, s1r=s1r:
                             v.tensor_copy(
                    out=s_s1f[:, 0:nb * GRP],
                    in_=s1r[:, it0 * GRP:(it0 + nb) * GRP]),
                    after=(tmm1, g_s1f[0]), inc=chV)
                for it in range(it0, it0 + nb):
                    t_silu[g] = kb.op("S", lambda s, it=it, it0=it0:
                                      s.activation(
                        out=s_a[:, it * GRP:(it + 1) * GRP],
                        in_=s_s1f[:, (it - it0) * GRP:(it - it0 + 1) * GRP],
                        func=AF.Silu, bias=bsl[:, it:it + 1]),
                        after=(tcp1, tb) if it == it0 else (), inc=chS)
                g_s1f[0] = t_silu[g]
            g_ps1[pidx] = (tcp1,)
            for it in range(nit):
                t_s3b[g] = kb.op("V", lambda v, it=it, nit=nit, s3r=s3r:
                                 v.tensor_scalar(
                    out=s_s3b[:, it * GRP:(it + 1) * GRP],
                    in0=s3r[:, it * GRP:(it + 1) * GRP],
                    scalar1=bsl[:, nit + it:nit + it + 1], scalar2=None,
                    op0=ALU.add),
                    after=(tmm3, tb, last_smul) if it == 0 else (), inc=chV)
            sf_guard = tuple(g_sfgrp) if sfs != 1024 else (g_sfgrp[g],)
            t_smul[g] = kb.op("V", lambda v, sfo=sfo, nit=nit: v.tensor_tensor(
                out=s_sf[:, sfo:sfo + nit * GRP],
                in0=s_a[:, 0:nit * GRP], in1=s_s3b[:, 0:nit * GRP],
                op=ALU.mult),
                after=(t_silu[g], t_s3b[g]) + sf_guard, inc=chV)
            tsq = kb.op("S", lambda s, sfo=sfo, nit=nit, pg=pg: s.activation(
                out=s_sq[:, pg * 2048:pg * 2048 + nit * GRP],
                in_=s_sf[:, sfo:sfo + nit * GRP], func=AF.Square),
                after=(t_smul[g], g_sq[pg]), inc=chS)
            def emit_sum(g=g, s3r=s3r, tsq=tsq, ts3b=t_s3b[g], pidx=pidx):
                tsum = None
                for it in range(nit):
                    tsum = kb.op("T", lambda t, it=it, s3r=s3r, g=g: t.matmul(
                        s3r[0:1, 0:GRP], s_on1[:, :],
                        s_sq[:, (g % 2) * 2048 + it * GRP:
                             (g % 2) * 2048 + (it + 1) * GRP],
                        start=(it == 0), stop=(it == nit - 1)),
                        after=(tsq, ts3b) if it == 0 else (),
                        inc=chT if it == nit - 1 else None)
                g_sq[g % 2] = tsum
                t_sumcp[g] = kb.op("V", lambda v, g=g, s3r=s3r: v.tensor_copy(
                    out=s_sums[0:1, g * GRP:(g + 1) * GRP],
                    in_=s3r[0:1, 0:GRP]),
                    after=(tsum, g_time), inc=chV)
                g_ps3[pidx] = t_sumcp[g]

            last_smul = t_smul[g]
            if not s1_dbuf:
                emit_sum()
            else:
                if pending_sum is not None:
                    pending_sum()
                pending_sum = emit_sum
        if pending_sum is not None:
            pending_sum()
            pending_sum = None
        glast = groups[-1]
        tq = kb.op("S", lambda s: s.activation(
            out=s_timeb[0:1, g0 * GRP:(glast + 1) * GRP],
            in_=s_sums[0:1, g0 * GRP:(glast + 1) * GRP],
            func=AF.Sqrt, bias=c_one[0:1, :]),
            after=(t_sumcp[glast], t_ones_time, g_time, t_cst), inc=chS)
        t_last_k2 = None
        t_last_w2 = None
        for g in groups:
            sfo = (g - g0) * sfs
            for tt in range(2):
                lt = g * 2 + tt
                gi = h * (HALF // 128) + lt
                po = lt % 4
                cb = 32 * (lt % 2)
                pob = p_o2[po]
                tmm = None
                for kt in range(nkt):
                    for nh in range(2):
                        wnh = 512 if nh == 0 else 511
                        af = ()
                        if kt == 0 and nh == 0:
                            af = (t_smul[g], g_o2[po][0], g_o2[po][1],
                                  t_sumcp[glast])
                        tmm = kb.op("T", lambda t, sfo=sfo, tt=tt, kt=kt, nh=nh,
                                    wnh=wnh, pob=pob: t.matmul(
                            pob(nh * 512, nh * 512 + wnh),
                            s_sf[:, sfo + kt * GRP + tt * 128:
                                 sfo + kt * GRP + tt * 128 + 128],
                            wsl[:, w2o + kt * NSH + nh * 512:
                                w2o + kt * NSH + nh * 512 + wnh],
                            start=(kt == 0), stop=False), after=af)
                for nh in range(2):
                    wnh = 512 if nh == 0 else 511
                    tmm = kb.op("T", lambda t, g=g, tt=tt, nh=nh, wnh=wnh,
                                pob=pob: t.matmul(
                        pob(nh * 512, nh * 512 + wnh),
                        s_timeb[0:2, g * GRP + tt * 128:g * GRP + tt * 128 + 128],
                        wsl[0:2, r0o + nh * 512:r0o + nh * 512 + wnh],
                        start=False, stop=True),
                        after=(tq,) if nh == 0 else (),
                        inc=chT if nh == 1 else None)
                t_last_k2 = tmm
                g_sfgrp[g] = tmm
                t_last_w2 = tmm
                pz = lt % 2
                zsl = s_w32[:, pz * D:(pz + 1) * D]
                ysl = s_y[:, lt * D:(lt + 1) * D]
                bscr = s_b16[:, pz * D:pz * D + NSH]
                tob = kb.op("V", lambda v, pob=pob, bscr=bscr:
                            v.tensor_copy(out=bscr, in_=pob(0, NSH)),
                            after=(tmm, g_bscr[pz]), inc=chV)
                tq1 = kb.op("S", lambda s, bscr=bscr, cb=cb:
                            s.activation(out=s_sqs[:, 0:NSH], in_=bscr,
                                         func=AF.Square, accum_out=C(cb)),
                            after=(tob,), inc=chS)
                g_bscr[pz] = tq1
                tt0 = kb.op("S", lambda s, cb=cb: s.activation(
                    out=C(cb + 1), in_=C(cb), func=AF.Sqrt, bias=c_one),
                    after=(tq1, t_cst), inc=chS)
                if wte_e is not None:
                    wcol = s_wte[:, gi * 16 + wte_e:gi * 16 + wte_e + 1]
                    hcol = s_hit[:, gi * 16 + wte_e:gi * 16 + wte_e + 1]
                else:
                    wcol = hcol = None
                tzz = kb.op("V", lambda v, pob=pob, zsl=zsl, ysl=ysl,
                            wcol=wcol: v.scalar_tensor_tensor(
                    out=zsl[:, 1:D], in0=pob(0, NSH),
                    scalar=(wcol if wcol is not None else 1.0),
                    in1=ysl[:, 1:D], op0=ALU.mult, op1=ALU.add),
                    after=(tmm, g_zz[pz], g_y[lt]),
                    inc=chV)
                tz0 = kb.op("V", lambda v, zsl=zsl, ysl=ysl, cb=cb,
                            wcol=wcol: v.scalar_tensor_tensor(
                    out=zsl[:, 0:1], in0=C(cb + 1),
                    scalar=(wcol if wcol is not None else 1.0),
                    in1=ysl[:, 0:1], op0=ALU.mult, op1=ALU.add),
                    after=(tt0,), inc=chV)
                tS1 = kb.op("S", lambda s, zsl=zsl, cb=cb: s.activation(
                    out=s_sqs[:, 0:NSH], in_=zsl[:, 1:D], func=AF.Square,
                    accum_out=C(cb + 2)), after=(tzz,), inc=chS)
                tzq = kb.op("V", lambda v, zsl=zsl, cb=cb: v.tensor_tensor(
                    out=C(cb + 3), in0=zsl[:, 0:1], in1=zsl[:, 0:1],
                    op=ALU.mult), after=(tz0,), inc=chV)
                tdd = kb.op("V", lambda v, cb=cb: v.tensor_tensor(
                    out=C(cb + 4), in0=C(cb + 3), in1=C(cb + 2),
                    op=ALU.subtract), after=(tzq, tS1), inc=chV)
                tdm = kb.op("V", lambda v, cb=cb: v.tensor_scalar(
                    out=C(cb + 5), in0=C(cb + 4), scalar1=EPS, scalar2=None,
                    op0=ALU.max), after=(tdd,), inc=chV)
                trc = kb.op("V", lambda v, cb=cb: v.reciprocal(
                    out=C(cb + 6), in_=C(cb + 5)), after=(tdm,), inc=chV)
                tsq2 = kb.op("S", lambda s, cb=cb: s.activation(
                    out=C(cb + 7), in_=C(cb + 6), func=AF.Sqrt),
                    after=(trc,), inc=chS)
                if hcol is not None:
                    tr1 = kb.op("V", lambda v, cb=cb: v.tensor_scalar(
                        out=C(cb + 8), in0=C(cb + 7), scalar1=1.0,
                        scalar2=None, op0=ALU.subtract), after=(tsq2,),
                        inc=chV)
                    tr2 = kb.op("V", lambda v, cb=cb, hcol=hcol:
                                v.scalar_tensor_tensor(
                        out=C(cb + 9), in0=hcol, scalar=C(cb + 8),
                        in1=s_onef[:, :], op0=ALU.mult, op1=ALU.add),
                        after=(tr1, t_onef), inc=chV)
                    rcol = C(cb + 9)
                    rtok = tr2
                else:
                    rcol = C(cb + 7)
                    rtok = tsq2
                ty = kb.op("S", lambda s, zsl=zsl, ysl=ysl, rcol=rcol:
                           s.activation(out=ysl, in_=zsl, func=AF.Identity,
                                        scale=rcol),
                           after=(rtok, tzz, tz0), inc=chS)
                g_y[lt] = ty
                g_zz[pz] = ty
                g_o2[po] = (tzz, tob)
                if s1_dbuf:
                    if po < 2:
                        g_ps1[po] = (tzz, tob)
                    else:
                        g_ps3[po - 2] = tzz
                else:
                    g_ps1[0] = (tzz, tob)
        g_we[pe] = t_last_w2
        if we_cols > WE_COLS:
            g_we[0] = g_we[1] = t_last_w2
        g_beb[pe] = t_s3b[NG - 1]
        g_time = t_last_k2
        return ty

    kb.wcache = {}
    NLT = HALF // 128
    ty_last = None
    for h in range(2):
        yg = tuple(t for t in g_y if t is not None)
        ty0 = kb.op("G", lambda g: g.memset(s_y[:, :], 0.0),
                    after=yg + ((a_done) if h == 0 else ()), inc=chGc)
        ty1 = kb.op("G", lambda g: g.memset(
            bass.AP(s_y, 0, [[NLT * D, 128], [D, NLT], [1, 1]]), 1.0),
            after=(ty0,), inc=chGc)
        for lt in range(NLT):
            g_y[lt] = ty1
        esB = contextlib.ExitStack()
        psA = esB.enter_context(pt("psA_%d" % h, [128, 2048], F32))
        psB = esB.enter_context(pt("psB_%d" % h, [128, 2048], F32))
        psab = [psA, psB]
        ps1_of = lambda g: psab[g % 2][:, 0:1024]
        ps3_of = lambda g: psab[g % 2][:, 1024:2048]
        po2 = [lambda lo, hi: psA[:, lo:hi],
               lambda lo, hi: psB[:, lo:hi],
               lambda lo, hi: psA[:, 1024 + lo:1024 + hi],
               lambda lo, hi: psB[:, 1024 + lo:1024 + hi]]
        for e in range(NE):
            if e == 0:
                emit_wload(0, WE_COLS, d_wec[0, :, :], d_bexp[0, :, :],
                           first=(h == 0))
            pf = None
            if e + 1 < NE:
                pf = (lambda e1=e + 1: emit_wload(
                    e1 % 2, WE_COLS, d_wec[e1, :, :], d_bexp[e1, :, :]))
            ty_last = ffn_expert(h, e, e % 2, (h == 0 and e == 0),
                                 ps1_of, ps3_of, po2, WE_COLS,
                                 W1_OFF, W3_OFF, W2_OFF, R0_OFF, 4, 4,
                                 d_wec[e, :, :], d_bexp[e, :, :], 512, e,
                                 skip_w=True, prefetch=pf)
        esB.close()
        g_ps1[0] = g_ps1[1] = (ty_last,)
        g_ps3[0] = g_ps3[1] = ty_last
        for _i in range(4):
            g_o2[_i] = (ty_last, ty_last)
        esC = contextlib.ExitStack()
        p_s1C = esC.enter_context(pt("p_s1C_%d" % h, [128, 2048], F32))
        p_s3C = esC.enter_context(pt("p_s3C_%d" % h, [128, 2048], F32))
        po2C = [lambda lo, hi: p_s1C[:, lo:hi],
                lambda lo, hi: p_s1C[:, 1024 + lo:1024 + hi],
                lambda lo, hi: p_s3C[:, lo:hi],
                lambda lo, hi: p_s3C[:, 1024 + lo:1024 + hi]]
        ps1C_of = lambda g: p_s1C[:, :]
        ps3C_of = lambda g: p_s3C[:, :]
        for sb in range(2):
            groups = tuple(range(sb * 4, sb * 4 + 4))
            ty_last = ffn_expert(h, 100 + sb, 0, False,
                                 ps1C_of, ps3C_of, po2C, WS_COLS,
                                 SW1_OFF, SW3_OFF, SW2_OFF, SR0_OFF, 8, 8,
                                 d_wshc, None, 1024, None,
                                 groups=groups, sfs=2048, skip_w=(sb == 1),
                                 s1_dbuf=False)
            g_ps1[0] = g_ps1[1] = (ty_last,)
            g_ps3[0] = g_ps3[1] = ty_last
            for _i in range(4):
                g_o2[_i] = (ty_last, ty_last)
        esC.close()
        for lt in range(NLT):
            gi = h * NLT + lt
            px = lt % 2
            cb = 32 * px + 16
            ysl = s_y[:, lt * D:(lt + 1) * D]
            xsl = s_x[:, px * D:(px + 1) * D]
            wsl2 = s_w32[:, px * D:(px + 1) * D]
            tx2l = kb.op("Y", lambda y, gi=gi, xsl=xsl: y.dma_start(
                out=xsl, in_=d_x2[gi * 128:(gi + 1) * 128, :]),
                after=(g_x[px], tok_x2sp[NT - 2], tok_x2sp[NT - 1]),
                inc=chX[px], k=16)
            rstd, mb, tln = ln_stats(ysl, cb, "sqrt", (g_y[lt],))
            txf = kb.op("S", lambda s, xsl=xsl, wsl2=wsl2, mb=mb: s.activation(
                out=wsl2, in_=xsl, func=AF.Identity, bias=mb),
                after=(tx2l, tln[1], g_zz[px]), inc=chS)
            tst = kb.op("V", lambda v, ysl=ysl, xsl=xsl, wsl2=wsl2, rstd=rstd:
                        v.scalar_tensor_tensor(
                out=xsl, in0=ysl, scalar=rstd, in1=wsl2,
                op0=ALU.mult, op1=ALU.add), after=(txf, tln[0]), inc=chV)
            tout = kb.op("Y", lambda y, gi=gi, xsl=xsl: y.dma_start(
                out=d_out[gi * 128:(gi + 1) * 128, :], in_=xsl),
                after=(tst,), inc=chOut[px], k=16)
            out_toks.append(tout)
            g_x[px] = tout
            g_zz[px] = tst
            g_y[lt] = tst
    esF.close()

    if debug:
        tdbg1 = kb.op("Y", lambda y: y.dma_start(
            out=d_dbg[:, 0:NT * 16], in_=s_wte[:, :]),
            after=(out_toks[-1],), inc=chC, k=16)
        tdbg2 = kb.op("Y", lambda y: y.dma_start(
            out=d_dbg[:, 512:512 + NT * 16], in_=s_hit[:, :]),
            after=(tdbg1,), inc=chC, k=16)
        out_toks.append(tdbg2)

    kb.op("G", lambda g: g.nop(), after=tuple(out_toks[-3:]))

    with nc.Block() as block:
        kb.emit(block)
    es.close()
    return nc


_NC_CACHE = {}


def _get_nc(debug=False):
    if debug not in _NC_CACHE:
        _NC_CACHE[debug] = build_nc(debug)
    return _NC_CACHE[debug]


def kernel(**inputs):
    g, x = pack_host(inputs)
    nc = _get_nc(False)
    in_maps = []
    for b in range(B):
        m = dict(g)
        m["x"] = np.ascontiguousarray(x[b])
        in_maps.append(m)
    res = run_bass_kernel_spmd(nc, in_maps, core_ids=list(range(B)))
    return np.stack([r["out"] for r in res.results]).astype(np.float32)


# version 14
# speedup vs baseline: 1.1560x; 1.0056x over previous
import contextlib

import numpy as np

import concourse.bass as bass
import concourse.mybir as mybir
from concourse.bass_utils import run_bass_kernel_spmd

F32 = mybir.dt.float32
BF16 = mybir.dt.bfloat16
AF = mybir.ActivationFunctionType
ALU = mybir.AluOpType
AX = mybir.AxisListType

B, T, D, H = 8, 4096, 1024, 64
NE, NI = 16, 511
NSH = 1023
NT = T // 128
HALF = T // 2
GRP = 256
NG = HALF // GRP
EPS_LN = 1e-5
EPS = 1e-6

W1_OFF = 0
W3_OFF = 8 * 512
W2_OFF = 2 * 8 * 512
R0_OFF = W2_OFF + 4 * NSH
WE_COLS = R0_OFF + NSH

SW1_OFF = 0
SW3_OFF = 8 * 1024
SW2_OFF = 2 * 8 * 1024
SR0_OFF = SW2_OFF + 8 * NSH
WS_COLS = SR0_OFF + NSH


class _Chan:
    __slots__ = ("sem", "n", "name", "owner")

    def __init__(self, sem, name, owner):
        self.sem, self.n, self.name, self.owner = sem, 0, name, owner


class KB:
    ENGS = ("Y", "G", "T", "V", "S")

    def __init__(self, nc, es):
        self.nc = nc
        self.es = es
        self.items = []
        self.chans = []

    def chan(self, name, owner=None):
        ch = _Chan(self.es.enter_context(self.nc.semaphore(name)), name, owner)
        self.chans.append(ch)
        return ch

    def op(self, eng, emit, after=(), inc=None, k=1):
        idx = len(self.items)
        tok = None
        if inc is not None:
            tok = (inc, idx)
        after = tuple(t for t in after if t is not None)
        self.items.append([eng, after, emit, inc, k, idx])
        return tok

    def emit(self, block):
        used = set()
        for eng, after, _, _, _, _ in self.items:
            for (ch, idx) in after:
                if ch.owner == eng == "T":
                    continue
                used.add((ch.name, idx))
        val_at = {}
        counts = {}
        for it in self.items:
            eng, after, emitfn, inc, k, idx = it
            if inc is not None and (k == 16 or (inc.name, idx) in used):
                counts[inc.name] = counts.get(inc.name, 0) + k
                val_at[(inc.name, idx)] = counts[inc.name]
            elif inc is not None:
                it[3] = None
        decs = {"T": block.tensor, "V": block.vector, "S": block.scalar,
                "G": block.gpsimd, "Y": block.sync}
        for engname in self.ENGS:
            items = [it for it in self.items if it[0] == engname]
            if not items:
                continue

            def section(eng, items=items, engname=engname):
                waited = {}
                for _, after, emitfn, inc, k, idx in items:
                    for (ch, tidx) in after:
                        if ch.owner == engname == "T":
                            continue
                        val = val_at[(ch.name, tidx)]
                        if waited.get(ch.name, 0) < val:
                            eng.wait_ge(ch.sem, val)
                            waited[ch.name] = val
                    instr = emitfn(eng)
                    if inc is not None:
                        if instr is None:
                            instr = eng.nop()
                        instr.then_inc(inc.sem, k)

            decs[engname](section)


def _pack_T(w, rpad=None):
    R, K = w.shape
    assert K % 128 == 0
    rpad = R if rpad is None else rpad
    wt = np.zeros((K, rpad), np.float32)
    wt[:, :R] = w.T
    wt = wt.reshape(K // 128, 128, rpad)
    return np.ascontiguousarray(wt.transpose(1, 0, 2).reshape(128, (K // 128) * rpad))


def _pack_rows(wt, nrow_pad):
    R, C = wt.shape
    out = np.zeros((nrow_pad * 128, C), np.float32)
    out[:R] = wt
    out = out.reshape(nrow_pad, 128, C).transpose(1, 0, 2)
    return np.ascontiguousarray(out.reshape(128, nrow_pad * C))


def _bias_fm(b, ntile):
    out = np.zeros((ntile * 128,), np.float32)
    out[:b.shape[0]] = b
    return np.ascontiguousarray(out.reshape(ntile, 128).T)


def pack_host(inputs):
    f = np.float32
    g = {}
    g["wq"] = _pack_T(np.ascontiguousarray(inputs["L0_attn_W_q"].T))
    g["wk"] = _pack_T(np.ascontiguousarray(inputs["L0_attn_W_k"].T))
    g["wv"] = _pack_T(np.ascontiguousarray(inputs["L0_attn_W_v"].T))
    g["wo"] = np.ascontiguousarray(inputs["L0_attn_W_o"], dtype=f)
    g["gw"] = _pack_T(np.ascontiguousarray(inputs["L0_ffn_gate_w"]))
    blobs = []
    bias_blobs = []
    for e in range(NE):
        we1 = np.asarray(inputs["L0_ffn_We1"][e], f)
        we3 = np.asarray(inputs["L0_ffn_We3"][e], f)
        we2 = np.asarray(inputs["L0_ffn_We2"][e], f)
        be1 = np.asarray(inputs["L0_ffn_be1"][e], f)
        be3 = np.asarray(inputs["L0_ffn_be3"][e], f)
        be2 = np.asarray(inputs["L0_ffn_be2"][e], f)
        p1 = _pack_T(we1, 512)
        p3 = _pack_T(we3, 512)
        w2t = np.ascontiguousarray(we2.T)
        p2 = _pack_rows(w2t[1:], 4)
        r0 = np.zeros((128, NSH), f)
        r0[0] = w2t[0]
        r0[1] = be2
        blobs.append(np.concatenate([p1, p3, p2, r0], axis=1))
        bias_blobs.append(np.concatenate([_bias_fm(be1, 4), _bias_fm(be3, 4)],
                                         axis=1))
    g["wexp"] = np.stack(blobs)
    g["bexp"] = np.stack(bias_blobs)
    assert g["wexp"].shape == (NE, 128, WE_COLS), g["wexp"].shape
    ws1 = np.asarray(inputs["L0_ffn_Ws1"], f)
    ws3 = np.asarray(inputs["L0_ffn_Ws3"], f)
    ws2 = np.asarray(inputs["L0_ffn_Ws2"], f)
    sp1 = _pack_T(ws1, 1024)
    sp3 = _pack_T(ws3, 1024)
    ws2t = np.ascontiguousarray(ws2.T)
    sp2 = _pack_rows(ws2t[1:], 8)
    sr0 = np.zeros((128, NSH), f)
    sr0[0] = ws2t[0]
    sr0[1] = np.asarray(inputs["L0_ffn_bs2"], f)
    g["wsh"] = np.concatenate([sp1, sp3, sp2, sr0], axis=1)
    g["bsh"] = np.concatenate(
        [_bias_fm(np.asarray(inputs["L0_ffn_bs1"], f), 8),
         _bias_fm(np.asarray(inputs["L0_ffn_bs3"], f), 8)], axis=1)
    assert g["wsh"].shape == (128, WS_COLS), g["wsh"].shape
    import ml_dtypes
    ident = np.eye(128, dtype=f)
    g["identf"] = ident
    g["identb"] = ident.astype(ml_dtypes.bfloat16)
    ut = np.triu(np.ones((128, 128), f))
    g["utf"] = ut
    g["utb"] = ut.astype(ml_dtypes.bfloat16)
    onesb = np.ones((1, 128), f)
    g["onesb"] = onesb.astype(ml_dtypes.bfloat16)
    x = np.asarray(inputs["L0_attn_x"], f)
    return g, x


def build_nc(debug=False, stop_after=None, a3_gate=True, a3_xtspill=True,
             a3_level=9, detect_races=True):
    nc = bass.Bass("TRN2", target_bir_lowering=False,
                   detect_race_conditions=detect_races)
    es = contextlib.ExitStack()
    kb = KB(nc, es)

    dbg_kind = "ExternalOutput" if debug else "Internal"
    d_x = nc.dram_tensor("x", [T, D], F32, kind="ExternalInput").ap()
    d_wq = nc.dram_tensor("wq", [128, 8 * H], F32, kind="ExternalInput").ap()
    d_wk = nc.dram_tensor("wk", [128, 8 * H], F32, kind="ExternalInput").ap()
    d_wv = nc.dram_tensor("wv", [128, 8 * H], F32, kind="ExternalInput").ap()
    d_wo = nc.dram_tensor("wo", [H, D], F32, kind="ExternalInput").ap()
    d_gw = nc.dram_tensor("gw", [128, 8 * 16], F32, kind="ExternalInput").ap()
    d_wexp = nc.dram_tensor("wexp", [NE, 128, WE_COLS], F32, kind="ExternalInput").ap()
    d_wsh = nc.dram_tensor("wsh", [128, WS_COLS], F32, kind="ExternalInput").ap()
    d_bexp = nc.dram_tensor("bexp", [NE, 128, 8], F32, kind="ExternalInput").ap()
    d_bsh = nc.dram_tensor("bsh", [128, 16], F32, kind="ExternalInput").ap()
    d_idf = nc.dram_tensor("identf", [128, 128], F32, kind="ExternalInput").ap()
    d_idb = nc.dram_tensor("identb", [128, 128], BF16, kind="ExternalInput").ap()
    d_utf = nc.dram_tensor("utf", [128, 128], F32, kind="ExternalInput").ap()
    d_utb = nc.dram_tensor("utb", [128, 128], BF16, kind="ExternalInput").ap()
    d_onesb = nc.dram_tensor("onesb", [1, 128], BF16, kind="ExternalInput").ap()
    d_out = nc.dram_tensor("out", [T, D], F32, kind="ExternalOutput").ap()
    d_wec = nc.dram_tensor("wec", [NE, 128, WE_COLS], BF16, kind="Internal").ap()
    d_wshc = nc.dram_tensor("wshc", [128, WS_COLS], BF16, kind="Internal").ap()
    d_x2 = nc.dram_tensor("x2sp", [T, D], F32, kind=dbg_kind).ap()
    d_xt = nc.dram_tensor("xtsp", [8, 128, T], BF16, kind=dbg_kind).ap()
    d_dbg = None
    if debug:
        d_dbg = nc.dram_tensor("dbg", [128, 4096], F32, kind="ExternalOutput").ap()

    chL = [kb.chan("chL0"), kb.chan("chL1")]
    chLg = [kb.chan("chLg0"), kb.chan("chLg1")]
    chCv = [kb.chan("chCv0"), kb.chan("chCv1")]
    chC = kb.chan("chC")
    chX = [kb.chan("chX0"), kb.chan("chX1")]
    chSp = [kb.chan("chSp0"), kb.chan("chSp1")]
    chXt = [kb.chan("chXt0"), kb.chan("chXt1")]
    chXL = [kb.chan("chXL0"), kb.chan("chXL1")]
    chWt = [kb.chan("chWt0"), kb.chan("chWt1")]
    chBe = [kb.chan("chBe0"), kb.chan("chBe1")]
    chOut = [kb.chan("chOut0"), kb.chan("chOut1")]
    chT = kb.chan("chT", "T")
    chV = kb.chan("chV", "V")
    chS = kb.chan("chS", "S")
    chGc = kb.chan("chGc", "G")

    st = nc.sbuf_tensor

    s_x = es.enter_context(st("s_x", [128, 2 * D], F32))
    s_w32 = es.enter_context(st("s_w32", [128, 2 * D], F32))
    s_b16 = es.enter_context(st("s_b16", [128, 2 * D], BF16))
    s_sqs = es.enter_context(st("s_sqs", [128, D], BF16))
    s_wqb = es.enter_context(st("s_wqb", [128, 8 * H], BF16))
    s_wkb = es.enter_context(st("s_wkb", [128, 8 * H], BF16))
    s_wvb = es.enter_context(st("s_wvb", [128, 8 * H], BF16))
    s_wob = es.enter_context(st("s_wob", [H, D], BF16))
    s_gw = es.enter_context(st("s_gw", [128, 8 * 16], F32))
    s_idf = es.enter_context(st("s_idf", [128, 128], F32))
    s_idb = es.enter_context(st("s_idb", [128, 128], BF16))
    s_utf = es.enter_context(st("s_utf", [128, 128], F32))
    s_utb = es.enter_context(st("s_utb", [128, 128], BF16))
    s_onesb = es.enter_context(st("s_onesb", [1, 128], BF16))
    s_wte = es.enter_context(st("s_wte", [128, NT * 16], F32))
    s_hit = es.enter_context(st("s_hit", [128, NT * 16], F32))
    s_y = es.enter_context(st("s_y", [128, (HALF // 128) * D], F32))
    s_c = es.enter_context(st("s_c", [128, 64], F32))
    s_on1 = es.enter_context(st("s_on1", [128, 1], BF16))
    s_cst = es.enter_context(st("s_cst", [128, 3], F32))

    def C(i):
        return s_c[:, i:i + 1]

    c_epsln = s_cst[:, 0:1]
    c_one = s_cst[:, 1:2]
    c_mhalf = s_cst[:, 2:3]
    t_cst = kb.op("G", lambda g: g.memset(s_cst[:, 0:1], EPS_LN), inc=None)
    t_cst = kb.op("G", lambda g: g.memset(s_cst[:, 1:2], 1.0), inc=None)
    t_cst = kb.op("G", lambda g: g.memset(s_cst[:, 2:3], -0.5), inc=chGc)

    def emit_weight_conversions():
        tok = None
        for e in range(NE):
            tok = kb.op("G", lambda gp, e=e: gp.dma_start(
                out=d_wec[e, :, :], in_=d_wexp[e, :, :]), inc=chCv[0], k=16)
        tok = kb.op("G", lambda gp: gp.dma_start(out=d_wshc, in_=d_wsh),
                    inc=chCv[0], k=16)
        return (tok,)

    tok_wcache = emit_weight_conversions()

    def ld(dst, src):
        return kb.op("Y", lambda y: y.dma_start(out=dst, in_=src), inc=chC, k=16)

    for dst, srcc in ((s_idf[:, :], d_idf), (s_idb[:, :], d_idb),
                      (s_utf[:, :], d_utf), (s_utb[:, :], d_utb),
                      (s_onesb[:, :], d_onesb)):
        ld(dst, srcc)
    t_consts = [ld(s_gw[:, :], d_gw)]
    t_aw = []
    for dsrc, dst in ((d_wq, s_wqb), (d_wk, s_wkb), (d_wv, s_wvb)):
        t_aw.append(kb.op("G", lambda gp, s=dsrc, d=dst: gp.dma_start(
            out=d[:, :], in_=s), inc=chCv[1], k=16))
    t_aw.append(kb.op("G", lambda gp: gp.dma_start(
        out=s_wob[:, :], in_=d_wo), inc=chCv[1], k=16))
    t_consts += t_aw[-1:]

    if stop_after == "W":
        kb.op("G", lambda g: g.nop(), after=(t_consts[0],))
        with nc.Block() as block:
            kb.emit(block)
        es.close()
        return nc

    esA = contextlib.ExitStack()
    pt = nc.psum_tensor
    s_qf = esA.enter_context(st("s_qf", [128, T], BF16))
    s_vf = esA.enter_context(st("s_vf", [128, T], BF16))
    s_qt = esA.enter_context(st("s_qt", [128, NT * H], F32))
    s_kt = esA.enter_context(st("s_kt", [128, NT * H], BF16))
    s_vt = esA.enter_context(st("s_vt", [128, NT * H], BF16))
    s_ot = esA.enter_context(st("s_ot", [128, NT * H], BF16))
    s_of = esA.enter_context(st("s_of", [128, T], BF16))
    s_lnf = esA.enter_context(st("s_lnf", [128, 2 * D], BF16))
    s_ph = esA.enter_context(st("s_ph", [128, 832], F32))
    s_S = esA.enter_context(st("s_S", [64, 2 * H], F32))
    s_Sb = esA.enter_context(st("s_Sb", [64, H], BF16))
    s_z = esA.enter_context(st("s_z", [1, 2 * H], F32))
    s_zb = esA.enter_context(st("s_zb", [1, H], BF16))
    s_fm32 = esA.enter_context(st("s_fm32", [128, D], F32))
    s_g16 = esA.enter_context(st("s_g16", [128, 2 * 16], F32))
    s_g16b = esA.enter_context(st("s_g16b", [128, 2 * 16], F32))
    s_gm = esA.enter_context(st("s_gm", [128, 2 * 128], BF16))

    t_on1 = kb.op("G", lambda g: g.memset(s_on1[:, :], 1.0), inc=chGc)
    t_S0 = kb.op("G", lambda g: g.memset(s_S[:, :], 0.0), inc=chGc)
    t_S0b = kb.op("G", lambda g: g.memset(s_Sb[:, :], 0.0), inc=chGc)
    t_z0 = kb.op("G", lambda g: g.memset(s_z[:, :], 0.0), inc=chGc)
    t_z0b = kb.op("G", lambda g: g.memset(s_zb[:, :], 0.0), inc=chGc)

    def ln_stats(src_ap, cb, variant, after_src):
        t1 = kb.op("V", lambda v: v.tensor_reduce(
            out=C(cb), in_=src_ap, axis=AX.X, op=ALU.add),
            after=after_src, inc=chV)
        t2 = kb.op("S", lambda s: s.activation(
            out=s_sqs[:, :], in_=src_ap, func=AF.Square, accum_out=C(cb + 1)),
            after=after_src, inc=chS)
        t3 = kb.op("V", lambda v: v.tensor_scalar(
            out=C(cb + 2), in0=C(cb), scalar1=-1.0 / D, scalar2=None,
            op0=ALU.mult), after=(t1,), inc=chV)
        t4 = kb.op("V", lambda v: v.tensor_tensor(
            out=C(cb + 3), in0=C(cb + 2), in1=C(cb + 2), op=ALU.mult),
            after=(t3,), inc=chV)
        t5 = kb.op("V", lambda v: v.scalar_tensor_tensor(
            out=C(cb + 4), in0=C(cb + 1), scalar=1.0 / D, in1=C(cb + 3),
            op0=ALU.mult, op1=ALU.subtract), after=(t2, t4), inc=chV)
        if variant == "expln":
            t6 = kb.op("S", lambda s: s.activation(
                out=C(cb + 5), in_=C(cb + 4), func=AF.Ln, bias=c_epsln),
                after=(t5, t_cst), inc=chS)
            t7 = kb.op("S", lambda s: s.activation(
                out=C(cb + 6), in_=C(cb + 5), func=AF.Exp, scale=c_mhalf),
                after=(t6,), inc=chS)
            t8 = kb.op("S", lambda s: s.activation(
                out=C(cb + 7), in_=C(cb + 2), func=AF.Identity,
                scale=C(cb + 6)), after=(t7,), inc=chS)
        else:
            t6 = kb.op("S", lambda s: s.activation(
                out=C(cb + 5), in_=C(cb + 4), func=AF.Sqrt, bias=c_epsln),
                after=(t5, t_cst), inc=chS)
            t7 = kb.op("V", lambda v: v.reciprocal(
                out=C(cb + 6), in_=C(cb + 5)), after=(t6,), inc=chV)
            t8 = kb.op("V", lambda v: v.tensor_scalar(
                out=C(cb + 7), in0=C(cb + 2), scalar1=C(cb + 6), scalar2=None,
                op0=ALU.mult), after=(t7,), inc=chV)
        return C(cb + 6), C(cb + 7), (t7, t8)

    def phi(src_ap, dst_ap, w, after, np_=128, base=0):
        ta = kb.op("V", lambda v: v.tensor_scalar(
            out=s_ph[0:np_, base:base + w], in0=src_ap, scalar1=0.0,
            scalar2=None, op0=ALU.min), after=after, inc=chV)
        tb = kb.op("S", lambda s: s.activation(
            out=s_ph[0:np_, base + w:base + 2 * w],
            in_=s_ph[0:np_, base:base + w], func=AF.Exp),
            after=(ta,), inc=chS)
        tc = kb.op("V", lambda v: v.tensor_scalar(
            out=s_ph[0:np_, base + 2 * w:base + 3 * w], in0=src_ap,
            scalar1=0.0, scalar2=None, op0=ALU.max), after=after, inc=chV)
        td = kb.op("V", lambda v: v.tensor_tensor(
            out=dst_ap, in0=s_ph[0:np_, base + w:base + 2 * w],
            in1=s_ph[0:np_, base + 2 * w:base + 3 * w],
            op=ALU.add), after=(tb, tc), inc=chV)
        return td

    esA1 = contextlib.ExitStack()
    p_trA = esA1.enter_context(pt("p_trA", [128, D], BF16))
    p_qkv = esA1.enter_context(pt("p_qkv", [128, 512], F32))
    p_G = esA1.enter_context(pt("p_G", [128, 128], F32))
    p_oc = esA1.enter_context(pt("p_oc", [128, H], F32))
    p_zc = esA1.enter_context(pt("p_zc", [128, H], F32))
    p_z = esA1.enter_context(pt("p_z", [1, H], F32))
    p_S = esA1.enter_context(pt("p_S", [64, H], F32))
    p_of = esA1.enter_context(pt("p_of", [64, 128], BF16))
    g_x = [None, None]
    g_trA = None
    g_qkv = ()
    tok_qf = [None] * NT
    tok_vf = [None] * NT
    tok_qt = [None] * NT
    tok_kt = [None] * NT
    tok_vt = [None] * NT
    g_lnf = [None, None]
    a1_tr_last = None

    def a1_tile(i):
        nonlocal g_trA, g_qkv, a1_tr_last
        p = i % 2
        cb = 32 * p
        xsl = s_x[:, p * D:(p + 1) * D]
        tx = kb.op("Y", lambda y, i=i, xsl=xsl: y.dma_start(
            out=xsl, in_=d_x[i * 128:(i + 1) * 128, :]),
            after=(g_x[p],) + (tuple(t_consts) if i < 2 else ()),
            inc=chX[p], k=16)
        rstd, mb, tln = ln_stats(xsl, cb, "expln", (tx,))
        lnsl = s_b16[:, p * D:(p + 1) * D]
        tln1 = kb.op("S", lambda s, xsl=xsl, lnsl=lnsl, rstd=rstd, mb=mb:
                     s.activation(out=lnsl, in_=xsl, func=AF.Identity,
                                  scale=rstd, bias=mb),
                     after=tln + (g_lnf[p],), inc=chS)
        g_x[p] = tln1
        ttr = None
        for k in range(8):
            ttr = kb.op("T", lambda t, k=k, lnsl=lnsl: t.transpose(
                p_trA[:, k * 128:(k + 1) * 128], lnsl[:, k * 128:(k + 1) * 128],
                s_idb[:, :]),
                after=(tln1, g_trA) if k == 0 else (),
                inc=chT if k == 7 else None)
        a1_tr_last = ttr
        fmsl = s_lnf[:, p * D:(p + 1) * D]
        tfm = kb.op("V", lambda v, fmsl=fmsl: v.tensor_copy(
            out=fmsl, in_=p_trA[:, :]), after=(ttr,), inc=chV)
        g_trA = tfm
        tmm = None
        forms = [
            lambda t, k, fmsl: t.matmul(
                p_qkv[0:64, 0:128], s_wqb[:, k * H:(k + 1) * H],
                fmsl[:, k * 128:(k + 1) * 128], start=(k == 0), stop=(k == 7)),
            lambda t, k, fmsl: t.matmul(
                p_qkv[0:64, 128:256], s_wvb[:, k * H:(k + 1) * H],
                fmsl[:, k * 128:(k + 1) * 128], start=(k == 0), stop=(k == 7)),
            lambda t, k, fmsl: t.matmul(
                p_qkv[:, 256:320], fmsl[:, k * 128:(k + 1) * 128],
                s_wqb[:, k * H:(k + 1) * H], start=(k == 0), stop=(k == 7)),
            lambda t, k, fmsl: t.matmul(
                p_qkv[:, 320:384], fmsl[:, k * 128:(k + 1) * 128],
                s_wkb[:, k * H:(k + 1) * H], start=(k == 0), stop=(k == 7)),
            lambda t, k, fmsl: t.matmul(
                p_qkv[:, 384:448], fmsl[:, k * 128:(k + 1) * 128],
                s_wvb[:, k * H:(k + 1) * H], start=(k == 0), stop=(k == 7)),
        ]
        for fi, fn in enumerate(forms):
            for k in range(8):
                af = (tfm,) + g_qkv if (fi == 0 and k == 0) else ()
                tmm = kb.op("T",
                            lambda t, fn=fn, k=k, fmsl=fmsl: fn(t, k, fmsl),
                            after=af,
                            inc=chT if (fi == 4 and k == 7) else None)
        g_lnf[p] = tmm
        tok_vf[i] = kb.op("V", lambda v, i=i: v.tensor_copy(
            out=s_vf[0:64, i * 128:(i + 1) * 128], in_=p_qkv[0:64, 128:256]),
            after=(tmm,), inc=chV)
        tok_vt[i] = kb.op("V", lambda v, i=i: v.tensor_copy(
            out=s_vt[:, i * H:(i + 1) * H], in_=p_qkv[:, 384:448]),
            after=(tmm,), inc=chV)
        specs = [
            (p_qkv[0:64, 0:128], s_qf[0:64, i * 128:(i + 1) * 128], 128, 64, 0),
            (p_qkv[:, 256:320], s_qt[:, i * H:(i + 1) * H], 64, 128, 384),
            (p_qkv[:, 320:384], s_kt[:, i * H:(i + 1) * H], 64, 128, 576),
        ]
        tas, tcs, tbs, tds = [], [], [], []
        for src, _, w, np_, base in specs:
            tas.append(kb.op("V", lambda v, src=src, w=w, np_=np_, base=base:
                             v.tensor_scalar(
                out=s_ph[0:np_, base:base + w], in0=src, scalar1=0.0,
                scalar2=None, op0=ALU.min), after=(tmm,), inc=chV))
            tcs.append(kb.op("V", lambda v, src=src, w=w, np_=np_, base=base:
                             v.tensor_scalar(
                out=s_ph[0:np_, base + 2 * w:base + 3 * w], in0=src,
                scalar1=0.0, scalar2=None, op0=ALU.max), after=(tmm,),
                inc=chV))
        for j, (_, _, w, np_, base) in enumerate(specs):
            tbs.append(kb.op("S", lambda s, w=w, np_=np_, base=base:
                             s.activation(
                out=s_ph[0:np_, base + w:base + 2 * w],
                in_=s_ph[0:np_, base:base + w], func=AF.Exp),
                after=(tas[j],), inc=chS))
        for j, (_, dst, w, np_, base) in enumerate(specs):
            tds.append(kb.op("V", lambda v, dst=dst, w=w, np_=np_, base=base:
                             v.tensor_tensor(
                out=dst, in0=s_ph[0:np_, base + w:base + 2 * w],
                in1=s_ph[0:np_, base + 2 * w:base + 3 * w],
                op=ALU.add), after=(tbs[j], tcs[j]), inc=chV))
        tok_qf[i], tok_qt[i], tok_kt[i] = tds
        g_qkv = (tok_vf[i], tok_vt[i], tok_qf[i], tok_qt[i], tok_kt[i])

    g_G = None
    g_gm = [None, None]
    g_oc = None
    g_zc = None
    g_of = None
    t_Scp = t_S0b
    t_zcp = t_z0b
    t_dSmm = None
    t_zsum = None
    tok_o = [None] * NT

    def a2_chunk(c):
        nonlocal g_G, g_oc, g_zc, g_of, t_Scp, t_zcp, t_dSmm, t_zsum
        p = c % 2
        cbase = 40 + 4 * p
        tG = kb.op("T", lambda t, c=c: t.matmul(
            p_G[:, :], s_vf[0:64, c * 128:(c + 1) * 128],
            s_qf[0:64, c * 128:(c + 1) * 128], start=True, stop=True),
            after=(tok_vf[c], tok_qf[c], g_G), inc=chT)
        gm = s_gm[:, p * 128:(p + 1) * 128]
        tGm = kb.op("V", lambda v, gm=gm: v.tensor_tensor(
            out=gm, in0=p_G[:, :], in1=s_utf[:, :], op=ALU.mult),
            after=(tG, g_gm[p]), inc=chV)
        g_G = tGm
        to1 = kb.op("T", lambda t, c=c, gm=gm: t.matmul(
            p_oc[:, :], gm, s_kt[:, c * H:(c + 1) * H], start=True, stop=False),
            after=(tGm, tok_kt[c], g_oc), inc=None)
        to2 = kb.op("T", lambda t, c=c: t.matmul(
            p_oc[:, :], s_qf[0:64, c * 128:(c + 1) * 128], s_Sb[:, :],
            start=False, stop=True), after=(t_Scp,), inc=chT)
        g_gm[p] = to2
        tz1 = kb.op("T", lambda t, c=c: t.matmul(
            p_zc[:, :], s_utb[:, :], s_kt[:, c * H:(c + 1) * H],
            start=True, stop=False), after=(g_zc, tok_kt[c]), inc=None)
        tz2 = kb.op("T", lambda t: t.matmul(
            p_zc[:, :], s_onesb[0:1, :], s_zb[0:1, :],
            start=False, stop=True), after=(t_zcp, t_on1), inc=chT)
        t_zsum = kb.op("T", lambda t, c=c: t.matmul(
            p_z[0:1, :], s_on1[:, :], s_kt[:, c * H:(c + 1) * H],
            start=True, stop=True),
            after=(t_zcp,), inc=chT)
        t_dSmm = kb.op("T", lambda t, c=c: t.matmul(
            p_S[:, :], s_vt[:, c * H:(c + 1) * H], s_kt[:, c * H:(c + 1) * H],
            start=True, stop=True),
            after=(tok_vt[c], t_Scp, to2), inc=chT)
        td1 = kb.op("V", lambda v, c=c: v.tensor_tensor(
            out=s_ph[:, 768:832], in0=p_zc[:, :],
            in1=s_qt[:, c * H:(c + 1) * H],
            op=ALU.mult), after=(tz2, tok_qt[c]), inc=chV)
        g_zc = td1
        td2 = kb.op("V", lambda v, cb=cbase: v.tensor_reduce(
            out=C(cb), in_=s_ph[:, 768:832], axis=AX.X, op=ALU.add),
            after=(td1,), inc=chV)
        td3 = kb.op("V", lambda v, cb=cbase: v.tensor_scalar(
            out=C(cb + 1), in0=C(cb), scalar1=1e-6, scalar2=None,
            op0=ALU.max), after=(td2,), inc=chV)
        td4 = kb.op("V", lambda v, cb=cbase: v.reciprocal(
            out=C(cb + 2), in_=C(cb + 1)), after=(td3,), inc=chV)
        to3 = kb.op("V", lambda v, c=c, cb=cbase: v.tensor_scalar(
            out=s_ot[:, c * H:(c + 1) * H], in0=p_oc[:, :],
            scalar1=C(cb + 2), scalar2=None, op0=ALU.mult),
            after=(to2, td4), inc=chV)
        g_oc = to3
        pc, pn = c % 2, (c + 1) % 2
        tSa = kb.op("V", lambda v, pc=pc, pn=pn: v.tensor_tensor(
            out=s_S[:, pn * H:(pn + 1) * H], in0=p_S[:, :],
            in1=s_S[:, pc * H:(pc + 1) * H], op=ALU.add),
            after=(t_dSmm, t_S0, t_Scp), inc=chV)
        t_Scp = kb.op("V", lambda v, pn=pn: v.tensor_copy(
            out=s_Sb[:, :], in_=s_S[:, pn * H:(pn + 1) * H]),
            after=(tSa, t_S0b), inc=chV)
        tza = kb.op("V", lambda v, pc=pc, pn=pn: v.tensor_tensor(
            out=s_z[0:1, pn * H:(pn + 1) * H], in0=p_z[0:1, :],
            in1=s_z[0:1, pc * H:(pc + 1) * H], op=ALU.add),
            after=(t_zsum, tz2, t_z0, t_zcp), inc=chV)
        t_zcp = kb.op("V", lambda v, pn=pn: v.tensor_copy(
            out=s_zb[0:1, :], in_=s_z[0:1, pn * H:(pn + 1) * H]),
            after=(tza, t_z0b), inc=chV)
        tof = kb.op("T", lambda t, c=c: t.transpose(
            p_of[:, :], s_ot[:, c * H:(c + 1) * H], s_idb[:, :]),
            after=(to3, g_of), inc=chT)
        tok_o[c] = kb.op("V", lambda v, c=c: v.tensor_copy(
            out=s_of[0:64, c * 128:(c + 1) * 128], in_=p_of[:, :]),
            after=(tof,), inc=chV)
        g_of = tok_o[c]

    for i in range(NT + 1):
        if i < NT:
            a1_tile(i)
        if i >= 1:
            a2_chunk(i - 1)
    esA1.close()

    esA3 = contextlib.ExitStack()
    p_at = [esA3.enter_context(pt("p_at0", [128, D], F32)),
            esA3.enter_context(pt("p_at1", [128, D], F32))]
    p_tr2 = esA3.enter_context(pt("p_tr2", [128, D], F32))
    p_gate = esA3.enter_context(pt("p_gate", [128, 16], F32))
    s_at = esA3.enter_context(st("s_at", [128, 2 * D], F32))
    s_glog = esA3.enter_context(st("s_glog", [128, 2 * 16], F32))
    g_at = [None, None]
    g_atS = [None, None]
    g_x = [g_x[0], g_x[1]]
    g_tr2 = None
    g_fm32 = None
    g_gate = None
    g_b16 = [g_lnf[0], g_lnf[1]]
    tok_x2sp = [None] * NT
    tok_xtsp = [None] * NT
    g_w32 = [None, None]
    for i in range(NT):
        p = i % 2
        cb = 32 * p
        atp = p_at[p]
        t1 = kb.op("T", lambda t, i=i, atp=atp: t.matmul(
            atp[:, 0:512], s_of[0:64, i * 128:(i + 1) * 128], s_wob[:, 0:512],
            start=True, stop=True),
            after=(tok_o[i], g_at[p]) + ((g_of, g_oc) if i < 2 else ()),
            inc=None)
        t2 = kb.op("T", lambda t, i=i, atp=atp: t.matmul(
            atp[:, 512:1024], s_of[0:64, i * 128:(i + 1) * 128],
            s_wob[:, 512:1024], start=True, stop=True), after=(), inc=chT)
        if a3_level < 2:
            tev = kb.op("V", lambda v, atp=atp, p=p: v.tensor_copy(
                out=s_w32[:, p * D:(p + 1) * D], in_=atp[:, :]),
                after=(t2, g_at[p], g_w32[p]), inc=chV)
            g_at[p] = tev
            g_w32[p] = tev
            tok_x2sp[i] = tok_xtsp[i] = tw2 = tev
            continue
        atsl = s_at[:, p * D:(p + 1) * D]
        tat = kb.op("V", lambda v, atp=atp, atsl=atsl: v.tensor_copy(
            out=atsl, in_=atp[:, :]), after=(t2, g_at[p], g_atS[p]), inc=chV)
        g_at[p] = tat
        if 11 <= a3_level <= 15:
            toks = []
            t1 = kb.op("V", lambda v, atsl=atsl, cb=cb: v.tensor_reduce(
                out=C(cb), in_=atsl, axis=AX.X, op=ALU.add),
                after=(tat,), inc=chV)
            toks.append(t1)
            if a3_level >= 12:
                t2s = kb.op("S", lambda s, atsl=atsl, cb=cb: s.activation(
                    out=s_sqs[:, :], in_=atsl, func=AF.Square,
                    accum_out=C(cb + 1)), after=(tat,), inc=chS)
                toks.append(t2s)
            if a3_level >= 13:
                t3 = kb.op("V", lambda v, cb=cb: v.tensor_scalar(
                    out=C(cb + 2), in0=C(cb), scalar1=1.0 / D, scalar2=None,
                    op0=ALU.mult), after=(t1,), inc=chV)
                t4 = kb.op("V", lambda v, cb=cb: v.tensor_tensor(
                    out=C(cb + 3), in0=C(cb + 2), in1=C(cb + 2), op=ALU.mult),
                    after=(t3,), inc=chV)
                t5 = kb.op("V", lambda v, cb=cb: v.scalar_tensor_tensor(
                    out=C(cb + 4), in0=C(cb + 1), scalar=1.0 / D,
                    in1=C(cb + 3), op0=ALU.mult, op1=ALU.subtract),
                    after=(t2s, t4), inc=chV)
                t8 = kb.op("V", lambda v, cb=cb: v.tensor_scalar(
                    out=C(cb + 7), in0=C(cb + 2), scalar1=C(cb + 4),
                    scalar2=-1.0, op0=ALU.mult, op1=ALU.mult),
                    after=(t5,), inc=chV)
                toks += [t8]
            if a3_level >= 14:
                t6 = kb.op("S", lambda s, cb=cb: s.activation(
                    out=C(cb + 5), in_=C(cb + 4), func=AF.Ln, bias=c_epsln),
                    after=(t5, t_cst), inc=chS)
                t7 = kb.op("S", lambda s, cb=cb: s.activation(
                    out=C(cb + 6), in_=C(cb + 5), func=AF.Exp, scale=c_mhalf),
                    after=(t6,), inc=chS)
                toks += [t7]
            tev = kb.op("V", lambda v, atsl=atsl, p=p: v.tensor_copy(
                out=s_w32[:, p * D:(p + 1) * D], in_=atsl),
                after=tuple(toks) + (g_w32[p],), inc=chV)
            g_atS[p] = tev
            g_w32[p] = tev
            tok_x2sp[i] = tok_xtsp[i] = tw2 = tev
            continue
        rstd, mb, tln = ln_stats(atsl, cb, "expln", (tat,))
        xsl = s_x[:, p * D:(p + 1) * D]
        tx = kb.op("Y", lambda y, i=i, xsl=xsl: y.dma_start(
            out=xsl, in_=d_x[i * 128:(i + 1) * 128, :]),
            after=(g_x[p],), inc=chX[p], k=16)
        wsl = s_w32[:, p * D:(p + 1) * D]
        txf = kb.op("S", lambda s, xsl=xsl, wsl=wsl, mb=mb: s.activation(
            out=wsl, in_=xsl, func=AF.Identity, bias=mb),
            after=(tx, tln[1], g_w32[p]), inc=chS)
        tx2 = kb.op("V", lambda v, atsl=atsl, xsl=xsl, wsl=wsl, rstd=rstd:
                    v.scalar_tensor_tensor(
                        out=xsl, in0=atsl, scalar=rstd, in1=wsl,
                        op0=ALU.mult, op1=ALU.add),
                    after=(txf, tln[0]), inc=chV)
        g_atS[p] = tx2
        tok_x2sp[i] = kb.op("Y", lambda y, i=i, xsl=xsl: y.dma_start(
            out=d_x2[i * 128:(i + 1) * 128, :], in_=xsl),
            after=(tx2,), inc=chSp[p], k=16)
        if a3_level < 3:
            tok_xtsp[i] = tw2 = tok_x2sp[i]
            g_x[p] = tok_x2sp[i]
            continue
        rstd2, mb2, tln2 = ln_stats(xsl, cb + 8, "expln", (tx2,))
        tln2f = kb.op("S", lambda s, xsl=xsl, wsl=wsl, rstd2=rstd2, mb2=mb2:
                      s.activation(out=wsl, in_=xsl, func=AF.Identity,
                                   scale=rstd2, bias=mb2),
                      after=tln2 + (txf,), inc=chS)
        g_x[p] = tok_x2sp[i]
        if a3_level < 4:
            tok_xtsp[i] = tw2 = tln2f
            g_w32[p] = tln2f
            continue
        ttr = None
        for k in range(8):
            ttr = kb.op("T", lambda t, k=k, wsl=wsl: t.transpose(
                p_tr2[:, k * 128:(k + 1) * 128], wsl[:, k * 128:(k + 1) * 128],
                s_idf[:, :]),
                after=(tln2f, g_tr2) if k == 0 else (),
                inc=chT if k == 7 else None)
        g_w32[p] = ttr
        tfm32 = kb.op("V", lambda v: v.tensor_copy(
            out=s_fm32[:, :], in_=p_tr2[:, :]), after=(ttr, g_fm32), inc=chV)
        bsl = s_b16[:, p * D:(p + 1) * D]
        txtb = kb.op("V", lambda v, bsl=bsl: v.tensor_copy(
            out=bsl, in_=p_tr2[:, :]),
            after=(ttr, g_b16[p], tfm32), inc=chV)
        g_tr2 = txtb
        if a3_xtspill:
            tok_xtsp[i] = kb.op("Y", lambda y, i=i, bsl=bsl: y.dma_start(
                out=d_xt[:, :, i * 128:(i + 1) * 128].rearrange("k p n -> p k n"),
                in_=bsl.rearrange("p (k n) -> p k n", k=8)),
                after=(txtb,), inc=chXt[p], k=16)
        else:
            tok_xtsp[i] = txtb
        g_b16[p] = tok_xtsp[i]
        if not a3_gate:
            tw2 = txtb
            continue
        tg = None
        for k in range(8):
            tg = kb.op("T", lambda t, k=k: t.matmul(
                p_gate[:, :], s_fm32[:, k * 128:(k + 1) * 128],
                s_gw[:, k * 16:(k + 1) * 16], start=(k == 0), stop=(k == 7)),
                after=(tfm32, g_gate) if k == 0 else (),
                inc=chT if k == 7 else None)
        g_fm32 = tg
        gsl = s_glog[:, p * 16:(p + 1) * 16]
        tgc = kb.op("V", lambda v, gsl=gsl: v.tensor_copy(
            out=gsl, in_=p_gate[:, :]), after=(tg, g_gate), inc=chV)
        tmx = kb.op("V", lambda v, cb=cb, gsl=gsl: v.reduce_max(
            out=C(cb + 16), in_=gsl, axis=AX.X), after=(tgc,), inc=chV)
        tnm = kb.op("V", lambda v, cb=cb: v.tensor_scalar(
            out=C(cb + 17), in0=C(cb + 16), scalar1=-1.0, scalar2=None,
            op0=ALU.mult), after=(tmx,), inc=chV)
        esl = s_g16[:, p * 16:(p + 1) * 16]
        bsl2 = s_g16b[:, p * 16:(p + 1) * 16]
        te = kb.op("S", lambda s, cb=cb, esl=esl, gsl=gsl: s.activation(
            out=esl, in_=gsl, func=AF.Exp, bias=C(cb + 17)),
            after=(tnm,), inc=chS)
        g_gate = tgc
        tse = kb.op("V", lambda v, cb=cb, esl=esl: v.tensor_reduce(
            out=C(cb + 18), in_=esl, axis=AX.X, op=ALU.add),
            after=(te,), inc=chV)
        trs = kb.op("V", lambda v, cb=cb: v.reciprocal(
            out=C(cb + 19), in_=C(cb + 18)), after=(tse,), inc=chV)
        tm1 = kb.op("V", lambda v, cb=cb, esl=esl: v.reduce_max(
            out=C(cb + 20), in_=esl, axis=AX.X), after=(te,), inc=chV)
        teq = kb.op("V", lambda v, cb=cb, esl=esl, bsl2=bsl2: v.tensor_scalar(
            out=bsl2, in0=esl, scalar1=C(cb + 20), scalar2=0.0,
            op0=ALU.subtract, op1=ALU.is_ge), after=(tm1,), inc=chV)
        tmk = kb.op("V", lambda v, esl=esl, bsl2=bsl2: v.scalar_tensor_tensor(
            out=bsl2, in0=bsl2, scalar=-1e30, in1=esl,
            op0=ALU.mult, op1=ALU.add), after=(teq,), inc=chV)
        tm2 = kb.op("V", lambda v, cb=cb, bsl2=bsl2: v.reduce_max(
            out=C(cb + 21), in_=bsl2, axis=AX.X), after=(tmk,), inc=chV)
        th = kb.op("V", lambda v, cb=cb, esl=esl, i=i: v.tensor_scalar(
            out=s_hit[:, i * 16:(i + 1) * 16], in0=esl, scalar1=C(cb + 21),
            scalar2=None, op0=ALU.is_ge), after=(tm2,), inc=chV)
        tw1 = kb.op("V", lambda v, cb=cb, esl=esl, bsl2=bsl2: v.tensor_scalar(
            out=bsl2, in0=esl, scalar1=C(cb + 19), scalar2=None,
            op0=ALU.mult), after=(trs, tm2), inc=chV)
        tw2 = kb.op("V", lambda v, i=i, bsl2=bsl2: v.tensor_tensor(
            out=s_wte[:, i * 16:(i + 1) * 16], in0=bsl2,
            in1=s_hit[:, i * 16:(i + 1) * 16], op=ALU.mult),
            after=(th, tw1), inc=chV)
    esA3.close()
    esA.close()
    a_done = (tok_x2sp[NT - 2], tok_x2sp[NT - 1], tok_xtsp[NT - 2],
              tok_xtsp[NT - 1], tw2) + tok_wcache

    if stop_after == "A":
        kb.op("G", lambda g: g.nop(), after=a_done)
        with nc.Block() as block:
            kb.emit(block)
        es.close()
        return nc

    esF = contextlib.ExitStack()
    s_we = esF.enter_context(st("s_we", [128, 2 * WE_COLS], BF16))
    s_beb = esF.enter_context(st("s_beb", [128, 2 * 8], F32))
    s_xtb = esF.enter_context(st("s_xtb", [128, 2 * 8 * GRP], BF16))
    s_sf = esF.enter_context(st("s_sf", [128, NG * 1024], BF16))
    s_a = esF.enter_context(st("s_a", [128, 2048], BF16))
    s_s3b = esF.enter_context(st("s_s3b", [128, 2048], BF16))
    s_sq = esF.enter_context(st("s_sq", [128, 2 * 2048], BF16))
    s_sums = esF.enter_context(st("s_sums", [1, NG * GRP], F32))
    s_timeb = esF.enter_context(st("s_timeb", [2, NG * GRP], BF16))
    s_onef = esF.enter_context(st("s_onef", [128, 1], F32))
    s_s1f = esF.enter_context(st("s_s1f", [128, 1024], F32))
    s_wshb = None

    t_onef = kb.op("G", lambda g: g.memset(s_onef[:, :], 1.0),
                   after=a_done, inc=chGc)
    t_ones_time = kb.op("G", lambda g: g.memset(s_timeb[:, :], 1.0),
                        after=a_done, inc=chGc)

    g_y = [None] * (HALF // 128)
    g_we = [None, None]
    g_xt = [None, None]
    g_zz = [None, None]
    g_o2 = [(None, None)] * 4
    g_sfgrp = [None] * NG
    g_time = None
    g_beb = [None, None]
    g_ps1 = [None, None]
    g_ps3 = [None, None]
    g_s1f = [None]
    g_bscr = [None, None]
    g_sq = [None, None]

    def emit_wload(pe, we_cols, d_wsrc, d_bsrc, first=False):
        wsl = s_we[:, pe * WE_COLS: pe * WE_COLS + we_cols]
        w_after = ((g_we[0], g_we[1]) if we_cols > WE_COLS else (g_we[pe],))
        tw = kb.op("Y", lambda y: y.dma_start(out=wsl, in_=d_wsrc),
                   after=w_after + (a_done if first else ()),
                   inc=chWt[pe], k=16)
        if d_bsrc is not None:
            bsl = s_beb[:, pe * 8:(pe + 1) * 8]
            tb = kb.op("Y", lambda y, bsl=bsl: y.dma_start(out=bsl, in_=d_bsrc),
                       after=(g_beb[pe],), inc=chBe[pe], k=16)
        else:
            bsl = s_beb[:, 0:16]
            tb = kb.op("Y", lambda y, bsl=bsl: y.dma_start(out=bsl, in_=d_bsh),
                       after=(g_beb[0], g_beb[1]), inc=chBe[0], k=16)
        kb.wcache[(pe, we_cols)] = (tw, tb)
    out_toks = []

    def ffn_expert(h, e, pe, first, ps1_of, ps3_of, p_o2, we_cols, w1o, w3o,
                   w2o, r0o, nit, nkt, d_wsrc, d_bsrc, stride1, wte_e,
                   groups=tuple(range(NG)), sfs=1024, skip_w=False,
                   s1_dbuf=True, prefetch=None):
        nonlocal g_time
        HB = h * HALF
        g0 = groups[0]
        wsl = s_we[:, pe * WE_COLS: pe * WE_COLS + we_cols]
        if not skip_w:
            emit_wload(pe, we_cols, d_wsrc, d_bsrc, first)
        bsl = s_beb[:, 0:16] if d_bsrc is None else s_beb[:, pe * 8:(pe + 1) * 8]
        tw, tb = kb.wcache[(pe, we_cols)]
        pending_sum = None
        last_smul = None
        t_silu = [None] * NG
        t_s3b = [None] * NG
        t_smul = [None] * NG
        t_sumcp = [None] * NG
        for g in groups:
            pg = g % 2
            pidx = (g % 2) if s1_dbuf else 0
            s1r = ps1_of(g)
            s3r = ps3_of(g)
            sfo = (g - g0) * sfs
            xsl = s_xtb[:, pg * 8 * GRP:(pg + 1) * 8 * GRP]
            txt = kb.op("Y", lambda y, g=g, xsl=xsl, HB=HB: y.dma_start(
                out=xsl,
                in_=d_xt[:, :, HB + g * GRP:HB + (g + 1) * GRP]
                .rearrange("k p n -> p k n")),
                after=(g_xt[pg],) + (a_done if first and g < 2 else ()),
                inc=chXL[pg], k=16)
            if g == groups[0] and prefetch is not None:
                prefetch()
            tmm1 = tmm3 = None
            for it in range(nit):
                for k in range(8):
                    af = ((tw, txt) + tuple(g_ps1[pidx] or ())
                          if (it == 0 and k == 0) else ())
                    tmm1 = kb.op("T", lambda t, it=it, k=k, xsl=xsl, s1r=s1r:
                                 t.matmul(
                        s1r[:, it * GRP:(it + 1) * GRP],
                        wsl[:, w1o + k * stride1 + it * 128:
                            w1o + k * stride1 + it * 128 + 128],
                        xsl[:, k * GRP:(k + 1) * GRP],
                        start=(k == 0), stop=(k == 7)), after=af,
                        inc=chT if (it == nit - 1 and k == 7) else None)
            for it in range(nit):
                for k in range(8):
                    af = (g_ps3[pidx],) if (it == 0 and k == 0) else ()
                    tmm3 = kb.op("T", lambda t, it=it, k=k, xsl=xsl, s3r=s3r:
                                 t.matmul(
                        s3r[:, it * GRP:(it + 1) * GRP],
                        wsl[:, w3o + k * stride1 + it * 128:
                            w3o + k * stride1 + it * 128 + 128],
                        xsl[:, k * GRP:(k + 1) * GRP],
                        start=(k == 0), stop=(k == 7)), after=af,
                        inc=chT if (it == nit - 1 and k == 7) else None)
            g_xt[pg] = tmm3
            for it0 in range(0, nit, 4):
                nb = min(4, nit - it0)
                tcp1 = kb.op("V", lambda v, it0=it0, nb=nb, s1r=s1r:
                             v.tensor_copy(
                    out=s_s1f[:, 0:nb * GRP],
                    in_=s1r[:, it0 * GRP:(it0 + nb) * GRP]),
                    after=(tmm1, g_s1f[0]), inc=chV)
                for it in range(it0, it0 + nb):
                    t_silu[g] = kb.op("S", lambda s, it=it, it0=it0:
                                      s.activation(
                        out=s_a[:, it * GRP:(it + 1) * GRP],
                        in_=s_s1f[:, (it - it0) * GRP:(it - it0 + 1) * GRP],
                        func=AF.Silu, bias=bsl[:, it:it + 1]),
                        after=(tcp1, tb) if it == it0 else (), inc=chS)
                g_s1f[0] = t_silu[g]
            g_ps1[pidx] = (tcp1,)
            for it in range(nit):
                t_s3b[g] = kb.op("V", lambda v, it=it, nit=nit, s3r=s3r:
                                 v.tensor_scalar(
                    out=s_s3b[:, it * GRP:(it + 1) * GRP],
                    in0=s3r[:, it * GRP:(it + 1) * GRP],
                    scalar1=bsl[:, nit + it:nit + it + 1], scalar2=None,
                    op0=ALU.add),
                    after=(tmm3, tb, last_smul) if it == 0 else (), inc=chV)
            sf_guard = tuple(g_sfgrp) if sfs != 1024 else (g_sfgrp[g],)
            t_smul[g] = kb.op("V", lambda v, sfo=sfo, nit=nit: v.tensor_tensor(
                out=s_sf[:, sfo:sfo + nit * GRP],
                in0=s_a[:, 0:nit * GRP], in1=s_s3b[:, 0:nit * GRP],
                op=ALU.mult),
                after=(t_silu[g], t_s3b[g]) + sf_guard, inc=chV)
            tsq = kb.op("S", lambda s, sfo=sfo, nit=nit, pg=pg: s.activation(
                out=s_sq[:, pg * 2048:pg * 2048 + nit * GRP],
                in_=s_sf[:, sfo:sfo + nit * GRP], func=AF.Square),
                after=(t_smul[g], g_sq[pg]), inc=chS)
            def emit_sum(g=g, s3r=s3r, tsq=tsq, ts3b=t_s3b[g], pidx=pidx):
                tsum = None
                for it in range(nit):
                    tsum = kb.op("T", lambda t, it=it, s3r=s3r, g=g: t.matmul(
                        s3r[0:1, 0:GRP], s_on1[:, :],
                        s_sq[:, (g % 2) * 2048 + it * GRP:
                             (g % 2) * 2048 + (it + 1) * GRP],
                        start=(it == 0), stop=(it == nit - 1)),
                        after=(tsq, ts3b) if it == 0 else (),
                        inc=chT if it == nit - 1 else None)
                g_sq[g % 2] = tsum
                t_sumcp[g] = kb.op("V", lambda v, g=g, s3r=s3r: v.tensor_copy(
                    out=s_sums[0:1, g * GRP:(g + 1) * GRP],
                    in_=s3r[0:1, 0:GRP]),
                    after=(tsum, g_time), inc=chV)
                g_ps3[pidx] = t_sumcp[g]

            last_smul = t_smul[g]
            if not s1_dbuf:
                emit_sum()
            else:
                if pending_sum is not None:
                    pending_sum()
                pending_sum = emit_sum
        if pending_sum is not None:
            pending_sum()
            pending_sum = None
        glast = groups[-1]
        tq = kb.op("S", lambda s: s.activation(
            out=s_timeb[0:1, g0 * GRP:(glast + 1) * GRP],
            in_=s_sums[0:1, g0 * GRP:(glast + 1) * GRP],
            func=AF.Sqrt, bias=c_one[0:1, :]),
            after=(t_sumcp[glast], t_ones_time, g_time, t_cst), inc=chS)
        t_last_k2 = None
        t_last_w2 = None
        for g in groups:
            sfo = (g - g0) * sfs
            for tt in range(2):
                lt = g * 2 + tt
                gi = h * (HALF // 128) + lt
                po = lt % 4
                cb = 32 * (lt % 2)
                pob = p_o2[po]
                tmm = None
                for kt in range(nkt):
                    for nh in range(2):
                        wnh = 512 if nh == 0 else 511
                        af = ()
                        if kt == 0 and nh == 0:
                            af = (t_smul[g], g_o2[po][0], g_o2[po][1],
                                  t_sumcp[glast])
                        tmm = kb.op("T", lambda t, sfo=sfo, tt=tt, kt=kt, nh=nh,
                                    wnh=wnh, pob=pob: t.matmul(
                            pob(nh * 512, nh * 512 + wnh),
                            s_sf[:, sfo + kt * GRP + tt * 128:
                                 sfo + kt * GRP + tt * 128 + 128],
                            wsl[:, w2o + kt * NSH + nh * 512:
                                w2o + kt * NSH + nh * 512 + wnh],
                            start=(kt == 0), stop=False), after=af)
                for nh in range(2):
                    wnh = 512 if nh == 0 else 511
                    tmm = kb.op("T", lambda t, g=g, tt=tt, nh=nh, wnh=wnh,
                                pob=pob: t.matmul(
                        pob(nh * 512, nh * 512 + wnh),
                        s_timeb[0:2, g * GRP + tt * 128:g * GRP + tt * 128 + 128],
                        wsl[0:2, r0o + nh * 512:r0o + nh * 512 + wnh],
                        start=False, stop=True),
                        after=(tq,) if nh == 0 else (),
                        inc=chT if nh == 1 else None)
                t_last_k2 = tmm
                g_sfgrp[g] = tmm
                t_last_w2 = tmm
                pz = lt % 2
                zsl = s_w32[:, pz * D:(pz + 1) * D]
                ysl = s_y[:, lt * D:(lt + 1) * D]
                bscr = s_b16[:, pz * D:pz * D + NSH]
                tob = kb.op("V", lambda v, pob=pob, bscr=bscr:
                            v.tensor_copy(out=bscr, in_=pob(0, NSH)),
                            after=(tmm, g_bscr[pz]), inc=chV)
                tq1 = kb.op("S", lambda s, bscr=bscr, cb=cb:
                            s.activation(out=s_sqs[:, 0:NSH], in_=bscr,
                                         func=AF.Square, accum_out=C(cb)),
                            after=(tob,), inc=chS)
                g_bscr[pz] = tq1
                tt0 = kb.op("S", lambda s, cb=cb: s.activation(
                    out=C(cb + 1), in_=C(cb), func=AF.Sqrt, bias=c_one),
                    after=(tq1, t_cst), inc=chS)
                if wte_e is not None:
                    wcol = s_wte[:, gi * 16 + wte_e:gi * 16 + wte_e + 1]
                    hcol = s_hit[:, gi * 16 + wte_e:gi * 16 + wte_e + 1]
                else:
                    wcol = hcol = None
                tzz = kb.op("V", lambda v, pob=pob, zsl=zsl, ysl=ysl,
                            wcol=wcol: v.scalar_tensor_tensor(
                    out=zsl[:, 1:D], in0=pob(0, NSH),
                    scalar=(wcol if wcol is not None else 1.0),
                    in1=ysl[:, 1:D], op0=ALU.mult, op1=ALU.add),
                    after=(tmm, g_zz[pz], g_y[lt]),
                    inc=chV)
                tz0 = kb.op("V", lambda v, zsl=zsl, ysl=ysl, cb=cb,
                            wcol=wcol: v.scalar_tensor_tensor(
                    out=zsl[:, 0:1], in0=C(cb + 1),
                    scalar=(wcol if wcol is not None else 1.0),
                    in1=ysl[:, 0:1], op0=ALU.mult, op1=ALU.add),
                    after=(tt0,), inc=chV)
                tS1 = kb.op("S", lambda s, zsl=zsl, cb=cb: s.activation(
                    out=s_sqs[:, 0:NSH], in_=zsl[:, 1:D], func=AF.Square,
                    accum_out=C(cb + 2)), after=(tzz,), inc=chS)
                tzq = kb.op("V", lambda v, zsl=zsl, cb=cb: v.tensor_tensor(
                    out=C(cb + 3), in0=zsl[:, 0:1], in1=zsl[:, 0:1],
                    op=ALU.mult), after=(tz0,), inc=chV)
                tdd = kb.op("V", lambda v, cb=cb: v.tensor_tensor(
                    out=C(cb + 4), in0=C(cb + 3), in1=C(cb + 2),
                    op=ALU.subtract), after=(tzq, tS1), inc=chV)
                tdm = kb.op("V", lambda v, cb=cb: v.tensor_scalar(
                    out=C(cb + 5), in0=C(cb + 4), scalar1=EPS, scalar2=None,
                    op0=ALU.max), after=(tdd,), inc=chV)
                trc = kb.op("V", lambda v, cb=cb: v.reciprocal(
                    out=C(cb + 6), in_=C(cb + 5)), after=(tdm,), inc=chV)
                tsq2 = kb.op("S", lambda s, cb=cb: s.activation(
                    out=C(cb + 7), in_=C(cb + 6), func=AF.Sqrt),
                    after=(trc,), inc=chS)
                if hcol is not None:
                    tr1 = kb.op("V", lambda v, cb=cb: v.tensor_scalar(
                        out=C(cb + 8), in0=C(cb + 7), scalar1=1.0,
                        scalar2=None, op0=ALU.subtract), after=(tsq2,),
                        inc=chV)
                    tr2 = kb.op("V", lambda v, cb=cb, hcol=hcol:
                                v.scalar_tensor_tensor(
                        out=C(cb + 9), in0=hcol, scalar=C(cb + 8),
                        in1=s_onef[:, :], op0=ALU.mult, op1=ALU.add),
                        after=(tr1, t_onef), inc=chV)
                    rcol = C(cb + 9)
                    rtok = tr2
                else:
                    rcol = C(cb + 7)
                    rtok = tsq2
                ty = kb.op("S", lambda s, zsl=zsl, ysl=ysl, rcol=rcol:
                           s.activation(out=ysl, in_=zsl, func=AF.Identity,
                                        scale=rcol),
                           after=(rtok, tzz, tz0), inc=chS)
                g_y[lt] = ty
                g_zz[pz] = ty
                g_o2[po] = (tzz, tob)
                if s1_dbuf:
                    if po < 2:
                        g_ps1[po] = (tzz, tob)
                    else:
                        g_ps3[po - 2] = tzz
                else:
                    g_ps1[0] = (tzz, tob)
        g_we[pe] = t_last_w2
        if we_cols > WE_COLS:
            g_we[0] = g_we[1] = t_last_w2
        g_beb[pe] = t_s3b[NG - 1]
        g_time = t_last_k2
        return ty

    kb.wcache = {}
    NLT = HALF // 128
    ty_last = None
    for h in range(2):
        yg = tuple(t for t in g_y if t is not None)
        ty0 = kb.op("G", lambda g: g.memset(s_y[:, :], 0.0),
                    after=yg + ((a_done) if h == 0 else ()), inc=chGc)
        ty1 = kb.op("G", lambda g: g.memset(
            bass.AP(s_y, 0, [[NLT * D, 128], [D, NLT], [1, 1]]), 1.0),
            after=(ty0,), inc=chGc)
        for lt in range(NLT):
            g_y[lt] = ty1
        esB = contextlib.ExitStack()
        psA = esB.enter_context(pt("psA_%d" % h, [128, 2048], F32))
        psB = esB.enter_context(pt("psB_%d" % h, [128, 2048], F32))
        psab = [psA, psB]
        ps1_of = lambda g: psab[g % 2][:, 0:1024]
        ps3_of = lambda g: psab[g % 2][:, 1024:2048]
        po2 = [lambda lo, hi: psA[:, lo:hi],
               lambda lo, hi: psB[:, lo:hi],
               lambda lo, hi: psA[:, 1024 + lo:1024 + hi],
               lambda lo, hi: psB[:, 1024 + lo:1024 + hi]]
        for e in range(NE):
            if e == 0:
                emit_wload(0, WE_COLS, d_wec[0, :, :], d_bexp[0, :, :],
                           first=(h == 0))
            pf = None
            if e + 1 < NE:
                pf = (lambda e1=e + 1: emit_wload(
                    e1 % 2, WE_COLS, d_wec[e1, :, :], d_bexp[e1, :, :]))
            ty_last = ffn_expert(h, e, e % 2, (h == 0 and e == 0),
                                 ps1_of, ps3_of, po2, WE_COLS,
                                 W1_OFF, W3_OFF, W2_OFF, R0_OFF, 4, 4,
                                 d_wec[e, :, :], d_bexp[e, :, :], 512, e,
                                 skip_w=True, prefetch=pf)
        esB.close()
        g_ps1[0] = g_ps1[1] = (ty_last,)
        g_ps3[0] = g_ps3[1] = ty_last
        for _i in range(4):
            g_o2[_i] = (ty_last, ty_last)
        esC = contextlib.ExitStack()
        p_s1C = esC.enter_context(pt("p_s1C_%d" % h, [128, 2048], F32))
        p_s3C = esC.enter_context(pt("p_s3C_%d" % h, [128, 2048], F32))
        po2C = [lambda lo, hi: p_s1C[:, lo:hi],
                lambda lo, hi: p_s1C[:, 1024 + lo:1024 + hi],
                lambda lo, hi: p_s3C[:, lo:hi],
                lambda lo, hi: p_s3C[:, 1024 + lo:1024 + hi]]
        ps1C_of = lambda g: p_s1C[:, :]
        ps3C_of = lambda g: p_s3C[:, :]
        for sb in range(2):
            groups = tuple(range(sb * 4, sb * 4 + 4))
            ty_last = ffn_expert(h, 100 + sb, 0, False,
                                 ps1C_of, ps3C_of, po2C, WS_COLS,
                                 SW1_OFF, SW3_OFF, SW2_OFF, SR0_OFF, 8, 8,
                                 d_wshc, None, 1024, None,
                                 groups=groups, sfs=2048, skip_w=(sb == 1),
                                 s1_dbuf=False)
            g_ps1[0] = g_ps1[1] = (ty_last,)
            g_ps3[0] = g_ps3[1] = ty_last
            for _i in range(4):
                g_o2[_i] = (ty_last, ty_last)
        esC.close()
        for lt in range(NLT):
            gi = h * NLT + lt
            px = lt % 2
            cb = 32 * px + 16
            ysl = s_y[:, lt * D:(lt + 1) * D]
            xsl = s_x[:, px * D:(px + 1) * D]
            wsl2 = s_w32[:, px * D:(px + 1) * D]
            tx2l = kb.op("Y", lambda y, gi=gi, xsl=xsl: y.dma_start(
                out=xsl, in_=d_x2[gi * 128:(gi + 1) * 128, :]),
                after=(g_x[px], tok_x2sp[NT - 2], tok_x2sp[NT - 1]),
                inc=chX[px], k=16)
            rstd, mb, tln = ln_stats(ysl, cb, "sqrt", (g_y[lt],))
            txf = kb.op("S", lambda s, xsl=xsl, wsl2=wsl2, mb=mb: s.activation(
                out=wsl2, in_=xsl, func=AF.Identity, bias=mb),
                after=(tx2l, tln[1], g_zz[px]), inc=chS)
            tst = kb.op("V", lambda v, ysl=ysl, xsl=xsl, wsl2=wsl2, rstd=rstd:
                        v.scalar_tensor_tensor(
                out=xsl, in0=ysl, scalar=rstd, in1=wsl2,
                op0=ALU.mult, op1=ALU.add), after=(txf, tln[0]), inc=chV)
            tout = kb.op("Y", lambda y, gi=gi, xsl=xsl: y.dma_start(
                out=d_out[gi * 128:(gi + 1) * 128, :], in_=xsl),
                after=(tst,), inc=chOut[px], k=16)
            out_toks.append(tout)
            g_x[px] = tout
            g_zz[px] = tst
            g_y[lt] = tst
    esF.close()

    if debug:
        tdbg1 = kb.op("Y", lambda y: y.dma_start(
            out=d_dbg[:, 0:NT * 16], in_=s_wte[:, :]),
            after=(out_toks[-1],), inc=chC, k=16)
        tdbg2 = kb.op("Y", lambda y: y.dma_start(
            out=d_dbg[:, 512:512 + NT * 16], in_=s_hit[:, :]),
            after=(tdbg1,), inc=chC, k=16)
        out_toks.append(tdbg2)

    kb.op("G", lambda g: g.nop(), after=tuple(out_toks[-3:]))

    with nc.Block() as block:
        kb.emit(block)
    es.close()
    return nc


_NC_CACHE = {}


def _get_nc(debug=False):
    if debug not in _NC_CACHE:
        _NC_CACHE[debug] = build_nc(debug)
    return _NC_CACHE[debug]


def kernel(**inputs):
    g, x = pack_host(inputs)
    nc = _get_nc(False)
    in_maps = []
    for b in range(B):
        m = dict(g)
        m["x"] = np.ascontiguousarray(x[b])
        in_maps.append(m)
    res = run_bass_kernel_spmd(nc, in_maps, core_ids=list(range(B)))
    return np.stack([r["out"] for r in res.results]).astype(np.float32)


# version 15
# speedup vs baseline: 1.1788x; 1.0197x over previous
import contextlib

import numpy as np

import concourse.bass as bass
import concourse.mybir as mybir
from concourse.bass_utils import run_bass_kernel_spmd

F32 = mybir.dt.float32
BF16 = mybir.dt.bfloat16
AF = mybir.ActivationFunctionType
ALU = mybir.AluOpType
AX = mybir.AxisListType

B, T, D, H = 8, 4096, 1024, 64
NE, NI = 16, 511
NSH = 1023
NT = T // 128
HALF = T // 2
GRP = 256
NG = HALF // GRP
EPS_LN = 1e-5
EPS = 1e-6

W1_OFF = 0
W3_OFF = 8 * 512
W2_OFF = 2 * 8 * 512
R0_OFF = W2_OFF + 4 * NSH
WE_COLS = R0_OFF + NSH

SW1_OFF = 0
SW3_OFF = 8 * 1024
SW2_OFF = 2 * 8 * 1024
SR0_OFF = SW2_OFF + 8 * NSH
WS_COLS = SR0_OFF + NSH


class _Chan:
    __slots__ = ("sem", "n", "name", "owner")

    def __init__(self, sem, name, owner):
        self.sem, self.n, self.name, self.owner = sem, 0, name, owner


class KB:
    ENGS = ("Y", "G", "T", "V", "S")

    def __init__(self, nc, es):
        self.nc = nc
        self.es = es
        self.items = []
        self.chans = []

    def chan(self, name, owner=None):
        ch = _Chan(self.es.enter_context(self.nc.semaphore(name)), name, owner)
        self.chans.append(ch)
        return ch

    def op(self, eng, emit, after=(), inc=None, k=1):
        idx = len(self.items)
        tok = None
        if inc is not None:
            tok = (inc, idx)
        after = tuple(t for t in after if t is not None)
        self.items.append([eng, after, emit, inc, k, idx])
        return tok

    def emit(self, block):
        used = set()
        for eng, after, _, _, _, _ in self.items:
            for (ch, idx) in after:
                if ch.owner == eng == "T":
                    continue
                used.add((ch.name, idx))
        val_at = {}
        counts = {}
        for it in self.items:
            eng, after, emitfn, inc, k, idx = it
            if inc is not None and (k == 16 or (inc.name, idx) in used):
                counts[inc.name] = counts.get(inc.name, 0) + k
                val_at[(inc.name, idx)] = counts[inc.name]
            elif inc is not None:
                it[3] = None
        decs = {"T": block.tensor, "V": block.vector, "S": block.scalar,
                "G": block.gpsimd, "Y": block.sync}
        for engname in self.ENGS:
            items = [it for it in self.items if it[0] == engname]
            if not items:
                continue

            def section(eng, items=items, engname=engname):
                waited = {}
                for _, after, emitfn, inc, k, idx in items:
                    for (ch, tidx) in after:
                        if ch.owner == engname == "T":
                            continue
                        val = val_at[(ch.name, tidx)]
                        if waited.get(ch.name, 0) < val:
                            eng.wait_ge(ch.sem, val)
                            waited[ch.name] = val
                    instr = emitfn(eng)
                    if inc is not None:
                        if instr is None:
                            instr = eng.nop()
                        instr.then_inc(inc.sem, k)

            decs[engname](section)


def _pack_T(w, rpad=None):
    R, K = w.shape
    assert K % 128 == 0
    rpad = R if rpad is None else rpad
    wt = np.zeros((K, rpad), np.float32)
    wt[:, :R] = w.T
    wt = wt.reshape(K // 128, 128, rpad)
    return np.ascontiguousarray(wt.transpose(1, 0, 2).reshape(128, (K // 128) * rpad))


def _pack_rows(wt, nrow_pad):
    R, C = wt.shape
    out = np.zeros((nrow_pad * 128, C), np.float32)
    out[:R] = wt
    out = out.reshape(nrow_pad, 128, C).transpose(1, 0, 2)
    return np.ascontiguousarray(out.reshape(128, nrow_pad * C))


def _bias_fm(b, ntile):
    out = np.zeros((ntile * 128,), np.float32)
    out[:b.shape[0]] = b
    return np.ascontiguousarray(out.reshape(ntile, 128).T)


def pack_host(inputs):
    f = np.float32
    g = {}
    g["wq"] = _pack_T(np.ascontiguousarray(inputs["L0_attn_W_q"].T))
    g["wk"] = _pack_T(np.ascontiguousarray(inputs["L0_attn_W_k"].T))
    g["wv"] = _pack_T(np.ascontiguousarray(inputs["L0_attn_W_v"].T))
    g["wo"] = np.ascontiguousarray(inputs["L0_attn_W_o"], dtype=f)
    g["gw"] = _pack_T(np.ascontiguousarray(inputs["L0_ffn_gate_w"]))
    blobs = []
    bias_blobs = []
    for e in range(NE):
        we1 = np.asarray(inputs["L0_ffn_We1"][e], f)
        we3 = np.asarray(inputs["L0_ffn_We3"][e], f)
        we2 = np.asarray(inputs["L0_ffn_We2"][e], f)
        be1 = np.asarray(inputs["L0_ffn_be1"][e], f)
        be3 = np.asarray(inputs["L0_ffn_be3"][e], f)
        be2 = np.asarray(inputs["L0_ffn_be2"][e], f)
        p1 = _pack_T(we1, 512)
        p3 = _pack_T(we3, 512)
        w2t = np.ascontiguousarray(we2.T)
        p2 = _pack_rows(w2t[1:], 4)
        r0 = np.zeros((128, NSH), f)
        r0[0] = w2t[0]
        r0[1] = be2
        blobs.append(np.concatenate([p1, p3, p2, r0], axis=1))
        bias_blobs.append(np.concatenate([_bias_fm(be1, 4), _bias_fm(be3, 4)],
                                         axis=1))
    g["wexp"] = np.stack(blobs)
    g["bexp"] = np.stack(bias_blobs)
    assert g["wexp"].shape == (NE, 128, WE_COLS), g["wexp"].shape
    ws1 = np.asarray(inputs["L0_ffn_Ws1"], f)
    ws3 = np.asarray(inputs["L0_ffn_Ws3"], f)
    ws2 = np.asarray(inputs["L0_ffn_Ws2"], f)
    sp1 = _pack_T(ws1, 1024)
    sp3 = _pack_T(ws3, 1024)
    ws2t = np.ascontiguousarray(ws2.T)
    sp2 = _pack_rows(ws2t[1:], 8)
    sr0 = np.zeros((128, NSH), f)
    sr0[0] = ws2t[0]
    sr0[1] = np.asarray(inputs["L0_ffn_bs2"], f)
    g["wsh"] = np.concatenate([sp1, sp3, sp2, sr0], axis=1)
    g["bsh"] = np.concatenate(
        [_bias_fm(np.asarray(inputs["L0_ffn_bs1"], f), 8),
         _bias_fm(np.asarray(inputs["L0_ffn_bs3"], f), 8)], axis=1)
    assert g["wsh"].shape == (128, WS_COLS), g["wsh"].shape
    import ml_dtypes
    ident = np.eye(128, dtype=f)
    g["identf"] = ident
    g["identb"] = ident.astype(ml_dtypes.bfloat16)
    ut = np.triu(np.ones((128, 128), f))
    g["utf"] = ut
    g["utb"] = ut.astype(ml_dtypes.bfloat16)
    onesb = np.ones((1, 128), f)
    g["onesb"] = onesb.astype(ml_dtypes.bfloat16)
    x = np.asarray(inputs["L0_attn_x"], f)
    return g, x


def build_nc(debug=False, stop_after=None, a3_gate=True, a3_xtspill=True,
             a3_level=9, detect_races=True):
    nc = bass.Bass("TRN2", target_bir_lowering=False,
                   detect_race_conditions=detect_races)
    es = contextlib.ExitStack()
    kb = KB(nc, es)

    dbg_kind = "ExternalOutput" if debug else "Internal"
    d_x = nc.dram_tensor("x", [T, D], F32, kind="ExternalInput").ap()
    d_wq = nc.dram_tensor("wq", [128, 8 * H], F32, kind="ExternalInput").ap()
    d_wk = nc.dram_tensor("wk", [128, 8 * H], F32, kind="ExternalInput").ap()
    d_wv = nc.dram_tensor("wv", [128, 8 * H], F32, kind="ExternalInput").ap()
    d_wo = nc.dram_tensor("wo", [H, D], F32, kind="ExternalInput").ap()
    d_gw = nc.dram_tensor("gw", [128, 8 * 16], F32, kind="ExternalInput").ap()
    d_wexp = nc.dram_tensor("wexp", [NE, 128, WE_COLS], F32, kind="ExternalInput").ap()
    d_wsh = nc.dram_tensor("wsh", [128, WS_COLS], F32, kind="ExternalInput").ap()
    d_bexp = nc.dram_tensor("bexp", [NE, 128, 8], F32, kind="ExternalInput").ap()
    d_bsh = nc.dram_tensor("bsh", [128, 16], F32, kind="ExternalInput").ap()
    d_idf = nc.dram_tensor("identf", [128, 128], F32, kind="ExternalInput").ap()
    d_idb = nc.dram_tensor("identb", [128, 128], BF16, kind="ExternalInput").ap()
    d_utf = nc.dram_tensor("utf", [128, 128], F32, kind="ExternalInput").ap()
    d_utb = nc.dram_tensor("utb", [128, 128], BF16, kind="ExternalInput").ap()
    d_onesb = nc.dram_tensor("onesb", [1, 128], BF16, kind="ExternalInput").ap()
    d_out = nc.dram_tensor("out", [T, D], F32, kind="ExternalOutput").ap()
    d_wec = nc.dram_tensor("wec", [NE, 128, WE_COLS], BF16, kind="Internal").ap()
    d_wshc = nc.dram_tensor("wshc", [128, WS_COLS], BF16, kind="Internal").ap()
    d_x2 = nc.dram_tensor("x2sp", [T, D], F32, kind=dbg_kind).ap()
    d_xt = nc.dram_tensor("xtsp", [8, 128, T], BF16, kind=dbg_kind).ap()
    d_dbg = None
    if debug:
        d_dbg = nc.dram_tensor("dbg", [128, 4096], F32, kind="ExternalOutput").ap()

    chL = [kb.chan("chL0"), kb.chan("chL1")]
    chLg = [kb.chan("chLg0"), kb.chan("chLg1")]
    chCv = [kb.chan("chCv0"), kb.chan("chCv1")]
    chC = kb.chan("chC")
    chX = [kb.chan("chX0"), kb.chan("chX1")]
    chSp = [kb.chan("chSp0"), kb.chan("chSp1")]
    chXt = [kb.chan("chXt0"), kb.chan("chXt1")]
    chXL = [kb.chan("chXL0"), kb.chan("chXL1")]
    chWt = [kb.chan("chWt0"), kb.chan("chWt1")]
    chBe = [kb.chan("chBe0"), kb.chan("chBe1")]
    chOut = [kb.chan("chOut0"), kb.chan("chOut1")]
    chT = kb.chan("chT", "T")
    chV = kb.chan("chV", "V")
    chS = kb.chan("chS", "S")
    chGc = kb.chan("chGc", "G")

    st = nc.sbuf_tensor

    s_x = es.enter_context(st("s_x", [128, 2 * D], F32))
    s_w32 = es.enter_context(st("s_w32", [128, 2 * D], F32))
    s_b16 = es.enter_context(st("s_b16", [128, 2 * D], BF16))
    s_sqs = es.enter_context(st("s_sqs", [128, D], BF16))
    s_wqb = es.enter_context(st("s_wqb", [128, 8 * H], BF16))
    s_wkb = es.enter_context(st("s_wkb", [128, 8 * H], BF16))
    s_wvb = es.enter_context(st("s_wvb", [128, 8 * H], BF16))
    s_wob = es.enter_context(st("s_wob", [H, D], BF16))
    s_gw = es.enter_context(st("s_gw", [128, 8 * 16], F32))
    s_idf = es.enter_context(st("s_idf", [128, 128], F32))
    s_idb = es.enter_context(st("s_idb", [128, 128], BF16))
    s_utf = es.enter_context(st("s_utf", [128, 128], F32))
    s_utb = es.enter_context(st("s_utb", [128, 128], BF16))
    s_onesb = es.enter_context(st("s_onesb", [1, 128], BF16))
    s_wte = es.enter_context(st("s_wte", [128, NT * 16], F32))
    s_hit = es.enter_context(st("s_hit", [128, NT * 16], F32))
    s_y = es.enter_context(st("s_y", [128, (HALF // 128) * D], F32))
    s_c = es.enter_context(st("s_c", [128, 64], F32))
    s_on1 = es.enter_context(st("s_on1", [128, 1], BF16))
    s_cst = es.enter_context(st("s_cst", [128, 3], F32))

    def C(i):
        return s_c[:, i:i + 1]

    c_epsln = s_cst[:, 0:1]
    c_one = s_cst[:, 1:2]
    c_mhalf = s_cst[:, 2:3]
    t_cst = kb.op("G", lambda g: g.memset(s_cst[:, 0:1], EPS_LN), inc=None)
    t_cst = kb.op("G", lambda g: g.memset(s_cst[:, 1:2], 1.0), inc=None)
    t_cst = kb.op("G", lambda g: g.memset(s_cst[:, 2:3], -0.5), inc=chGc)

    def emit_weight_conversions(after=()):
        tok = None
        for e in range(NE):
            tok = kb.op("G", lambda gp, e=e: gp.dma_start(
                out=d_wec[e, :, :], in_=d_wexp[e, :, :]),
                after=after if e == 0 else (), inc=chCv[0], k=16)
        tok = kb.op("G", lambda gp: gp.dma_start(out=d_wshc, in_=d_wsh),
                    inc=chCv[0], k=16)
        return (tok,)

    tok_wcache = None

    def ld(dst, src):
        return kb.op("Y", lambda y: y.dma_start(out=dst, in_=src), inc=chC, k=16)

    for dst, srcc in ((s_idf[:, :], d_idf), (s_idb[:, :], d_idb),
                      (s_utf[:, :], d_utf), (s_utb[:, :], d_utb),
                      (s_onesb[:, :], d_onesb)):
        ld(dst, srcc)
    t_consts = [ld(s_gw[:, :], d_gw)]
    t_aw = []
    for dsrc, dst in ((d_wq, s_wqb), (d_wk, s_wkb), (d_wv, s_wvb)):
        t_aw.append(kb.op("G", lambda gp, s=dsrc, d=dst: gp.dma_start(
            out=d[:, :], in_=s), inc=chCv[1], k=16))
    t_aw.append(kb.op("G", lambda gp: gp.dma_start(
        out=s_wob[:, :], in_=d_wo), inc=chCv[1], k=16))
    t_consts += t_aw[-1:]

    if stop_after == "W":
        kb.op("G", lambda g: g.nop(), after=(t_consts[0],))
        with nc.Block() as block:
            kb.emit(block)
        es.close()
        return nc

    esA = contextlib.ExitStack()
    pt = nc.psum_tensor
    s_qf = esA.enter_context(st("s_qf", [128, T], BF16))
    s_vf = esA.enter_context(st("s_vf", [128, T], BF16))
    s_qt = esA.enter_context(st("s_qt", [128, NT * H], F32))
    s_kt = esA.enter_context(st("s_kt", [128, NT * H], BF16))
    s_vt = esA.enter_context(st("s_vt", [128, NT * H], BF16))
    s_ot = esA.enter_context(st("s_ot", [128, NT * H], BF16))
    s_of = esA.enter_context(st("s_of", [128, T], BF16))
    s_lnf = esA.enter_context(st("s_lnf", [128, 2 * D], BF16))
    s_ph = esA.enter_context(st("s_ph", [128, 832], F32))
    s_S = esA.enter_context(st("s_S", [64, 2 * H], F32))
    s_Sb = esA.enter_context(st("s_Sb", [64, H], BF16))
    s_z = esA.enter_context(st("s_z", [1, 2 * H], F32))
    s_zb = esA.enter_context(st("s_zb", [1, H], BF16))
    s_fm32 = esA.enter_context(st("s_fm32", [128, D], F32))
    s_g16 = esA.enter_context(st("s_g16", [128, 2 * 16], F32))
    s_g16b = esA.enter_context(st("s_g16b", [128, 2 * 16], F32))
    s_gm = esA.enter_context(st("s_gm", [128, 2 * 128], BF16))

    t_on1 = kb.op("G", lambda g: g.memset(s_on1[:, :], 1.0), inc=chGc)
    t_S0 = kb.op("G", lambda g: g.memset(s_S[:, :], 0.0), inc=chGc)
    t_S0b = kb.op("G", lambda g: g.memset(s_Sb[:, :], 0.0), inc=chGc)
    t_z0 = kb.op("G", lambda g: g.memset(s_z[:, :], 0.0), inc=chGc)
    t_z0b = kb.op("G", lambda g: g.memset(s_zb[:, :], 0.0), inc=chGc)

    def ln_stats(src_ap, cb, variant, after_src):
        t1 = kb.op("V", lambda v: v.tensor_reduce(
            out=C(cb), in_=src_ap, axis=AX.X, op=ALU.add),
            after=after_src, inc=chV)
        t2 = kb.op("S", lambda s: s.activation(
            out=s_sqs[:, :], in_=src_ap, func=AF.Square, accum_out=C(cb + 1)),
            after=after_src, inc=chS)
        t3 = kb.op("V", lambda v: v.tensor_scalar(
            out=C(cb + 2), in0=C(cb), scalar1=-1.0 / D, scalar2=None,
            op0=ALU.mult), after=(t1,), inc=chV)
        t4 = kb.op("V", lambda v: v.tensor_tensor(
            out=C(cb + 3), in0=C(cb + 2), in1=C(cb + 2), op=ALU.mult),
            after=(t3,), inc=chV)
        t5 = kb.op("V", lambda v: v.scalar_tensor_tensor(
            out=C(cb + 4), in0=C(cb + 1), scalar=1.0 / D, in1=C(cb + 3),
            op0=ALU.mult, op1=ALU.subtract), after=(t2, t4), inc=chV)
        if variant == "expln":
            t6 = kb.op("S", lambda s: s.activation(
                out=C(cb + 5), in_=C(cb + 4), func=AF.Ln, bias=c_epsln),
                after=(t5, t_cst), inc=chS)
            t7 = kb.op("S", lambda s: s.activation(
                out=C(cb + 6), in_=C(cb + 5), func=AF.Exp, scale=c_mhalf),
                after=(t6,), inc=chS)
            t8 = kb.op("S", lambda s: s.activation(
                out=C(cb + 7), in_=C(cb + 2), func=AF.Identity,
                scale=C(cb + 6)), after=(t7,), inc=chS)
        else:
            t6 = kb.op("S", lambda s: s.activation(
                out=C(cb + 5), in_=C(cb + 4), func=AF.Sqrt, bias=c_epsln),
                after=(t5, t_cst), inc=chS)
            t7 = kb.op("V", lambda v: v.reciprocal(
                out=C(cb + 6), in_=C(cb + 5)), after=(t6,), inc=chV)
            t8 = kb.op("V", lambda v: v.tensor_scalar(
                out=C(cb + 7), in0=C(cb + 2), scalar1=C(cb + 6), scalar2=None,
                op0=ALU.mult), after=(t7,), inc=chV)
        return C(cb + 6), C(cb + 7), (t7, t8)

    def phi(src_ap, dst_ap, w, after, np_=128, base=0):
        ta = kb.op("V", lambda v: v.tensor_scalar(
            out=s_ph[0:np_, base:base + w], in0=src_ap, scalar1=0.0,
            scalar2=None, op0=ALU.min), after=after, inc=chV)
        tb = kb.op("S", lambda s: s.activation(
            out=s_ph[0:np_, base + w:base + 2 * w],
            in_=s_ph[0:np_, base:base + w], func=AF.Exp),
            after=(ta,), inc=chS)
        tc = kb.op("V", lambda v: v.tensor_scalar(
            out=s_ph[0:np_, base + 2 * w:base + 3 * w], in0=src_ap,
            scalar1=0.0, scalar2=None, op0=ALU.max), after=after, inc=chV)
        td = kb.op("V", lambda v: v.tensor_tensor(
            out=dst_ap, in0=s_ph[0:np_, base + w:base + 2 * w],
            in1=s_ph[0:np_, base + 2 * w:base + 3 * w],
            op=ALU.add), after=(tb, tc), inc=chV)
        return td

    esA1 = contextlib.ExitStack()
    p_trA = esA1.enter_context(pt("p_trA", [128, D], BF16))
    p_qkv = esA1.enter_context(pt("p_qkv", [128, 512], F32))
    p_G = esA1.enter_context(pt("p_G", [128, 128], F32))
    p_oc = esA1.enter_context(pt("p_oc", [128, H], F32))
    p_zc = esA1.enter_context(pt("p_zc", [128, H], F32))
    p_z = esA1.enter_context(pt("p_z", [1, H], F32))
    p_S = esA1.enter_context(pt("p_S", [64, H], F32))
    p_of = esA1.enter_context(pt("p_of", [64, 128], BF16))
    g_x = [None, None]
    g_trA = None
    g_qkv = ()
    tok_qf = [None] * NT
    tok_vf = [None] * NT
    tok_qt = [None] * NT
    tok_kt = [None] * NT
    tok_vt = [None] * NT
    g_lnf = [None, None]
    a1_tr_last = None

    def a1_tile(i):
        nonlocal g_trA, g_qkv, a1_tr_last
        p = i % 2
        cb = 32 * p
        xsl = s_x[:, p * D:(p + 1) * D]
        tx = kb.op("Y", lambda y, i=i, xsl=xsl: y.dma_start(
            out=xsl, in_=d_x[i * 128:(i + 1) * 128, :]),
            after=(g_x[p],) + (tuple(t_consts) if i < 2 else ()),
            inc=chX[p], k=16)
        rstd, mb, tln = ln_stats(xsl, cb, "expln", (tx,))
        lnsl = s_b16[:, p * D:(p + 1) * D]
        tln1 = kb.op("S", lambda s, xsl=xsl, lnsl=lnsl, rstd=rstd, mb=mb:
                     s.activation(out=lnsl, in_=xsl, func=AF.Identity,
                                  scale=rstd, bias=mb),
                     after=tln + (g_lnf[p],), inc=chS)
        g_x[p] = tln1
        ttr = None
        for k in range(8):
            ttr = kb.op("T", lambda t, k=k, lnsl=lnsl: t.transpose(
                p_trA[:, k * 128:(k + 1) * 128], lnsl[:, k * 128:(k + 1) * 128],
                s_idb[:, :]),
                after=(tln1, g_trA) if k == 0 else (),
                inc=chT if k == 7 else None)
        a1_tr_last = ttr
        fmsl = s_lnf[:, p * D:(p + 1) * D]
        tfm = kb.op("V", lambda v, fmsl=fmsl: v.tensor_copy(
            out=fmsl, in_=p_trA[:, :]), after=(ttr,), inc=chV)
        g_trA = tfm
        tmm = None
        forms = [
            lambda t, k, fmsl: t.matmul(
                p_qkv[0:64, 0:128], s_wqb[:, k * H:(k + 1) * H],
                fmsl[:, k * 128:(k + 1) * 128], start=(k == 0), stop=(k == 7)),
            lambda t, k, fmsl: t.matmul(
                p_qkv[0:64, 128:256], s_wvb[:, k * H:(k + 1) * H],
                fmsl[:, k * 128:(k + 1) * 128], start=(k == 0), stop=(k == 7)),
            lambda t, k, fmsl: t.matmul(
                p_qkv[:, 256:320], fmsl[:, k * 128:(k + 1) * 128],
                s_wqb[:, k * H:(k + 1) * H], start=(k == 0), stop=(k == 7)),
            lambda t, k, fmsl: t.matmul(
                p_qkv[:, 320:384], fmsl[:, k * 128:(k + 1) * 128],
                s_wkb[:, k * H:(k + 1) * H], start=(k == 0), stop=(k == 7)),
            lambda t, k, fmsl: t.matmul(
                p_qkv[:, 384:448], fmsl[:, k * 128:(k + 1) * 128],
                s_wvb[:, k * H:(k + 1) * H], start=(k == 0), stop=(k == 7)),
        ]
        for fi, fn in enumerate(forms):
            for k in range(8):
                af = (tfm,) + g_qkv if (fi == 0 and k == 0) else ()
                tmm = kb.op("T",
                            lambda t, fn=fn, k=k, fmsl=fmsl: fn(t, k, fmsl),
                            after=af,
                            inc=chT if (fi == 4 and k == 7) else None)
        g_lnf[p] = tmm
        tok_vf[i] = kb.op("V", lambda v, i=i: v.tensor_copy(
            out=s_vf[0:64, i * 128:(i + 1) * 128], in_=p_qkv[0:64, 128:256]),
            after=(tmm,), inc=chV)
        tok_vt[i] = kb.op("V", lambda v, i=i: v.tensor_copy(
            out=s_vt[:, i * H:(i + 1) * H], in_=p_qkv[:, 384:448]),
            after=(tmm,), inc=chV)
        specs = [
            (p_qkv[0:64, 0:128], s_qf[0:64, i * 128:(i + 1) * 128], 128, 64, 0),
            (p_qkv[:, 256:320], s_qt[:, i * H:(i + 1) * H], 64, 128, 384),
            (p_qkv[:, 320:384], s_kt[:, i * H:(i + 1) * H], 64, 128, 576),
        ]
        tas, tcs, tbs, tds = [], [], [], []
        for src, _, w, np_, base in specs:
            tas.append(kb.op("V", lambda v, src=src, w=w, np_=np_, base=base:
                             v.tensor_scalar(
                out=s_ph[0:np_, base:base + w], in0=src, scalar1=0.0,
                scalar2=None, op0=ALU.min), after=(tmm,), inc=chV))
            tcs.append(kb.op("V", lambda v, src=src, w=w, np_=np_, base=base:
                             v.tensor_scalar(
                out=s_ph[0:np_, base + 2 * w:base + 3 * w], in0=src,
                scalar1=0.0, scalar2=None, op0=ALU.max), after=(tmm,),
                inc=chV))
        for j, (_, _, w, np_, base) in enumerate(specs):
            tbs.append(kb.op("S", lambda s, w=w, np_=np_, base=base:
                             s.activation(
                out=s_ph[0:np_, base + w:base + 2 * w],
                in_=s_ph[0:np_, base:base + w], func=AF.Exp),
                after=(tas[j],), inc=chS))
        for j, (_, dst, w, np_, base) in enumerate(specs):
            tds.append(kb.op("V", lambda v, dst=dst, w=w, np_=np_, base=base:
                             v.tensor_tensor(
                out=dst, in0=s_ph[0:np_, base + w:base + 2 * w],
                in1=s_ph[0:np_, base + 2 * w:base + 3 * w],
                op=ALU.add), after=(tbs[j], tcs[j]), inc=chV))
        tok_qf[i], tok_qt[i], tok_kt[i] = tds
        g_qkv = (tok_vf[i], tok_vt[i], tok_qf[i], tok_qt[i], tok_kt[i])

    g_G = None
    g_gm = [None, None]
    g_oc = None
    g_zc = None
    g_of = None
    t_Scp = t_S0b
    t_zcp = t_z0b
    t_dSmm = None
    t_zsum = None
    tok_o = [None] * NT

    def a2_chunk(c):
        nonlocal g_G, g_oc, g_zc, g_of, t_Scp, t_zcp, t_dSmm, t_zsum
        p = c % 2
        cbase = 40 + 4 * p
        tG = kb.op("T", lambda t, c=c: t.matmul(
            p_G[:, :], s_vf[0:64, c * 128:(c + 1) * 128],
            s_qf[0:64, c * 128:(c + 1) * 128], start=True, stop=True),
            after=(tok_vf[c], tok_qf[c], g_G), inc=chT)
        gm = s_gm[:, p * 128:(p + 1) * 128]
        tGm = kb.op("V", lambda v, gm=gm: v.tensor_tensor(
            out=gm, in0=p_G[:, :], in1=s_utf[:, :], op=ALU.mult),
            after=(tG, g_gm[p]), inc=chV)
        g_G = tGm
        to1 = kb.op("T", lambda t, c=c, gm=gm: t.matmul(
            p_oc[:, :], gm, s_kt[:, c * H:(c + 1) * H], start=True, stop=False),
            after=(tGm, tok_kt[c], g_oc), inc=None)
        to2 = kb.op("T", lambda t, c=c: t.matmul(
            p_oc[:, :], s_qf[0:64, c * 128:(c + 1) * 128], s_Sb[:, :],
            start=False, stop=True), after=(t_Scp,), inc=chT)
        g_gm[p] = to2
        tz1 = kb.op("T", lambda t, c=c: t.matmul(
            p_zc[:, :], s_utb[:, :], s_kt[:, c * H:(c + 1) * H],
            start=True, stop=False), after=(g_zc, tok_kt[c]), inc=None)
        tz2 = kb.op("T", lambda t: t.matmul(
            p_zc[:, :], s_onesb[0:1, :], s_zb[0:1, :],
            start=False, stop=True), after=(t_zcp, t_on1), inc=chT)
        t_zsum = kb.op("T", lambda t, c=c: t.matmul(
            p_z[0:1, :], s_on1[:, :], s_kt[:, c * H:(c + 1) * H],
            start=True, stop=True),
            after=(t_zcp,), inc=chT)
        t_dSmm = kb.op("T", lambda t, c=c: t.matmul(
            p_S[:, :], s_vt[:, c * H:(c + 1) * H], s_kt[:, c * H:(c + 1) * H],
            start=True, stop=True),
            after=(tok_vt[c], t_Scp, to2), inc=chT)
        td1 = kb.op("V", lambda v, c=c: v.tensor_tensor(
            out=s_ph[:, 768:832], in0=p_zc[:, :],
            in1=s_qt[:, c * H:(c + 1) * H],
            op=ALU.mult), after=(tz2, tok_qt[c]), inc=chV)
        g_zc = td1
        td2 = kb.op("V", lambda v, cb=cbase: v.tensor_reduce(
            out=C(cb), in_=s_ph[:, 768:832], axis=AX.X, op=ALU.add),
            after=(td1,), inc=chV)
        td3 = kb.op("V", lambda v, cb=cbase: v.tensor_scalar(
            out=C(cb + 1), in0=C(cb), scalar1=1e-6, scalar2=None,
            op0=ALU.max), after=(td2,), inc=chV)
        td4 = kb.op("V", lambda v, cb=cbase: v.reciprocal(
            out=C(cb + 2), in_=C(cb + 1)), after=(td3,), inc=chV)
        to3 = kb.op("V", lambda v, c=c, cb=cbase: v.tensor_scalar(
            out=s_ot[:, c * H:(c + 1) * H], in0=p_oc[:, :],
            scalar1=C(cb + 2), scalar2=None, op0=ALU.mult),
            after=(to2, td4), inc=chV)
        g_oc = to3
        pc, pn = c % 2, (c + 1) % 2
        tSa = kb.op("V", lambda v, pc=pc, pn=pn: v.tensor_tensor(
            out=s_S[:, pn * H:(pn + 1) * H], in0=p_S[:, :],
            in1=s_S[:, pc * H:(pc + 1) * H], op=ALU.add),
            after=(t_dSmm, t_S0, t_Scp), inc=chV)
        t_Scp = kb.op("V", lambda v, pn=pn: v.tensor_copy(
            out=s_Sb[:, :], in_=s_S[:, pn * H:(pn + 1) * H]),
            after=(tSa, t_S0b), inc=chV)
        tza = kb.op("V", lambda v, pc=pc, pn=pn: v.tensor_tensor(
            out=s_z[0:1, pn * H:(pn + 1) * H], in0=p_z[0:1, :],
            in1=s_z[0:1, pc * H:(pc + 1) * H], op=ALU.add),
            after=(t_zsum, tz2, t_z0, t_zcp), inc=chV)
        t_zcp = kb.op("V", lambda v, pn=pn: v.tensor_copy(
            out=s_zb[0:1, :], in_=s_z[0:1, pn * H:(pn + 1) * H]),
            after=(tza, t_z0b), inc=chV)
        tof = kb.op("T", lambda t, c=c: t.transpose(
            p_of[:, :], s_ot[:, c * H:(c + 1) * H], s_idb[:, :]),
            after=(to3, g_of), inc=chT)
        tok_o[c] = kb.op("V", lambda v, c=c: v.tensor_copy(
            out=s_of[0:64, c * 128:(c + 1) * 128], in_=p_of[:, :]),
            after=(tof,), inc=chV)
        g_of = tok_o[c]

    for i in range(NT + 1):
        if i < NT:
            a1_tile(i)
        if i == 7:
            tok_wcache = emit_weight_conversions(after=(tok_vt[6],))
        if i >= 1:
            a2_chunk(i - 1)
    esA1.close()

    esA3 = contextlib.ExitStack()
    p_at = [esA3.enter_context(pt("p_at0", [128, D], F32)),
            esA3.enter_context(pt("p_at1", [128, D], F32))]
    p_tr2 = esA3.enter_context(pt("p_tr2", [128, D], F32))
    p_gate = esA3.enter_context(pt("p_gate", [128, 16], F32))
    s_at = esA3.enter_context(st("s_at", [128, 2 * D], F32))
    s_glog = esA3.enter_context(st("s_glog", [128, 2 * 16], F32))
    g_at = [None, None]
    g_atS = [None, None]
    g_x = [g_x[0], g_x[1]]
    g_tr2 = None
    g_fm32 = None
    g_gate = None
    g_b16 = [g_lnf[0], g_lnf[1]]
    tok_x2sp = [None] * NT
    tok_xtsp = [None] * NT
    g_w32 = [None, None]
    for i in range(NT):
        p = i % 2
        cb = 32 * p
        atp = p_at[p]
        t1 = kb.op("T", lambda t, i=i, atp=atp: t.matmul(
            atp[:, 0:512], s_of[0:64, i * 128:(i + 1) * 128], s_wob[:, 0:512],
            start=True, stop=True),
            after=(tok_o[i], g_at[p]) + ((g_of, g_oc) if i < 2 else ()),
            inc=None)
        t2 = kb.op("T", lambda t, i=i, atp=atp: t.matmul(
            atp[:, 512:1024], s_of[0:64, i * 128:(i + 1) * 128],
            s_wob[:, 512:1024], start=True, stop=True), after=(), inc=chT)
        if a3_level < 2:
            tev = kb.op("V", lambda v, atp=atp, p=p: v.tensor_copy(
                out=s_w32[:, p * D:(p + 1) * D], in_=atp[:, :]),
                after=(t2, g_at[p], g_w32[p]), inc=chV)
            g_at[p] = tev
            g_w32[p] = tev
            tok_x2sp[i] = tok_xtsp[i] = tw2 = tev
            continue
        atsl = s_at[:, p * D:(p + 1) * D]
        tat = kb.op("V", lambda v, atp=atp, atsl=atsl: v.tensor_copy(
            out=atsl, in_=atp[:, :]), after=(t2, g_at[p], g_atS[p]), inc=chV)
        g_at[p] = tat
        if 11 <= a3_level <= 15:
            toks = []
            t1 = kb.op("V", lambda v, atsl=atsl, cb=cb: v.tensor_reduce(
                out=C(cb), in_=atsl, axis=AX.X, op=ALU.add),
                after=(tat,), inc=chV)
            toks.append(t1)
            if a3_level >= 12:
                t2s = kb.op("S", lambda s, atsl=atsl, cb=cb: s.activation(
                    out=s_sqs[:, :], in_=atsl, func=AF.Square,
                    accum_out=C(cb + 1)), after=(tat,), inc=chS)
                toks.append(t2s)
            if a3_level >= 13:
                t3 = kb.op("V", lambda v, cb=cb: v.tensor_scalar(
                    out=C(cb + 2), in0=C(cb), scalar1=1.0 / D, scalar2=None,
                    op0=ALU.mult), after=(t1,), inc=chV)
                t4 = kb.op("V", lambda v, cb=cb: v.tensor_tensor(
                    out=C(cb + 3), in0=C(cb + 2), in1=C(cb + 2), op=ALU.mult),
                    after=(t3,), inc=chV)
                t5 = kb.op("V", lambda v, cb=cb: v.scalar_tensor_tensor(
                    out=C(cb + 4), in0=C(cb + 1), scalar=1.0 / D,
                    in1=C(cb + 3), op0=ALU.mult, op1=ALU.subtract),
                    after=(t2s, t4), inc=chV)
                t8 = kb.op("V", lambda v, cb=cb: v.tensor_scalar(
                    out=C(cb + 7), in0=C(cb + 2), scalar1=C(cb + 4),
                    scalar2=-1.0, op0=ALU.mult, op1=ALU.mult),
                    after=(t5,), inc=chV)
                toks += [t8]
            if a3_level >= 14:
                t6 = kb.op("S", lambda s, cb=cb: s.activation(
                    out=C(cb + 5), in_=C(cb + 4), func=AF.Ln, bias=c_epsln),
                    after=(t5, t_cst), inc=chS)
                t7 = kb.op("S", lambda s, cb=cb: s.activation(
                    out=C(cb + 6), in_=C(cb + 5), func=AF.Exp, scale=c_mhalf),
                    after=(t6,), inc=chS)
                toks += [t7]
            tev = kb.op("V", lambda v, atsl=atsl, p=p: v.tensor_copy(
                out=s_w32[:, p * D:(p + 1) * D], in_=atsl),
                after=tuple(toks) + (g_w32[p],), inc=chV)
            g_atS[p] = tev
            g_w32[p] = tev
            tok_x2sp[i] = tok_xtsp[i] = tw2 = tev
            continue
        rstd, mb, tln = ln_stats(atsl, cb, "expln", (tat,))
        xsl = s_x[:, p * D:(p + 1) * D]
        tx = kb.op("Y", lambda y, i=i, xsl=xsl: y.dma_start(
            out=xsl, in_=d_x[i * 128:(i + 1) * 128, :]),
            after=(g_x[p],), inc=chX[p], k=16)
        wsl = s_w32[:, p * D:(p + 1) * D]
        txf = kb.op("S", lambda s, xsl=xsl, wsl=wsl, mb=mb: s.activation(
            out=wsl, in_=xsl, func=AF.Identity, bias=mb),
            after=(tx, tln[1], g_w32[p]), inc=chS)
        tx2 = kb.op("V", lambda v, atsl=atsl, xsl=xsl, wsl=wsl, rstd=rstd:
                    v.scalar_tensor_tensor(
                        out=xsl, in0=atsl, scalar=rstd, in1=wsl,
                        op0=ALU.mult, op1=ALU.add),
                    after=(txf, tln[0]), inc=chV)
        g_atS[p] = tx2
        tok_x2sp[i] = kb.op("Y", lambda y, i=i, xsl=xsl: y.dma_start(
            out=d_x2[i * 128:(i + 1) * 128, :], in_=xsl),
            after=(tx2,), inc=chSp[p], k=16)
        if a3_level < 3:
            tok_xtsp[i] = tw2 = tok_x2sp[i]
            g_x[p] = tok_x2sp[i]
            continue
        rstd2, mb2, tln2 = ln_stats(xsl, cb + 8, "expln", (tx2,))
        tln2f = kb.op("S", lambda s, xsl=xsl, wsl=wsl, rstd2=rstd2, mb2=mb2:
                      s.activation(out=wsl, in_=xsl, func=AF.Identity,
                                   scale=rstd2, bias=mb2),
                      after=tln2 + (txf,), inc=chS)
        g_x[p] = tok_x2sp[i]
        if a3_level < 4:
            tok_xtsp[i] = tw2 = tln2f
            g_w32[p] = tln2f
            continue
        ttr = None
        for k in range(8):
            ttr = kb.op("T", lambda t, k=k, wsl=wsl: t.transpose(
                p_tr2[:, k * 128:(k + 1) * 128], wsl[:, k * 128:(k + 1) * 128],
                s_idf[:, :]),
                after=(tln2f, g_tr2) if k == 0 else (),
                inc=chT if k == 7 else None)
        g_w32[p] = ttr
        tfm32 = kb.op("V", lambda v: v.tensor_copy(
            out=s_fm32[:, :], in_=p_tr2[:, :]), after=(ttr, g_fm32), inc=chV)
        bsl = s_b16[:, p * D:(p + 1) * D]
        txtb = kb.op("V", lambda v, bsl=bsl: v.tensor_copy(
            out=bsl, in_=p_tr2[:, :]),
            after=(ttr, g_b16[p], tfm32), inc=chV)
        g_tr2 = txtb
        if a3_xtspill:
            tok_xtsp[i] = kb.op("Y", lambda y, i=i, bsl=bsl: y.dma_start(
                out=d_xt[:, :, i * 128:(i + 1) * 128].rearrange("k p n -> p k n"),
                in_=bsl.rearrange("p (k n) -> p k n", k=8)),
                after=(txtb,), inc=chXt[p], k=16)
        else:
            tok_xtsp[i] = txtb
        g_b16[p] = tok_xtsp[i]
        if not a3_gate:
            tw2 = txtb
            continue
        tg = None
        for k in range(8):
            tg = kb.op("T", lambda t, k=k: t.matmul(
                p_gate[:, :], s_fm32[:, k * 128:(k + 1) * 128],
                s_gw[:, k * 16:(k + 1) * 16], start=(k == 0), stop=(k == 7)),
                after=(tfm32, g_gate) if k == 0 else (),
                inc=chT if k == 7 else None)
        g_fm32 = tg
        gsl = s_glog[:, p * 16:(p + 1) * 16]
        tgc = kb.op("V", lambda v, gsl=gsl: v.tensor_copy(
            out=gsl, in_=p_gate[:, :]), after=(tg, g_gate), inc=chV)
        tmx = kb.op("V", lambda v, cb=cb, gsl=gsl: v.reduce_max(
            out=C(cb + 16), in_=gsl, axis=AX.X), after=(tgc,), inc=chV)
        tnm = kb.op("V", lambda v, cb=cb: v.tensor_scalar(
            out=C(cb + 17), in0=C(cb + 16), scalar1=-1.0, scalar2=None,
            op0=ALU.mult), after=(tmx,), inc=chV)
        esl = s_g16[:, p * 16:(p + 1) * 16]
        bsl2 = s_g16b[:, p * 16:(p + 1) * 16]
        te = kb.op("S", lambda s, cb=cb, esl=esl, gsl=gsl: s.activation(
            out=esl, in_=gsl, func=AF.Exp, bias=C(cb + 17)),
            after=(tnm,), inc=chS)
        g_gate = tgc
        tse = kb.op("V", lambda v, cb=cb, esl=esl: v.tensor_reduce(
            out=C(cb + 18), in_=esl, axis=AX.X, op=ALU.add),
            after=(te,), inc=chV)
        trs = kb.op("V", lambda v, cb=cb: v.reciprocal(
            out=C(cb + 19), in_=C(cb + 18)), after=(tse,), inc=chV)
        tm1 = kb.op("V", lambda v, cb=cb, esl=esl: v.reduce_max(
            out=C(cb + 20), in_=esl, axis=AX.X), after=(te,), inc=chV)
        teq = kb.op("V", lambda v, cb=cb, esl=esl, bsl2=bsl2: v.tensor_scalar(
            out=bsl2, in0=esl, scalar1=C(cb + 20), scalar2=0.0,
            op0=ALU.subtract, op1=ALU.is_ge), after=(tm1,), inc=chV)
        tmk = kb.op("V", lambda v, esl=esl, bsl2=bsl2: v.scalar_tensor_tensor(
            out=bsl2, in0=bsl2, scalar=-1e30, in1=esl,
            op0=ALU.mult, op1=ALU.add), after=(teq,), inc=chV)
        tm2 = kb.op("V", lambda v, cb=cb, bsl2=bsl2: v.reduce_max(
            out=C(cb + 21), in_=bsl2, axis=AX.X), after=(tmk,), inc=chV)
        th = kb.op("V", lambda v, cb=cb, esl=esl, i=i: v.tensor_scalar(
            out=s_hit[:, i * 16:(i + 1) * 16], in0=esl, scalar1=C(cb + 21),
            scalar2=None, op0=ALU.is_ge), after=(tm2,), inc=chV)
        tw1 = kb.op("V", lambda v, cb=cb, esl=esl, bsl2=bsl2: v.tensor_scalar(
            out=bsl2, in0=esl, scalar1=C(cb + 19), scalar2=None,
            op0=ALU.mult), after=(trs, tm2), inc=chV)
        tw2 = kb.op("V", lambda v, i=i, bsl2=bsl2: v.tensor_tensor(
            out=s_wte[:, i * 16:(i + 1) * 16], in0=bsl2,
            in1=s_hit[:, i * 16:(i + 1) * 16], op=ALU.mult),
            after=(th, tw1), inc=chV)
    esA3.close()
    esA.close()
    a_done = (tok_x2sp[NT - 2], tok_x2sp[NT - 1], tok_xtsp[NT - 2],
              tok_xtsp[NT - 1], tw2) + tok_wcache

    if stop_after == "A":
        kb.op("G", lambda g: g.nop(), after=a_done)
        with nc.Block() as block:
            kb.emit(block)
        es.close()
        return nc

    esF = contextlib.ExitStack()
    s_we = esF.enter_context(st("s_we", [128, 2 * WE_COLS], BF16))
    s_beb = esF.enter_context(st("s_beb", [128, 2 * 8], F32))
    s_xtb = esF.enter_context(st("s_xtb", [128, 2 * 8 * GRP], BF16))
    s_sf = esF.enter_context(st("s_sf", [128, NG * 1024], BF16))
    s_a = esF.enter_context(st("s_a", [128, 2048], BF16))
    s_s3b = esF.enter_context(st("s_s3b", [128, 2048], BF16))
    s_sq = esF.enter_context(st("s_sq", [128, 2 * 2048], BF16))
    s_sums = esF.enter_context(st("s_sums", [1, NG * GRP], F32))
    s_timeb = esF.enter_context(st("s_timeb", [2, NG * GRP], BF16))
    s_onef = esF.enter_context(st("s_onef", [128, 1], F32))
    s_s1f = esF.enter_context(st("s_s1f", [128, 1024], F32))
    s_wshb = None

    t_onef = kb.op("G", lambda g: g.memset(s_onef[:, :], 1.0),
                   after=a_done, inc=chGc)
    t_ones_time = kb.op("G", lambda g: g.memset(s_timeb[:, :], 1.0),
                        after=a_done, inc=chGc)

    g_y = [None] * (HALF // 128)
    g_we = [None, None]
    g_xt = [None, None]
    g_zz = [None, None]
    g_o2 = [(None, None)] * 4
    g_sfgrp = [None] * NG
    g_time = None
    g_beb = [None, None]
    g_ps1 = [None, None]
    g_ps3 = [None, None]
    g_s1f = [None]
    g_bscr = [None, None]
    g_sq = [None, None]

    def emit_wload(pe, we_cols, d_wsrc, d_bsrc, first=False):
        wsl = s_we[:, pe * WE_COLS: pe * WE_COLS + we_cols]
        w_after = ((g_we[0], g_we[1]) if we_cols > WE_COLS else (g_we[pe],))
        tw = kb.op("Y", lambda y: y.dma_start(out=wsl, in_=d_wsrc),
                   after=w_after + (a_done if first else ()),
                   inc=chWt[pe], k=16)
        if d_bsrc is not None:
            bsl = s_beb[:, pe * 8:(pe + 1) * 8]
            tb = kb.op("Y", lambda y, bsl=bsl: y.dma_start(out=bsl, in_=d_bsrc),
                       after=(g_beb[pe],), inc=chBe[pe], k=16)
        else:
            bsl = s_beb[:, 0:16]
            tb = kb.op("Y", lambda y, bsl=bsl: y.dma_start(out=bsl, in_=d_bsh),
                       after=(g_beb[0], g_beb[1]), inc=chBe[0], k=16)
        kb.wcache[(pe, we_cols)] = (tw, tb)
    out_toks = []

    def ffn_expert(h, e, pe, first, ps1_of, ps3_of, p_o2, we_cols, w1o, w3o,
                   w2o, r0o, nit, nkt, d_wsrc, d_bsrc, stride1, wte_e,
                   groups=tuple(range(NG)), sfs=1024, skip_w=False,
                   s1_dbuf=True, prefetch=None):
        nonlocal g_time
        HB = h * HALF
        g0 = groups[0]
        wsl = s_we[:, pe * WE_COLS: pe * WE_COLS + we_cols]
        if not skip_w:
            emit_wload(pe, we_cols, d_wsrc, d_bsrc, first)
        bsl = s_beb[:, 0:16] if d_bsrc is None else s_beb[:, pe * 8:(pe + 1) * 8]
        tw, tb = kb.wcache[(pe, we_cols)]
        pending_sum = None
        last_smul = None
        t_silu = [None] * NG
        t_s3b = [None] * NG
        t_smul = [None] * NG
        t_sumcp = [None] * NG
        for g in groups:
            pg = g % 2
            pidx = (g % 2) if s1_dbuf else 0
            s1r = ps1_of(g)
            s3r = ps3_of(g)
            sfo = (g - g0) * sfs
            xsl = s_xtb[:, pg * 8 * GRP:(pg + 1) * 8 * GRP]
            txt = kb.op("Y", lambda y, g=g, xsl=xsl, HB=HB: y.dma_start(
                out=xsl,
                in_=d_xt[:, :, HB + g * GRP:HB + (g + 1) * GRP]
                .rearrange("k p n -> p k n")),
                after=(g_xt[pg],) + (a_done if first and g < 2 else ()),
                inc=chXL[pg], k=16)
            if g == groups[0] and prefetch is not None:
                prefetch()
            tmm1 = tmm3 = None
            for it in range(nit):
                for k in range(8):
                    af = ((tw, txt) + tuple(g_ps1[pidx] or ())
                          if (it == 0 and k == 0) else ())
                    tmm1 = kb.op("T", lambda t, it=it, k=k, xsl=xsl, s1r=s1r:
                                 t.matmul(
                        s1r[:, it * GRP:(it + 1) * GRP],
                        wsl[:, w1o + k * stride1 + it * 128:
                            w1o + k * stride1 + it * 128 + 128],
                        xsl[:, k * GRP:(k + 1) * GRP],
                        start=(k == 0), stop=(k == 7)), after=af,
                        inc=chT if (it == nit - 1 and k == 7) else None)
            for it in range(nit):
                for k in range(8):
                    af = (g_ps3[pidx],) if (it == 0 and k == 0) else ()
                    tmm3 = kb.op("T", lambda t, it=it, k=k, xsl=xsl, s3r=s3r:
                                 t.matmul(
                        s3r[:, it * GRP:(it + 1) * GRP],
                        wsl[:, w3o + k * stride1 + it * 128:
                            w3o + k * stride1 + it * 128 + 128],
                        xsl[:, k * GRP:(k + 1) * GRP],
                        start=(k == 0), stop=(k == 7)), after=af,
                        inc=chT if (it == nit - 1 and k == 7) else None)
            g_xt[pg] = tmm3
            for it0 in range(0, nit, 4):
                nb = min(4, nit - it0)
                tcp1 = kb.op("V", lambda v, it0=it0, nb=nb, s1r=s1r:
                             v.tensor_copy(
                    out=s_s1f[:, 0:nb * GRP],
                    in_=s1r[:, it0 * GRP:(it0 + nb) * GRP]),
                    after=(tmm1, g_s1f[0]), inc=chV)
                for it in range(it0, it0 + nb):
                    t_silu[g] = kb.op("S", lambda s, it=it, it0=it0:
                                      s.activation(
                        out=s_a[:, it * GRP:(it + 1) * GRP],
                        in_=s_s1f[:, (it - it0) * GRP:(it - it0 + 1) * GRP],
                        func=AF.Silu, bias=bsl[:, it:it + 1]),
                        after=(tcp1, tb) if it == it0 else (), inc=chS)
                g_s1f[0] = t_silu[g]
            g_ps1[pidx] = (tcp1,)
            for it in range(nit):
                t_s3b[g] = kb.op("V", lambda v, it=it, nit=nit, s3r=s3r:
                                 v.tensor_scalar(
                    out=s_s3b[:, it * GRP:(it + 1) * GRP],
                    in0=s3r[:, it * GRP:(it + 1) * GRP],
                    scalar1=bsl[:, nit + it:nit + it + 1], scalar2=None,
                    op0=ALU.add),
                    after=(tmm3, tb, last_smul) if it == 0 else (), inc=chV)
            sf_guard = tuple(g_sfgrp) if sfs != 1024 else (g_sfgrp[g],)
            t_smul[g] = kb.op("V", lambda v, sfo=sfo, nit=nit: v.tensor_tensor(
                out=s_sf[:, sfo:sfo + nit * GRP],
                in0=s_a[:, 0:nit * GRP], in1=s_s3b[:, 0:nit * GRP],
                op=ALU.mult),
                after=(t_silu[g], t_s3b[g]) + sf_guard, inc=chV)
            tsq = kb.op("S", lambda s, sfo=sfo, nit=nit, pg=pg: s.activation(
                out=s_sq[:, pg * 2048:pg * 2048 + nit * GRP],
                in_=s_sf[:, sfo:sfo + nit * GRP], func=AF.Square),
                after=(t_smul[g], g_sq[pg]), inc=chS)
            def emit_sum(g=g, s3r=s3r, tsq=tsq, ts3b=t_s3b[g], pidx=pidx):
                tsum = None
                for it in range(nit):
                    tsum = kb.op("T", lambda t, it=it, s3r=s3r, g=g: t.matmul(
                        s3r[0:1, 0:GRP], s_on1[:, :],
                        s_sq[:, (g % 2) * 2048 + it * GRP:
                             (g % 2) * 2048 + (it + 1) * GRP],
                        start=(it == 0), stop=(it == nit - 1)),
                        after=(tsq, ts3b) if it == 0 else (),
                        inc=chT if it == nit - 1 else None)
                g_sq[g % 2] = tsum
                t_sumcp[g] = kb.op("V", lambda v, g=g, s3r=s3r: v.tensor_copy(
                    out=s_sums[0:1, g * GRP:(g + 1) * GRP],
                    in_=s3r[0:1, 0:GRP]),
                    after=(tsum, g_time), inc=chV)
                g_ps3[pidx] = t_sumcp[g]

            last_smul = t_smul[g]
            if not s1_dbuf:
                emit_sum()
            else:
                if pending_sum is not None:
                    pending_sum()
                pending_sum = emit_sum
        if pending_sum is not None:
            pending_sum()
            pending_sum = None
        glast = groups[-1]
        tq = kb.op("S", lambda s: s.activation(
            out=s_timeb[0:1, g0 * GRP:(glast + 1) * GRP],
            in_=s_sums[0:1, g0 * GRP:(glast + 1) * GRP],
            func=AF.Sqrt, bias=c_one[0:1, :]),
            after=(t_sumcp[glast], t_ones_time, g_time, t_cst), inc=chS)
        t_last_k2 = None
        t_last_w2 = None
        for g in groups:
            sfo = (g - g0) * sfs
            for tt in range(2):
                lt = g * 2 + tt
                gi = h * (HALF // 128) + lt
                po = lt % 4
                cb = 32 * (lt % 2)
                pob = p_o2[po]
                tmm = None
                for kt in range(nkt):
                    for nh in range(2):
                        wnh = 512 if nh == 0 else 511
                        af = ()
                        if kt == 0 and nh == 0:
                            af = (t_smul[g], g_o2[po][0], g_o2[po][1],
                                  t_sumcp[glast])
                        tmm = kb.op("T", lambda t, sfo=sfo, tt=tt, kt=kt, nh=nh,
                                    wnh=wnh, pob=pob: t.matmul(
                            pob(nh * 512, nh * 512 + wnh),
                            s_sf[:, sfo + kt * GRP + tt * 128:
                                 sfo + kt * GRP + tt * 128 + 128],
                            wsl[:, w2o + kt * NSH + nh * 512:
                                w2o + kt * NSH + nh * 512 + wnh],
                            start=(kt == 0), stop=False), after=af)
                for nh in range(2):
                    wnh = 512 if nh == 0 else 511
                    tmm = kb.op("T", lambda t, g=g, tt=tt, nh=nh, wnh=wnh,
                                pob=pob: t.matmul(
                        pob(nh * 512, nh * 512 + wnh),
                        s_timeb[0:2, g * GRP + tt * 128:g * GRP + tt * 128 + 128],
                        wsl[0:2, r0o + nh * 512:r0o + nh * 512 + wnh],
                        start=False, stop=True),
                        after=(tq,) if nh == 0 else (),
                        inc=chT if nh == 1 else None)
                t_last_k2 = tmm
                g_sfgrp[g] = tmm
                t_last_w2 = tmm
                pz = lt % 2
                zsl = s_w32[:, pz * D:(pz + 1) * D]
                ysl = s_y[:, lt * D:(lt + 1) * D]
                bscr = s_b16[:, pz * D:pz * D + NSH]
                tob = kb.op("V", lambda v, pob=pob, bscr=bscr:
                            v.tensor_copy(out=bscr, in_=pob(0, NSH)),
                            after=(tmm, g_bscr[pz]), inc=chV)
                tq1 = kb.op("S", lambda s, bscr=bscr, cb=cb:
                            s.activation(out=s_sqs[:, 0:NSH], in_=bscr,
                                         func=AF.Square, accum_out=C(cb)),
                            after=(tob,), inc=chS)
                g_bscr[pz] = tq1
                tt0 = kb.op("S", lambda s, cb=cb: s.activation(
                    out=C(cb + 1), in_=C(cb), func=AF.Sqrt, bias=c_one),
                    after=(tq1, t_cst), inc=chS)
                if wte_e is not None:
                    wcol = s_wte[:, gi * 16 + wte_e:gi * 16 + wte_e + 1]
                    hcol = s_hit[:, gi * 16 + wte_e:gi * 16 + wte_e + 1]
                else:
                    wcol = hcol = None
                tzz = kb.op("V", lambda v, pob=pob, zsl=zsl, ysl=ysl,
                            wcol=wcol: v.scalar_tensor_tensor(
                    out=zsl[:, 1:D], in0=pob(0, NSH),
                    scalar=(wcol if wcol is not None else 1.0),
                    in1=ysl[:, 1:D], op0=ALU.mult, op1=ALU.add),
                    after=(tmm, g_zz[pz], g_y[lt]),
                    inc=chV)
                tz0 = kb.op("V", lambda v, zsl=zsl, ysl=ysl, cb=cb,
                            wcol=wcol: v.scalar_tensor_tensor(
                    out=zsl[:, 0:1], in0=C(cb + 1),
                    scalar=(wcol if wcol is not None else 1.0),
                    in1=ysl[:, 0:1], op0=ALU.mult, op1=ALU.add),
                    after=(tt0,), inc=chV)
                tS1 = kb.op("S", lambda s, zsl=zsl, cb=cb: s.activation(
                    out=s_sqs[:, 0:NSH], in_=zsl[:, 1:D], func=AF.Square,
                    accum_out=C(cb + 2)), after=(tzz,), inc=chS)
                tzq = kb.op("V", lambda v, zsl=zsl, cb=cb: v.tensor_tensor(
                    out=C(cb + 3), in0=zsl[:, 0:1], in1=zsl[:, 0:1],
                    op=ALU.mult), after=(tz0,), inc=chV)
                tdd = kb.op("V", lambda v, cb=cb: v.tensor_tensor(
                    out=C(cb + 4), in0=C(cb + 3), in1=C(cb + 2),
                    op=ALU.subtract), after=(tzq, tS1), inc=chV)
                tdm = kb.op("V", lambda v, cb=cb: v.tensor_scalar(
                    out=C(cb + 5), in0=C(cb + 4), scalar1=EPS, scalar2=None,
                    op0=ALU.max), after=(tdd,), inc=chV)
                trc = kb.op("V", lambda v, cb=cb: v.reciprocal(
                    out=C(cb + 6), in_=C(cb + 5)), after=(tdm,), inc=chV)
                tsq2 = kb.op("S", lambda s, cb=cb: s.activation(
                    out=C(cb + 7), in_=C(cb + 6), func=AF.Sqrt),
                    after=(trc,), inc=chS)
                if hcol is not None:
                    tr1 = kb.op("V", lambda v, cb=cb: v.tensor_scalar(
                        out=C(cb + 8), in0=C(cb + 7), scalar1=1.0,
                        scalar2=None, op0=ALU.subtract), after=(tsq2,),
                        inc=chV)
                    tr2 = kb.op("V", lambda v, cb=cb, hcol=hcol:
                                v.scalar_tensor_tensor(
                        out=C(cb + 9), in0=hcol, scalar=C(cb + 8),
                        in1=s_onef[:, :], op0=ALU.mult, op1=ALU.add),
                        after=(tr1, t_onef), inc=chV)
                    rcol = C(cb + 9)
                    rtok = tr2
                else:
                    rcol = C(cb + 7)
                    rtok = tsq2
                ty = kb.op("S", lambda s, zsl=zsl, ysl=ysl, rcol=rcol:
                           s.activation(out=ysl, in_=zsl, func=AF.Identity,
                                        scale=rcol),
                           after=(rtok, tzz, tz0), inc=chS)
                g_y[lt] = ty
                g_zz[pz] = ty
                g_o2[po] = (tzz, tob)
                if s1_dbuf:
                    if po < 2:
                        g_ps1[po] = (tzz, tob)
                    else:
                        g_ps3[po - 2] = tzz
                else:
                    g_ps1[0] = (tzz, tob)
        g_we[pe] = t_last_w2
        if we_cols > WE_COLS:
            g_we[0] = g_we[1] = t_last_w2
        g_beb[pe] = t_s3b[NG - 1]
        g_time = t_last_k2
        return ty

    kb.wcache = {}
    NLT = HALF // 128
    ty_last = None
    for h in range(2):
        yg = tuple(t for t in g_y if t is not None)
        ty0 = kb.op("G", lambda g: g.memset(s_y[:, :], 0.0),
                    after=yg + ((a_done) if h == 0 else ()), inc=chGc)
        ty1 = kb.op("G", lambda g: g.memset(
            bass.AP(s_y, 0, [[NLT * D, 128], [D, NLT], [1, 1]]), 1.0),
            after=(ty0,), inc=chGc)
        for lt in range(NLT):
            g_y[lt] = ty1
        esB = contextlib.ExitStack()
        psA = esB.enter_context(pt("psA_%d" % h, [128, 2048], F32))
        psB = esB.enter_context(pt("psB_%d" % h, [128, 2048], F32))
        psab = [psA, psB]
        ps1_of = lambda g: psab[g % 2][:, 0:1024]
        ps3_of = lambda g: psab[g % 2][:, 1024:2048]
        po2 = [lambda lo, hi: psA[:, lo:hi],
               lambda lo, hi: psB[:, lo:hi],
               lambda lo, hi: psA[:, 1024 + lo:1024 + hi],
               lambda lo, hi: psB[:, 1024 + lo:1024 + hi]]
        for e in range(NE):
            if e == 0:
                emit_wload(0, WE_COLS, d_wec[0, :, :], d_bexp[0, :, :],
                           first=(h == 0))
            pf = None
            if e + 1 < NE:
                pf = (lambda e1=e + 1: emit_wload(
                    e1 % 2, WE_COLS, d_wec[e1, :, :], d_bexp[e1, :, :]))
            ty_last = ffn_expert(h, e, e % 2, (h == 0 and e == 0),
                                 ps1_of, ps3_of, po2, WE_COLS,
                                 W1_OFF, W3_OFF, W2_OFF, R0_OFF, 4, 4,
                                 d_wec[e, :, :], d_bexp[e, :, :], 512, e,
                                 skip_w=True, prefetch=pf)
        esB.close()
        g_ps1[0] = g_ps1[1] = (ty_last,)
        g_ps3[0] = g_ps3[1] = ty_last
        for _i in range(4):
            g_o2[_i] = (ty_last, ty_last)
        esC = contextlib.ExitStack()
        p_s1C = esC.enter_context(pt("p_s1C_%d" % h, [128, 2048], F32))
        p_s3C = esC.enter_context(pt("p_s3C_%d" % h, [128, 2048], F32))
        po2C = [lambda lo, hi: p_s1C[:, lo:hi],
                lambda lo, hi: p_s1C[:, 1024 + lo:1024 + hi],
                lambda lo, hi: p_s3C[:, lo:hi],
                lambda lo, hi: p_s3C[:, 1024 + lo:1024 + hi]]
        ps1C_of = lambda g: p_s1C[:, :]
        ps3C_of = lambda g: p_s3C[:, :]
        for sb in range(2):
            groups = tuple(range(sb * 4, sb * 4 + 4))
            ty_last = ffn_expert(h, 100 + sb, 0, False,
                                 ps1C_of, ps3C_of, po2C, WS_COLS,
                                 SW1_OFF, SW3_OFF, SW2_OFF, SR0_OFF, 8, 8,
                                 d_wshc, None, 1024, None,
                                 groups=groups, sfs=2048, skip_w=(sb == 1),
                                 s1_dbuf=False)
            g_ps1[0] = g_ps1[1] = (ty_last,)
            g_ps3[0] = g_ps3[1] = ty_last
            for _i in range(4):
                g_o2[_i] = (ty_last, ty_last)
        esC.close()
        for lt in range(NLT):
            gi = h * NLT + lt
            px = lt % 2
            cb = 32 * px + 16
            ysl = s_y[:, lt * D:(lt + 1) * D]
            xsl = s_x[:, px * D:(px + 1) * D]
            wsl2 = s_w32[:, px * D:(px + 1) * D]
            tx2l = kb.op("Y", lambda y, gi=gi, xsl=xsl: y.dma_start(
                out=xsl, in_=d_x2[gi * 128:(gi + 1) * 128, :]),
                after=(g_x[px], tok_x2sp[NT - 2], tok_x2sp[NT - 1]),
                inc=chX[px], k=16)
            rstd, mb, tln = ln_stats(ysl, cb, "sqrt", (g_y[lt],))
            txf = kb.op("S", lambda s, xsl=xsl, wsl2=wsl2, mb=mb: s.activation(
                out=wsl2, in_=xsl, func=AF.Identity, bias=mb),
                after=(tx2l, tln[1], g_zz[px]), inc=chS)
            tst = kb.op("V", lambda v, ysl=ysl, xsl=xsl, wsl2=wsl2, rstd=rstd:
                        v.scalar_tensor_tensor(
                out=xsl, in0=ysl, scalar=rstd, in1=wsl2,
                op0=ALU.mult, op1=ALU.add), after=(txf, tln[0]), inc=chV)
            tout = kb.op("Y", lambda y, gi=gi, xsl=xsl: y.dma_start(
                out=d_out[gi * 128:(gi + 1) * 128, :], in_=xsl),
                after=(tst,), inc=chOut[px], k=16)
            out_toks.append(tout)
            g_x[px] = tout
            g_zz[px] = tst
            g_y[lt] = tst
    esF.close()

    if debug:
        tdbg1 = kb.op("Y", lambda y: y.dma_start(
            out=d_dbg[:, 0:NT * 16], in_=s_wte[:, :]),
            after=(out_toks[-1],), inc=chC, k=16)
        tdbg2 = kb.op("Y", lambda y: y.dma_start(
            out=d_dbg[:, 512:512 + NT * 16], in_=s_hit[:, :]),
            after=(tdbg1,), inc=chC, k=16)
        out_toks.append(tdbg2)

    kb.op("G", lambda g: g.nop(), after=tuple(out_toks[-3:]))

    with nc.Block() as block:
        kb.emit(block)
    es.close()
    return nc


_NC_CACHE = {}


def _get_nc(debug=False):
    if debug not in _NC_CACHE:
        _NC_CACHE[debug] = build_nc(debug)
    return _NC_CACHE[debug]


def kernel(**inputs):
    g, x = pack_host(inputs)
    nc = _get_nc(False)
    in_maps = []
    for b in range(B):
        m = dict(g)
        m["x"] = np.ascontiguousarray(x[b])
        in_maps.append(m)
    res = run_bass_kernel_spmd(nc, in_maps, core_ids=list(range(B)))
    return np.stack([r["out"] for r in res.results]).astype(np.float32)
